# version 25 (final)
# baseline (speedup 1.0000x reference)
import numpy as np
import ml_dtypes
import concourse.bass as bass
import concourse.mybir as mybir
from concourse.bass_utils import run_bass_kernel_spmd
from contextlib import ExitStack

F32 = mybir.dt.float32
BF16 = mybir.dt.bfloat16
FP16 = mybir.dt.float16
AF = mybir.ActivationFunctionType
ALU = mybir.AluOpType
AX = mybir.AxisListType

T = 2048
D = 1024
NT = 16
NC = 8
INNER = 4096
FFN_B_IGS = 8
FFN_B_TS = 16
FFN_B_PARTS = "msvd"

ENGS = ("te", "ve", "se", "sy", "gp")
A4A_SEQ = False
A4B_SEQ = False
SUBLAYERS = ("attn_a", "ffn1", "attn_b", "ffn2")


class Prog:
    def __init__(self, nc, ctx):
        self.nc = nc
        self.ctx = ctx
        self.q = {e: [] for e in ENGS}
        self.xs = {}
        self.phase_idx = 0
        self._new_sems()

    def _new_sems(self):
        self.sems = {e: self.ctx.enter_context(self.nc.semaphore(f"s{self.phase_idx}_{e}"))
                     for e in ENGS}
        self.cnt = {e: 0 for e in ENGS}

    def _roll(self, ch):
        self.phase_idx += 1
        self.sems[ch] = self.ctx.enter_context(
            self.nc.semaphore(f"s{self.phase_idx}_{ch}"))
        self.cnt[ch] = 0

    def _emit(self, eng, fn, tick=0, drain=False):
        if tick:
            self.cnt[eng] += tick
        self.q[eng].append([fn, self.sems[eng], tick, True, drain])

    def te(self, fn, tick=0):
        self._emit("te", fn, tick)

    def ve(self, fn, tick=0, drain=False):
        self._emit("ve", fn, tick, drain)

    def se(self, fn, tick=0, drain=False):
        self._emit("se", fn, tick, drain)

    def dma(self, eng, fn):
        self._emit(eng, fn, 16)

    def wait(self, eng, ch, n=None):
        if n is None:
            n = self.cnt[ch]
        if n <= 0:
            return
        sem = self.sems[ch]
        self.q[eng].append([lambda e, sem=sem, n=n: e.wait_ge(sem, n), None, 0, False, False])

    def n(self, ch):
        return self.cnt[ch]

    def xsem(self, name):
        if name not in self.xs:
            self.xs[name] = [self.ctx.enter_context(self.nc.semaphore(f"x_{name}")), 0]
        return name

    def dmax(self, eng, fn, xname):
        s = self.xs[xname]
        s[1] += 16
        self.q[eng].append([fn, s[0], 16, True, False])

    def xwait(self, eng, xname, n=None):
        s = self.xs[xname]
        if n is None:
            n = s[1]
        if n > 0:
            self.q[eng].append([lambda e, sem=s[0], n=n: e.wait_ge(sem, n), None, 0, False, False])

    def xn(self, xname):
        return self.xs[xname][1]

    def tick(self, eng):
        for ent in reversed(self.q[eng]):
            if ent[3]:
                if ent[1] is not self.sems[eng]:
                    break
                ent[2] += 1
                self.cnt[eng] += 1
                return

    def barrier(self):
        for e in ("te", "ve", "se"):
            self.tick(e)
        counts = dict(self.cnt)
        sems = dict(self.sems)
        for e in ENGS:
            for o in ENGS:
                if o != e and counts[o] > 0:
                    self.q[e].append([lambda eng, sem=sems[o], n=counts[o]: eng.wait_ge(sem, n),
                                      None, 0, False, False])
        for name, (xsem, xcnt) in self.xs.items():
            if xcnt > 0:
                for e in ENGS:
                    self.q[e].append([lambda eng, sem=xsem, n=xcnt: eng.wait_ge(sem, n),
                                      None, 0, False, False])
        for ch in ENGS:
            if self.cnt[ch] > 30000:
                self._roll(ch)

    def finalize(self, block):
        qs = {}
        for eng, ents in self.q.items():
            fns = []
            for fn, sem, tick, is_op, drain in ents:
                if tick:
                    if drain and is_op:
                        def wrapped(e, fn=fn, sem=sem, tick=tick):
                            fn(e).then_inc(sem, tick)
                            e.drain()
                        fns.append(wrapped)
                    else:
                        fns.append(lambda e, fn=fn, sem=sem, tick=tick: fn(e).then_inc(sem, tick))
                elif drain and is_op:
                    def wrapped2(e, fn=fn):
                        fn(e)
                        e.drain()
                    fns.append(wrapped2)
                else:
                    fns.append(fn)
            qs[eng] = fns

        @block.sync
        def _(e):
            for fn in qs["sy"]:
                fn(e)

        @block.gpsimd
        def _(e):
            for fn in qs["gp"]:
                fn(e)

        @block.vector
        def _(e):
            for fn in qs["ve"]:
                fn(e)

        @block.scalar
        def _(e):
            for fn in qs["se"]:
                fn(e)

        @block.tensor
        def _(e):
            for fn in qs["te"]:
                fn(e)


def build(enabled=SUBLAYERS):
    nc = bass.Bass("TRN2", target_bir_lowering=False, detect_race_conditions=False)

    def din(name, shape, dt=F32):
        return nc.dram_tensor(name, shape, dt, kind="ExternalInput")

    x_in = din("x", [T, D])
    aW = {n: din("a" + n, [D, D]) for n in ("Wq", "Wk", "Wv", "Wid", "Wo")}
    aWb = din("aWb", [D, 4])
    acw = {n: din("acw" + n, [D, 4]) for n in ("q", "k", "v")}
    afirs = din("afirs", [D, 3])
    afirl = din("afirl", [D, 63])
    agW1 = din("agW1", [1040, D])
    agb1 = din("agb1", [D, 1])
    agW2 = din("agW2", [D, 4])
    agb2 = din("agb2", [1, 4])
    amisc = din("amisc", [1, 16])
    alt = din("alt", [1, 16])
    afp = din("afp", [1, 16])
    aonw = din("aonw", [1, 256])
    bWq = din("bWq", [D, D]); bWk = din("bWk", [D, 256]); bWv = din("bWv", [D, 256])
    bbq = din("bbq", [D, 1]); bbk = din("bbk", [256, 1]); bbv = din("bbv", [256, 1])
    fw = {"g": din("fwg", [INNER, D]), "u": din("fwu", [INNER, D]), "d": din("fwd", [D, INNER])}
    c_id = din("c_id", [128, 128], BF16)
    c_mS = din("c_mS", [128, 128], BF16)
    c_mUin = din("c_mUin", [128, 128], BF16)
    c_ones1 = din("c_ones1", [1, 128], BF16)
    c_selB = din("c_selB", [128, 2], BF16)
    c_sel12 = din("c_sel12", [128, 288], BF16)

    out = nc.dram_tensor("out", [T, D], F32, kind="ExternalOutput")

    import os
    _dbg = os.environ.get("KDBG", "") == "1"

    def scr(name, shape, dt=BF16):
        return nc.dram_tensor(name, shape, dt, kind="ExternalOutput" if _dbg else "Internal")

    R_s = scr("R_s", [T, D], F32)
    h_s = scr("h_s", [T, D])
    k_s = scr("k_s", [D, T])
    v_s = scr("v_s", [D, T])
    ls_s = scr("ls_s", [D, T]); ll_s = scr("ll_s", [D, T])
    base_s = scr("base_s", [D, T])
    id_s = scr("id_s", [T, D])
    delta_s = scr("delta_s", [T, D])
    xq_s = scr("xq_s", [T, D])
    gu_s = scr("gu_s", [T, INNER])
    guq_s = scr("guq_s", [T, INNER])
    wq_s = {"g": scr("wqg_s", [INNER, D]), "u": scr("wqu_s", [INNER, D]),
            "d": scr("wqd_s", [D, INNER])}
    wqT_s = {"g": scr("wqgT_s", [NC, 128, INNER]), "u": scr("wquT_s", [NC, 128, INNER]),
             "d": scr("wqdT_s", [32, 128, D])}
    bv_s = scr("bv_s", [256, T])
    stk_s = scr("stk_s", [128, 1024])
    bc_s = scr("bc_s", [256], F32)
    rstd_s = scr("rstd_s", [8, T], F32)
    logits_s = scr("logits_s", [16, T], F32)

    with ExitStack() as ctx:
        sb = lambda n, s, d=F32: ctx.enter_context(nc.sbuf_tensor(n, s, d))

        A = [sb(f"A{i}", [128, 16384], BF16) for i in range(5)]
        WK = sb("WK", [128, 4096], F32)
        WK2 = sb("WK2", [128, 2048], F32)
        SM = sb("SM", [128, 1024], F32)
        SMB = sb("SMB", [128, 2048], BF16)
        STK = sb("STK", [128, 1024], BF16)
        LOWF = sb("LOWF", [32, 2048], F32)
        LOWB = sb("LOWB", [64, 2048], BF16)
        CST = sb("CST", [128, 384], BF16)
        CS2 = sb("CS2", [128, 192], BF16)
        SEL12 = sb("SEL12", [128, 288], BF16)

        ident = CST[:, 0:128]
        mS = CST[:, 128:256]
        mUin = CST[:, 256:384]
        ones1 = CS2[0:1, 0:128]
        selB = CS2[:, 128:130]

        PSBIG = ctx.enter_context(nc.psum_tensor("psbig", [128, 2048], F32))
        PS47 = [ctx.enter_context(nc.psum_tensor(f"ps{i}", [128, 512], F32)) for i in range(4)]
        PS = [PSBIG[:, i * 512:(i + 1) * 512] for i in range(4)] + [t[:] for t in PS47]
        PST = PSBIG[0:12, :]

        p = Prog(nc, ctx)

        def bf(a):
            return a[:].bitcast(BF16)

        def f32(a):
            return a[:].bitcast(F32)

        def R_tile(t):
            a = A[0] if t < 8 else A[1]
            return f32(a)[:, (t % 8) * 1024:(t % 8 + 1) * 1024]

        def emit_ln(dst_fn):
            for t in range(NT):
                p.ve(lambda e, t=t: e.bn_stats(out=SM[:, 12 * t:12 * t + 6],
                                               in_=R_tile(t)[:, 0:512]))
                p.ve(lambda e, t=t: e.bn_stats(out=SM[:, 12 * t + 6:12 * t + 12],
                                               in_=R_tile(t)[:, 512:1024]))
            for t in range(NT):
                p.ve(lambda e, t=t: e.bn_aggr(out=SM[:, 192 + 2 * t:192 + 2 * t + 2],
                                              in_=SM[:, 12 * t:12 * t + 12]),
                     tick=1 if t == NT - 1 else 0)
            p.wait("se", "ve")
            p.se(lambda e: e.activation(
                out=SM[:, 224:224 + NT].rearrange("p (t c) -> p t c", c=1),
                in_=SM[:, 192:192 + 2 * NT].rearrange("p (t c) -> p t c", c=2)[:, :, 1:2],
                func=AF.Sqrt, bias=SM[:, 1008:1009]), tick=1)
            p.wait("ve", "se")
            p.ve(lambda e: e.reciprocal(out=SM[:, 240:240 + NT],
                                        in_=SM[:, 224:224 + NT]), drain=True)
            for t in range(NT):
                p.ve(lambda e, t=t: e.tensor_scalar(
                    out=dst_fn(t), in0=R_tile(t),
                    scalar1=SM[:, 192 + 2 * t:192 + 2 * t + 1],
                    scalar2=SM[:, 240 + t:241 + t],
                    op0=ALU.subtract, op1=ALU.mult),
                    tick=1 if t == NT - 1 else 0)

        def trload_fm(dst_arena, src_dram, n_chunks, rows=T):
            for c in range(n_chunks):
                p.dma("sy", lambda e, c=c: e.dma_start_transpose(
                    out=bf(dst_arena)[:, c * rows:(c + 1) * rows],
                    in_=src_dram[:, c * 128:(c + 1) * 128]))

        def trload_tok(dst_arena, src_dram, width):
            for t in range(NT):
                p.dma("sy", lambda e, t=t: e.dma_start_transpose(
                    out=bf(dst_arena)[:, t * width:(t + 1) * width],
                    in_=src_dram[:, t * 128:(t + 1) * 128]))

        def trload_tok_off(dst_arena, off, src_dram, width):
            for t in range(NT):
                p.dma("sy", lambda e, t=t: e.dma_start_transpose(
                    out=bf(dst_arena)[:, off + t * width:off + (t + 1) * width],
                    in_=src_dram[:, t * 128:(t + 1) * 128]))

        def bcast(dst_sm_slice, src_bf16_row, n):
            p.wait("te", "ve")
            p.te(lambda e: e.matmul(out=PS[7][:, 0:n], lhsT=ones1, rhs=src_bf16_row,
                                    start=True, stop=True), tick=1)
            p.wait("ve", "te")
            p.ve(lambda e: e.tensor_copy(out=dst_sm_slice, in_=PS[7][:, 0:n]), tick=1)

        for t in range(NT):
            p.dma("sy", lambda e, t=t: e.dma_start(out=R_tile(t), in_=x_in[t * 128:(t + 1) * 128, :]))
        p.dma("sy", lambda e: e.dma_start(out=CST[:, 0:128], in_=c_id[:]))
        p.dma("sy", lambda e: e.dma_start(out=CST[:, 128:256], in_=c_mS[:]))
        p.dma("sy", lambda e: e.dma_start(out=CST[:, 256:384], in_=c_mUin[:]))
        p.dma("sy", lambda e: e.dma_start(out=CS2[0:1, 0:128], in_=c_ones1[:]))
        p.dma("sy", lambda e: e.dma_start(out=CS2[:, 128:130], in_=c_selB[:]))
        p.dma("sy", lambda e: e.dma_start(out=SEL12[:], in_=c_sel12[:]))
        p.ve(lambda e: e.memset(SM[:, 1008:1009], 1e-5), tick=1)
        p.ve(lambda e: e.memset(SM[:, 1009:1010], 1e-6), tick=1)
        p.ve(lambda e: e.memset(SM[:, 1016:1017], 1.0), tick=1)
        p.ve(lambda e: e.memset(SM[:, 1018:1019], 1536.0), tick=1)
        p.barrier()

        def emit_wprep(key, rows, cols, scol):
            w_dram = fw[key]
            nrt = rows // 128
            if cols <= 2048:
                bufs = [WK[:, 0:cols], WK[:, cols:2 * cols]]
            else:
                bufs = [WK[:, 0:cols], f32(A[2])[:, 4096:4096 + cols]]
            xl = [p.xsem("wpl0"), p.xsem("wpl1")]
            xs = [p.xsem("wps0"), p.xsem("wps1")]
            l_marks = {}
            ve_marks = {}

            def _load(rt, guard_ch, guard_marks):
                q = "gp" if rt % 3 == 2 else "sy"
                if rt >= 2 and (rt - 2) in guard_marks:
                    p.wait(q, guard_ch, guard_marks[rt - 2])
                p.dmax(q, lambda e, rt=rt, b=bufs[rt % 2]: e.dma_start(
                    out=b, in_=w_dram[rt * 128:(rt + 1) * 128, :]), xl[rt % 2])
                l_marks[rt] = p.xn(xl[rt % 2])

            for rt in range(min(2, nrt)):
                _load(rt, "ve", ve_marks)
            for rt in range(nrt):
                p.xwait("ve", xl[rt % 2], l_marks[rt])
                b = bufs[rt % 2]
                if rt == 0:
                    p.ve(lambda e, b=b: e.reduce_sum(out=SM[:, 512:513], in_=b, axis=AX.X,
                                                     apply_absolute_value=True), tick=1, drain=True)
                else:
                    p.ve(lambda e, b=b: e.reduce_sum(out=SM[:, 513:514], in_=b, axis=AX.X,
                                                     apply_absolute_value=True), drain=True)
                    p.ve(lambda e: e.tensor_add(out=SM[:, 512:513], in0=SM[:, 512:513],
                                                in1=SM[:, 513:514]), tick=1, drain=True)
                ve_marks[rt] = p.n("ve")
                if rt + 2 < nrt:
                    _load(rt + 2, "ve", ve_marks)
            p.wait("sy", "ve")
            p.dma("sy", lambda e: e.dma_start(out=bc_s[0:128], in_=SM[:, 512:513]))
            p.wait("sy", "sy")
            p.dma("sy", lambda e: e.dma_start(out=WK2[0:1, 0:128],
                                              in_=bc_s[0:128].rearrange("(o c) -> o c", o=1)))
            p.wait("ve", "sy")
            p.ve(lambda e: e.reduce_sum(out=SM[0:1, 519:520], in_=WK2[0:1, 0:128], axis=AX.X), drain=True)
            p.ve(lambda e: e.tensor_scalar(out=SM[0:1, 520:521], in0=SM[0:1, 519:520],
                                           scalar1=0.5 / (rows * cols), scalar2=None, op0=ALU.mult), drain=True)
            p.ve(lambda e: e.tensor_scalar(out=SM[0:1, 521:522], in0=SM[0:1, 519:520],
                                           scalar1=1.0 / (rows * cols), scalar2=1e-5,
                                           op0=ALU.mult, op1=ALU.max), tick=1)
            p.wait("sy", "ve")
            p.dma("sy", lambda e: e.dma_start(out=bc_s[128:130], in_=SM[0:1, 520:522]))
            p.wait("sy", "sy")
            p.dma("sy", lambda e, scol=scol: e.dma_start(
                out=SM[:, scol:scol + 2],
                in_=bc_s[128:130].rearrange("(o c) -> o c", o=1).partition_broadcast(128)))
            thr = SM[:, scol:scol + 1]
            l_marks = {}
            se_marks = {}
            ve2_marks = {}
            gp_marks = {}

            def _load2(rt):
                q = "gp" if rt % 3 == 2 else "sy"
                if rt >= 2 and (rt - 2) in se_marks:
                    p.wait(q, "se", se_marks[rt - 2])
                p.dmax(q, lambda e, rt=rt, b=bufs[rt % 2]: e.dma_start(
                    out=b, in_=w_dram[rt * 128:(rt + 1) * 128, :]), xl[rt % 2])
                l_marks[rt] = p.xn(xl[rt % 2])

            p.wait("ve", "sy")
            p.wait("gp", "ve")
            p.wait("sy", "ve")
            for rt in range(min(2, nrt)):
                _load2(rt)
            for rt in range(nrt):
                b = bufs[rt % 2]
                ubuf = f32(A[2])[:, (rt % 2) * cols:(rt % 2 + 1) * cols] if cols <= 2048 \
                    else f32(A[2])[:, 0:cols]
                sgn = f32(A[4])[:, (rt % 2) * cols:(rt % 2 + 1) * cols] if cols <= 2048 \
                    else f32(A[4])[:, 0:cols]
                qout = bf(A[3])[:, (rt % 2) * cols:(rt % 2 + 1) * cols]
                p.xwait("se", xl[rt % 2], l_marks[rt])
                if cols > 2048 and rt >= 1:
                    p.wait("se", "ve", ve2_marks[rt - 1])
                elif rt >= 2:
                    p.wait("se", "ve", ve2_marks[rt - 2])
                p.se(lambda e, ubuf=ubuf, b=b: e.activation(out=ubuf, in_=b, func=AF.Abs), tick=1)
                p.se(lambda e, sgn=sgn, b=b: e.sign(out=sgn, in_=b), tick=1)
                se_marks[rt] = p.n("se")
                p.wait("ve", "se", se_marks[rt])
                if rt >= 2:
                    p.xwait("ve", xs[rt % 2], gp_marks[rt - 2])
                p.ve(lambda e, ubuf=ubuf: e.tensor_scalar(out=ubuf, in0=ubuf, scalar1=thr,
                                                          scalar2=None, op0=ALU.is_ge))
                p.ve(lambda e, ubuf=ubuf, sgn=sgn, qout=qout: e.tensor_tensor(
                    out=qout, in0=ubuf, in1=sgn, op=ALU.mult), tick=1)
                ve2_marks[rt] = p.n("ve")
                p.wait("gp", "ve", ve2_marks[rt])
                p.dmax("gp", lambda e, rt=rt, qout=qout: e.dma_start(
                    out=wq_s[key][rt * 128:(rt + 1) * 128, :], in_=qout), xs[rt % 2])
                gp_marks[rt] = p.xn(xs[rt % 2])
                if rt + 2 < nrt:
                    _load2(rt + 2)
            p.barrier()
            if key in ("g", "u"):
                return
            tr_marks = {}
            st_marks = {}
            half = 8192
            for kc in range(cols // 128):
                tb = bf(A[3])[:, (kc % 2) * half:(kc % 2) * half + rows]
                if kc >= 2:
                    p.xwait("sy", xs[kc % 2], st_marks[kc - 2])
                p.dmax("sy", lambda e, kc=kc, tb=tb: e.dma_start_transpose(
                    out=tb, in_=wq_s[key][:, kc * 128:(kc + 1) * 128]), xl[kc % 2])
                tr_marks[kc] = p.xn(xl[kc % 2])
                p.xwait("gp", xl[kc % 2], tr_marks[kc])
                p.dmax("gp", lambda e, kc=kc, tb=tb: e.dma_start(
                    out=wqT_s[key][kc, :, :], in_=tb), xs[kc % 2])
                st_marks[kc] = p.xn(xs[kc % 2])
            p.barrier()

        need_ffn = any(tok.startswith(("ffn", "wprep")) for tok in enabled)
        if need_ffn:
            emit_wprep("g", INNER, D, 1010)
            emit_wprep("u", INNER, D, 1012)
            emit_wprep("d", D, INNER, 1014)

        def emit_ffn(ph="abcd"):
            for t in range(NT):
                p.ve(lambda e, t=t: e.bn_stats(out=SM[:, 12 * t:12 * t + 6],
                                               in_=R_tile(t)[:, 0:512]))
                p.ve(lambda e, t=t: e.bn_stats(out=SM[:, 12 * t + 6:12 * t + 12],
                                               in_=R_tile(t)[:, 512:1024]))
            for t in range(NT):
                p.ve(lambda e, t=t: e.bn_aggr(out=SM[:, 192 + 2 * t:192 + 2 * t + 2],
                                              in_=SM[:, 12 * t:12 * t + 12]),
                     tick=1 if t == NT - 1 else 0)
            p.wait("se", "ve")
            p.se(lambda e: e.activation(
                out=SM[:, 224:224 + NT].rearrange("p (t c) -> p t c", c=1),
                in_=SM[:, 192:192 + 2 * NT].rearrange("p (t c) -> p t c", c=2)[:, :, 1:2],
                func=AF.Sqrt, bias=SM[:, 1008:1009]), tick=1)
            p.wait("ve", "se")
            p.ve(lambda e: e.reciprocal(out=SM[:, 240:240 + NT], in_=SM[:, 224:224 + NT]), tick=1)
            hbuf = WK2[:, 0:1024]
            t16 = SMB[:].bitcast(FP16)[:, 0:1024]
            xq = bf(A[3])[:, 0:1024]
            for t in range(NT):
                if t > 0:
                    p.wait("ve", "se")
                p.ve(lambda e, t=t: e.tensor_scalar(
                    out=hbuf, in0=R_tile(t),
                    scalar1=SM[:, 192 + 2 * t:192 + 2 * t + 1],
                    scalar2=SM[:, 240 + t:241 + t],
                    op0=ALU.subtract, op1=ALU.mult))
                p.ve(lambda e, t=t: e.reduce_max(out=SM[:, 512 + t:513 + t], in_=hbuf,
                                                 axis=AX.X, apply_absolute_value=True), drain=True)
                p.ve(lambda e, t=t: e.tensor_scalar(out=SM[:, 544 + t:545 + t],
                                                    in0=SM[:, 512 + t:513 + t],
                                                    scalar1=1e-5, scalar2=None, op0=ALU.max), drain=True)
                p.ve(lambda e, t=t: e.reciprocal(out=SM[:, 560 + t:561 + t],
                                                 in_=SM[:, 544 + t:545 + t]), drain=True)
                p.ve(lambda e, t=t: e.tensor_scalar(out=SM[:, 560 + t:561 + t],
                                                    in0=SM[:, 560 + t:561 + t],
                                                    scalar1=127.0, scalar2=None, op0=ALU.mult), tick=1)
                p.wait("se", "ve")
                asc = SM[:, 560 + t:561 + t]
                p.se(lambda e, asc=asc: e.activation(out=t16, in_=hbuf, func=AF.Identity,
                                                     scale=asc, bias=SM[:, 1018:1019]), tick=1)
                p.wait("ve", "se")
                if t > 0:
                    p.wait("ve", "gp")
                p.ve(lambda e: e.tensor_scalar(out=xq, in0=t16, scalar1=-1536.0,
                                               scalar2=None, op0=ALU.add), tick=1)
                p.wait("gp", "ve")
                p.dma("gp", lambda e, t=t: e.dma_start(out=xq_s[t * 128:(t + 1) * 128, :], in_=xq))
            p.ve(lambda e: e.reciprocal(out=SM[:, 576:576 + NT], in_=SM[:, 560:560 + NT]), drain=True)
            p.ve(lambda e: e.tensor_scalar(out=SM[:, 592:592 + NT], in0=SM[:, 576:576 + NT],
                                           scalar1=SM[:, 1013:1014], scalar2=None, op0=ALU.mult), drain=True)
            p.ve(lambda e: e.tensor_scalar(out=SM[:, 576:576 + NT], in0=SM[:, 576:576 + NT],
                                           scalar1=SM[:, 1011:1012], scalar2=None, op0=ALU.mult), tick=1, drain=True)
            p.barrier()
            if "b" not in ph:
                return
            trload_fm(A[2], xq_s, NC)
            p.barrier()
            p.ve(lambda e: e.memset(SM[:, 512:512 + NT], 0.0), tick=1)
            fbw = [p.xsem("fbw0"), p.xsem("fbw1")]
            ve_marks = {}
            gp_marks = {}
            te_ig_marks = {}
            wmkb = {}

            def _bload(ig):
                if ig >= 2:
                    p.wait("sy", "te", te_ig_marks[ig - 2])
                for kc in range(NC):
                    p.dmax("sy", lambda e, ig=ig, kc=kc: e.dma_start_transpose(
                        out=bf(A[3])[:, (ig % 2) * 4096 + kc * 512:(ig % 2) * 4096 + (kc + 1) * 512],
                        in_=wq_s["g"][ig * 512:(ig + 1) * 512, kc * 128:(kc + 1) * 128]), fbw[ig % 2])
                    p.dmax("sy", lambda e, ig=ig, kc=kc: e.dma_start_transpose(
                        out=bf(A[4])[:, (ig % 2) * 4096 + kc * 512:(ig % 2) * 4096 + (kc + 1) * 512],
                        in_=wq_s["u"][ig * 512:(ig + 1) * 512, kc * 128:(kc + 1) * 128]), fbw[ig % 2])
                wmkb[ig] = p.xn(fbw[ig % 2])

            for ig in range(min(2, FFN_B_IGS)):
                _bload(ig)
            for ig in range(FFN_B_IGS):
                p.xwait("te", fbw[ig % 2], wmkb[ig])
                wo = (ig % 2) * 4096
                for t in range(FFN_B_TS):
                    gi = ig * FFN_B_TS + t
                    pa = PS[(2 * t) % 4]
                    pb = PS[(2 * t) % 4 + 1]
                    if gi >= 2:
                        p.wait("te", "ve", ve_marks[gi - 2])
                    if "m" in FFN_B_PARTS:
                        for kc in range(NC):
                            p.te(lambda e, pa=pa, kc=kc, t=t, wo=wo: e.matmul(
                                out=pa, lhsT=bf(A[2])[:, kc * 2048 + t * 128:kc * 2048 + (t + 1) * 128],
                                rhs=bf(A[3])[:, wo + kc * 512:wo + (kc + 1) * 512],
                                start=(kc == 0), stop=(kc == NC - 1)))
                        for kc in range(NC):
                            p.te(lambda e, pb=pb, kc=kc, t=t, wo=wo: e.matmul(
                                out=pb, lhsT=bf(A[2])[:, kc * 2048 + t * 128:kc * 2048 + (t + 1) * 128],
                                rhs=bf(A[4])[:, wo + kc * 512:wo + (kc + 1) * 512],
                                start=(kc == 0), stop=(kc == NC - 1)))
                    else:
                        for pp in (pa, pb):
                            p.te(lambda e, pp=pp: e.matmul(out=pp, lhsT=ident, rhs=bf(A[3])[:, 0:512],
                                                           start=True, stop=True))
                    p.tick("te")
                    p.wait("se", "te")
                    p.wait("se", "ve")
                    if "s" in FFN_B_PARTS:
                        p.se(lambda e, pa=pa, t=t: e.activation(out=WK2[:, 0:512], in_=pa,
                                                                func=AF.Silu,
                                                                scale=SM[:, 576 + t:577 + t]), tick=1)
                    else:
                        p.se(lambda e: e.nop(), tick=1)
                    p.wait("ve", "se")
                    gu = SMB[:, 1024 + (t % 2) * 512:1024 + (t % 2) * 512 + 512]
                    if gi >= 2:
                        p.wait("ve", "gp", gp_marks[gi - 2])
                    if "v" in FFN_B_PARTS:
                        p.ve(lambda e, pb=pb, t=t, gu=gu: e.scalar_tensor_tensor(
                            out=gu, in0=pb, scalar=SM[:, 592 + t:593 + t], in1=WK2[:, 0:512],
                            op0=ALU.mult, op1=ALU.mult))
                        p.ve(lambda e, gu=gu: e.reduce_max(out=SM[:, 528:529], in_=gu,
                                                           axis=AX.X, apply_absolute_value=True), drain=True)
                        p.ve(lambda e, t=t: e.tensor_tensor(out=SM[:, 512 + t:513 + t],
                                                            in0=SM[:, 512 + t:513 + t],
                                                            in1=SM[:, 528:529], op=ALU.max), tick=1, drain=True)
                    else:
                        p.ve(lambda e, gu=gu: e.memset(gu, 0.0), tick=1)
                    ve_marks[gi] = p.n("ve")
                    p.wait("gp", "ve")
                    if "d" in FFN_B_PARTS:
                        p.dma("gp", lambda e, t=t, ig=ig, gu=gu: e.dma_start(
                            out=gu_s[t * 128:(t + 1) * 128, ig * 512:(ig + 1) * 512], in_=gu))
                    gp_marks[gi] = p.n("gp")
                te_ig_marks[ig] = p.n("te")
                if ig + 2 < FFN_B_IGS:
                    _bload(ig + 2)
            p.barrier()
            if "c" not in ph:
                return
            p.ve(lambda e: e.tensor_scalar(out=SM[:, 544:544 + NT], in0=SM[:, 512:512 + NT],
                                           scalar1=1e-5, scalar2=None, op0=ALU.max), drain=True)
            p.ve(lambda e: e.reciprocal(out=SM[:, 560:560 + NT], in_=SM[:, 544:544 + NT]), drain=True)
            p.ve(lambda e: e.tensor_scalar(out=SM[:, 560:560 + NT], in0=SM[:, 560:560 + NT],
                                           scalar1=127.0, scalar2=None, op0=ALU.mult), drain=True)
            p.ve(lambda e: e.reciprocal(out=SM[:, 576:576 + NT], in_=SM[:, 560:560 + NT]), drain=True)
            p.ve(lambda e: e.tensor_scalar(out=SM[:, 576:576 + NT], in0=SM[:, 576:576 + NT],
                                           scalar1=SM[:, 1015:1016], scalar2=None, op0=ALU.mult), tick=1)
            gtb = [bf(A[4])[:, 0:4096], bf(A[4])[:, 4096:8192]]
            t164 = bf(A[4]).bitcast(FP16)[:, 8192:12288]
            gqb = [WK[:].bitcast(BF16)[:, 0:4096], WK[:].bitcast(BF16)[:, 4096:8192]]
            fcg = [p.xsem("fcg0"), p.xsem("fcg1")]
            fcq = [p.xsem("fcq0"), p.xsem("fcq1")]
            fdt = [p.xsem("fdt0"), p.xsem("fdt1")]
            fdw = [p.xsem("fdw0"), p.xsem("fdw1")]
            lg = {}
            semk = {}
            vemk = {}
            qmk = {}
            tmk = {}

            def _cload(t):
                if t >= 2:
                    p.wait("sy", "se", semk[t - 2])
                p.dmax("sy", lambda e, t=t, b=gtb[t % 2]: e.dma_start(
                    out=b, in_=gu_s[t * 128:(t + 1) * 128, :]), fcg[t % 2])
                lg[t] = p.xn(fcg[t % 2])

            def emit_transposes(tg):
                p.xwait("sy", fcq[0], qmk[4 * tg + 2])
                p.xwait("sy", fcq[1], qmk[4 * tg + 3])
                for kc in range(32):
                    dst = A[2] if kc < 16 else A[3]
                    off = (tg % 2) * 8192 + (kc % 16) * 512
                    p.dmax("sy", lambda e, tg=tg, kc=kc, dst=dst, off=off: e.dma_start_transpose(
                        out=bf(dst)[:, off:off + 512],
                        in_=guq_s[tg * 512:(tg + 1) * 512, kc * 128:(kc + 1) * 128]), fdt[tg % 2])
                tmk[tg] = p.xn(fdt[tg % 2])

            p.wait("se", "ve")
            _cload(0)
            _cload(1)
            for t in range(NT):
                p.xwait("se", fcg[t % 2], lg[t])
                if t >= 1:
                    p.wait("se", "ve", vemk[t - 1])
                asc = SM[:, 560 + t:561 + t]
                p.se(lambda e, asc=asc, b=gtb[t % 2]: e.activation(
                    out=t164, in_=b, func=AF.Identity, scale=asc, bias=SM[:, 1018:1019]), tick=1)
                semk[t] = p.n("se")
                p.wait("ve", "se", semk[t])
                if t >= 2:
                    p.xwait("ve", fcq[t % 2], qmk[t - 2])
                p.ve(lambda e, q=gqb[t % 2]: e.tensor_scalar(out=q, in0=t164, scalar1=-1536.0,
                                                             scalar2=None, op0=ALU.add), tick=1)
                vemk[t] = p.n("ve")
                p.wait("gp", "ve", vemk[t])
                p.dmax("gp", lambda e, t=t, q=gqb[t % 2]: e.dma_start(
                    out=guq_s[t * 128:(t + 1) * 128, :], in_=q), fcq[t % 2])
                qmk[t] = p.xn(fcq[t % 2])
                if t + 2 < NT:
                    _cload(t + 2)
                if t % 4 == 3 and t // 4 < 2:
                    emit_transposes(t // 4)
                if "d" in ph:
                    if t == 3:
                        C0 = p.n("te")
                        wmk0 = {}
                        for kc in range(2):
                            p.dmax("gp", lambda e, kc=kc: e.dma_start(
                                out=SMB[:, (kc % 2) * 1024:(kc % 2 + 1) * 1024],
                                in_=wqT_s["d"][kc, :, :]), fdw[kc % 2])
                            wmk0[kc] = p.xn(fdw[kc % 2])
                        next_w = [2]
                    elif t >= 4:
                        for kc in range(next_w[0], min(next_w[0] + 3, 32)):
                            p.wait("gp", "te", C0 + kc - 1)
                            p.dmax("gp", lambda e, kc=kc: e.dma_start(
                                out=SMB[:, (kc % 2) * 1024:(kc % 2 + 1) * 1024],
                                in_=wqT_s["d"][kc, :, :]), fdw[kc % 2])
                            wmk0[kc] = p.xn(fdw[kc % 2])
                        next_w[0] = min(next_w[0] + 3, 32)
            if "d" not in ph:
                p.barrier()
                return
            assert next_w[0] == 32
            assert p.n("te") == C0, "TE tick count drifted from prediction"
            prev_temk = None
            for tg in range(4):
                p.xwait("te", fdt[tg % 2], tmk[tg])
                temk = {}
                if tg == 0:
                    wmk = wmk0
                else:
                    wmk = {}
                tg_wmk = wmk

                def _wload(kc, prev_temk=prev_temk, wmk=tg_wmk, temk=temk):
                    if kc >= 2:
                        p.wait("gp", "te", temk[kc - 2])
                    elif prev_temk is not None:
                        p.wait("gp", "te", prev_temk[30 + kc])
                    p.dmax("gp", lambda e, kc=kc: e.dma_start(
                        out=SMB[:, (kc % 2) * 1024:(kc % 2 + 1) * 1024],
                        in_=wqT_s["d"][kc, :, :]), fdw[kc % 2])
                    wmk[kc] = p.xn(fdw[kc % 2])

                if tg > 0:
                    _wload(0)
                    _wload(1)
                for kc in range(32):
                    p.xwait("te", fdw[kc % 2], wmk[kc])
                    src = A[2] if kc < 16 else A[3]
                    soff = (tg % 2) * 8192 + (kc % 16) * 512
                    for tt in range(4):
                        for og in range(2):
                            p.te(lambda e, kc=kc, tt=tt, og=og, src=src, soff=soff: e.matmul(
                                out=PS[tt * 2 + og],
                                lhsT=bf(src)[:, soff + tt * 128:soff + (tt + 1) * 128],
                                rhs=SMB[:, (kc % 2) * 1024 + og * 512:(kc % 2) * 1024 + (og + 1) * 512],
                                start=(kc == 0), stop=(kc == 31), skip_group_check=True))
                    p.tick("te")
                    temk[kc] = p.n("te")
                    if tg > 0 and kc + 2 < 32:
                        _wload(kc + 2)
                prev_temk = temk
                if tg + 2 < 4:
                    p.wait("sy", "te", temk[31])
                    emit_transposes(tg + 2)
                p.wait("ve", "te")
                for tt in range(4):
                    t = tg * 4 + tt
                    for og in range(2):
                        p.ve(lambda e, t=t, tt=tt, og=og: e.scalar_tensor_tensor(
                            out=R_tile(t)[:, og * 512:(og + 1) * 512], in0=PS[tt * 2 + og],
                            scalar=SM[:, 576 + t:577 + t],
                            in1=R_tile(t)[:, og * 512:(og + 1) * 512],
                            op0=ALU.mult, op1=ALU.add),
                            tick=1 if (tt == 3 and og == 1) else 0)
                p.wait("te", "ve")
            p.barrier()

        def emit_attn_b():
            emit_ln(lambda t: bf(A[3])[:, t * 1024:(t + 1) * 1024])
            p.wait("gp", "ve")
            for t in range(NT):
                p.dma("gp", lambda e, t=t: e.dma_start(out=h_s[t * 128:(t + 1) * 128, :],
                                                       in_=bf(A[3])[:, t * 1024:(t + 1) * 1024]))
            p.barrier()
            trload_fm(A[2], h_s, NC)
            p.dma("sy", lambda e: e.dma_start(out=SM[:, 700:708],
                                              in_=bbq[:].rearrange("(c p) o -> p (c o)", p=128), allow_slow_non_contiguous=True))
            p.dma("sy", lambda e: e.dma_start(out=SM[:, 708:710],
                                              in_=bbk[:].rearrange("(c p) o -> p (c o)", p=128), allow_slow_non_contiguous=True))
            p.dma("sy", lambda e: e.dma_start(out=SM[:, 710:712],
                                              in_=bbv[:].rearrange("(c p) o -> p (c o)", p=128), allow_slow_non_contiguous=True))
            p.barrier()
            qT = bf(A[3])
            kT = bf(A[4])[:, 0:4096]
            vT = bf(A[4])[:, 4096:8192]
            jobs = [(bWq, 8, lambda mc, o: qT[:, mc * 2048 + o:mc * 2048 + o + 512], 700),
                    (bWk, 2, lambda mc, o: kT[:, mc * 2048 + o:mc * 2048 + o + 512], 708),
                    (bWv, 2, lambda mc, o: vT[:, mc * 2048 + o:mc * 2048 + o + 512], 710)]
            bpw = [p.xsem("bpw0"), p.xsem("bpw1")]
            bpq = [(wd, dstf, bcol, mc) for (wd, ncout, dstf, bcol) in jobs
                   for mc in range(ncout)]
            bwmk = {}
            btemk = {}

            def _bpl(gi):
                wd, _, _, mc = bpq[gi]
                if gi >= 2:
                    p.wait("gp", "te", btemk[gi - 2])
                half = (gi % 2) * 1024
                for kc in range(NC):
                    p.dmax("gp", lambda e, wd=wd, mc=mc, kc=kc, half=half: e.dma_start(
                        out=SMB[:, half + kc * 128:half + (kc + 1) * 128],
                        in_=wd[kc * 128:(kc + 1) * 128, mc * 128:(mc + 1) * 128]), bpw[gi % 2])
                bwmk[gi] = p.xn(bpw[gi % 2])

            p.wait("gp", "te")
            _bpl(0)
            _bpl(1)
            for gi, (wd, dstf, bcol, mc) in enumerate(bpq):
                half = (gi % 2) * 1024
                p.xwait("te", bpw[gi % 2], bwmk[gi])
                se_marks = {}
                for nt4 in range(4):
                    if nt4 >= 2:
                        p.wait("te", "se", se_marks[nt4 - 2])
                    for kc in range(NC):
                        p.te(lambda e, kc=kc, nt4=nt4, half=half: e.matmul(
                            out=PS[nt4 % 2],
                            lhsT=SMB[:, half + kc * 128:half + (kc + 1) * 128],
                            rhs=bf(A[2])[:, kc * 2048 + nt4 * 512:kc * 2048 + (nt4 + 1) * 512],
                            start=(kc == 0), stop=(kc == NC - 1)))
                    p.tick("te")
                    p.wait("se", "te")
                    p.se(lambda e, dstf=dstf, mc=mc, nt4=nt4, bcol=bcol: e.activation(
                        out=dstf(mc, nt4 * 512), in_=PS[nt4 % 2], func=AF.Identity,
                        bias=SM[:, bcol + mc:bcol + mc + 1]), tick=1)
                    se_marks[nt4] = p.n("se")
                btemk[gi] = p.n("te")
                if gi + 2 < len(bpq):
                    _bpl(gi + 2)
            p.barrier()
            for (src, nch, scol) in ((qT, 8, 760), (kT, 2, 792)):
                for c in range(nch):
                    p.wait("ve", "te")
                    p.ve(lambda e, src=src, c=c: e.tensor_mul(
                        out=SMB[:, 0:2048], in0=src[:, c * 2048:(c + 1) * 2048],
                        in1=src[:, c * 2048:(c + 1) * 2048]), tick=1)
                    p.wait("te", "ve")
                    for n4 in range(4):
                        p.te(lambda e, n4=n4: e.matmul(
                            out=PS[n4][0:2, 0:512], lhsT=selB,
                            rhs=SMB[:, n4 * 512:(n4 + 1) * 512], start=True, stop=True))
                    p.tick("te")
                    p.wait("ve", "te")
                    for n4 in range(4):
                        p.ve(lambda e, n4=n4, c=c, scol=scol: e.reduce_max(
                            out=SM[0:2, scol + 4 * c + n4:scol + 4 * c + n4 + 1],
                            in_=PS[n4][0:2, 0:512], axis=AX.X),
                            tick=1 if n4 == 3 else 0, drain=True)
            for c in range(8):
                p.ve(lambda e, c=c: e.reduce_max(out=SM[0:2, 826 + c:827 + c],
                                                 in_=SM[0:2, 760 + 4 * c:760 + 4 * c + 4], axis=AX.X), drain=True)
            for c in range(2):
                p.ve(lambda e, c=c: e.reduce_max(out=SM[0:2, 834 + c:835 + c],
                                                 in_=SM[0:2, 792 + 4 * c:792 + 4 * c + 4], axis=AX.X),
                     tick=1 if c == 1 else 0)
            p.wait("se", "ve")
            p.se(lambda e: e.activation(out=SM[0:2, 836:846], in_=SM[0:2, 826:836], func=AF.Sqrt), tick=1)
            p.wait("sy", "se")
            p.dma("sy", lambda e: e.dma_start(out=SM[0:1, 846:854], in_=SM[0:1, 836:844]))
            p.dma("sy", lambda e: e.dma_start(out=SM[0:1, 854:862], in_=SM[1:2, 836:844]))
            p.dma("sy", lambda e: e.dma_start(out=SM[0:1, 862:864], in_=SM[0:1, 844:846]))
            p.dma("sy", lambda e: e.dma_start(out=SM[0:1, 864:866], in_=SM[1:2, 844:846]))
            p.wait("ve", "sy")
            for h in range(16):
                qpos = (h % 2) * 8 + h // 2
                g = h // 4
                kpos = (g % 2) * 2 + g // 2
                p.ve(lambda e, h=h, qpos=qpos, kpos=kpos: e.tensor_mul(
                    out=SM[0:1, 866 + h:867 + h], in0=SM[0:1, 846 + qpos:847 + qpos],
                    in1=SM[0:1, 862 + kpos:863 + kpos]), drain=True)
            p.ve(lambda e: e.tensor_scalar(out=SM[0:1, 866:882], in0=SM[0:1, 866:882],
                                           scalar1=-0.125, scalar2=None, op0=ALU.mult), tick=1)
            p.wait("sy", "ve")
            p.dma("sy", lambda e: e.dma_start(out=bc_s[130:146], in_=SM[0:1, 866:882]))
            p.wait("sy", "sy")
            p.dma("sy", lambda e: e.dma_start(
                out=SM[:, 880:896],
                in_=bc_s[130:146].rearrange("(o c) -> o c", o=1).partition_broadcast(128)))
            p.barrier()
            for c in range(2):
                p.dma("gp", lambda e, c=c: e.dma_start(out=bv_s[c * 128:(c + 1) * 128, :],
                                                       in_=vT[:, c * 2048:(c + 1) * 2048]))
            p.barrier()
            trload_tok_off(A[4], 4096, bv_s, 256)
            p.barrier()
            for g in range(4):
                gc, gr = divmod(g, 2)
                for half in range(2):
                    p.dma("sy", lambda e, g=g, gc=gc, gr=gr, half=half: e.dma_start(
                        out=bf(A[2])[half * 64:half * 64 + 64, g * 2048:(g + 1) * 2048],
                        in_=kT[gr * 64:gr * 64 + 64, gc * 2048:(gc + 1) * 2048]))
            p.barrier()
            kT2 = bf(A[2])[:, 0:8192]
            v_tok = bf(A[4])[:, 4096:8192]
            vext = bf(A[4])[:, 8192:8192 + 16 * 65]
            e1 = bf(A[2])[:, 8192:16384]
            te_pv_prev = None
            pending_tail = [None]

            def flush_tail():
                if pending_tail[0] is None:
                    return
                h_, qg_, evac_mk = pending_tail[0]
                pending_tail[0] = None
                p.wait("te", "se", evac_mk)
                for sub in range(4):
                    t = qg_ * 4 + sub
                    p.te(lambda e, sub=sub: e.transpose(
                        out=PS[7].bitcast(BF16)[:, 0:65],
                        in_=SMB[0:65, sub * 128:(sub + 1) * 128],
                        identity=ident[0:65, 0:65]), tick=1)
                    p.wait("ve", "te")
                    p.ve(lambda e: e.reciprocal(out=SM[:, 896:897],
                                                in_=PS[7].bitcast(BF16)[:, 64:65]), drain=True)
                    p.ve(lambda e, t=t, h_=h_: e.scalar_tensor_tensor(
                        out=R_tile(t)[:, h_ * 64:(h_ + 1) * 64],
                        in0=PS[7].bitcast(BF16)[:, 0:64], scalar=SM[:, 896:897],
                        in1=R_tile(t)[:, h_ * 64:(h_ + 1) * 64],
                        op0=ALU.mult, op1=ALU.add), tick=1)
                    p.wait("te", "ve")

            for h in range(16):
                g = h // 4
                qc, qr = divmod(h, 2)
                if h % 4 == 0:
                    flush_tail()
                    p.wait("ve", "te")
                    for t in range(NT):
                        p.ve(lambda e, t=t, g=g: e.tensor_copy(
                            out=vext[:, t * 65:t * 65 + 64],
                            in_=v_tok[:, t * 256 + g * 64:t * 256 + (g + 1) * 64]))
                    p.ve(lambda e: e.memset(
                        bf(A[4])[:, 8192:8192 + 1040].rearrange("p (t c) -> p t c", c=65)[:, :, 64:65], 1.0),
                        tick=1)
                    p.wait("te", "ve")
                for qg in range(4):
                    se_marks = {}
                    te_smarks = {}
                    te_pv = {}
                    groups = [list(range(grp * 3, min(grp * 3 + 3, 16))) for grp in range(6)]

                    def emit_scores(grp):
                        base = 0 if grp % 2 == 0 else 3
                        for j, kt in enumerate(groups[grp]):
                            p.te(lambda e, j=j, kt=kt, base=base, g=g, qr=qr, qc=qc, qg=qg: e.matmul(
                                out=PS[base + j],
                                lhsT=kT2[qr * 64:qr * 64 + 64, g * 2048 + kt * 128:g * 2048 + (kt + 1) * 128],
                                rhs=qT[qr * 64:qr * 64 + 64, qc * 2048 + qg * 512:qc * 2048 + (qg + 1) * 512],
                                start=True, stop=True, skip_group_check=True))
                        p.tick("te")
                        te_smarks[grp] = p.n("te")

                    def emit_exps(grp):
                        base = 0 if grp % 2 == 0 else 3
                        p.wait("se", "te", te_smarks[grp])
                        if te_pv_prev is not None:
                            p.wait("se", "te", te_pv_prev[grp])
                        for j, kt in enumerate(groups[grp]):
                            p.se(lambda e, j=j, kt=kt, base=base, h=h: e.activation(
                                out=e1[:, kt * 512:(kt + 1) * 512], in_=PS[base + j],
                                func=AF.Exp, bias=SM[:, 880 + h:881 + h], scale=0.125), tick=1)
                        se_marks[grp] = p.n("se")

                    def emit_pv(grp):
                        p.wait("te", "se", se_marks[grp])
                        for kt in groups[grp]:
                            p.te(lambda e, kt=kt: e.matmul(
                                out=PS[6][0:65, :],
                                lhsT=vext[:, kt * 65:(kt + 1) * 65],
                                rhs=e1[:, kt * 512:(kt + 1) * 512],
                                start=(kt == 0), stop=(kt == 15), skip_group_check=True))
                        p.tick("te")
                        te_pv[grp] = p.n("te")

                    emit_scores(0)
                    emit_exps(0)
                    emit_scores(1)
                    emit_exps(1)
                    flush_tail()
                    for grp in range(2, 6):
                        emit_pv(grp - 2)
                        emit_scores(grp)
                        emit_exps(grp)
                    emit_pv(4)
                    emit_pv(5)
                    te_pv_prev = te_pv
                    p.wait("se", "te")
                    p.se(lambda e: e.activation(out=SMB[0:65, 0:512], in_=PS[6][0:65, :],
                                                func=AF.Identity), tick=1)
                    evac_mk = p.n("se")
                    pending_tail[0] = (h, qg, evac_mk)
            flush_tail()
            p.barrier()

        def emit_attn_a():
            emit_ln(lambda t: bf(A[2])[:, t * 1024:(t + 1) * 1024])
            p.wait("gp", "ve")
            for t in range(NT):
                p.dma("gp", lambda e, t=t: e.dma_start(out=h_s[t * 128:(t + 1) * 128, :],
                                                       in_=bf(A[2])[:, t * 1024:(t + 1) * 1024]))
            for t in range(NT):
                p.dma("sy", lambda e, t=t: e.dma_start(out=R_s[t * 128:(t + 1) * 128, :],
                                                       in_=R_tile(t)))
            p.barrier()
            trload_fm(A[3], h_s, NC)
            p.barrier()
            h_fm = bf(A[3])

            pw = [p.xsem("apw0"), p.xsem("apw1")]
            jobs_a = [(aW["Wq"], A[0]), (aW["Wk"], A[1]), (aW["Wv"], A[2]), (agW1, A[4])]
            pq = [(w, d, mc) for (w, d) in jobs_a for mc in range(NC)]
            pwmk = {}
            ptemk = {}

            def _pload(gi):
                w_dram, _, mc = pq[gi]
                if gi >= 2:
                    p.wait("gp", "te", ptemk[gi - 2])
                half = (gi % 2) * 1024
                for kc in range(NC):
                    p.dmax("gp", lambda e, kc=kc, mc=mc, half=half, w_dram=w_dram: e.dma_start(
                        out=SMB[:, half + kc * 128:half + (kc + 1) * 128],
                        in_=w_dram[kc * 128:(kc + 1) * 128, mc * 128:(mc + 1) * 128]), pw[gi % 2])
                pwmk[gi] = p.xn(pw[gi % 2])

            _pload(0)
            _pload(1)
            for gi, (w_dram, dst_arena, mc) in enumerate(pq):
                half = (gi % 2) * 1024
                p.xwait("te", pw[gi % 2], pwmk[gi])
                se_marks = {}
                for nt4 in range(4):
                    if nt4 >= 2:
                        p.wait("te", "se", se_marks[nt4 - 2])
                    for kc in range(NC):
                        p.te(lambda e, kc=kc, nt4=nt4, half=half: e.matmul(
                            out=PS[nt4 % 2],
                            lhsT=SMB[:, half + kc * 128:half + (kc + 1) * 128],
                            rhs=h_fm[:, kc * 2048 + nt4 * 512:kc * 2048 + (nt4 + 1) * 512],
                            start=(kc == 0), stop=(kc == NC - 1)), tick=1 if kc == NC - 1 else 0)
                    p.wait("se", "te")
                    p.se(lambda e, mc=mc, nt4=nt4, dst_arena=dst_arena: e.activation(
                        out=bf(dst_arena)[:, mc * 2048 + nt4 * 512:mc * 2048 + (nt4 + 1) * 512],
                        in_=PS[nt4 % 2], func=AF.Identity), tick=1)
                    se_marks[nt4] = p.n("se")
                ptemk[gi] = p.n("te")
                if gi + 2 < len(pq):
                    _pload(gi + 2)
            p.wait("gp", "se")
            for c in range(NC):
                p.dma("gp", lambda e, c=c: e.dma_start(out=base_s[c * 128:(c + 1) * 128, :],
                                                       in_=bf(A[4])[:, c * 2048:(c + 1) * 2048]))
            p.wait("gp", "te")
            for kc in range(NC):
                p.dma("gp", lambda e, kc=kc: e.dma_start(out=SMB[0:128, 1024 + kc * 4:1024 + (kc + 1) * 4],
                                                         in_=aWb[kc * 128:(kc + 1) * 128, :]))
            p.wait("te", "gp")
            for nt4 in range(4):
                for kc in range(NC):
                    p.te(lambda e, kc=kc, nt4=nt4: e.matmul(
                        out=PS[2][0:4, 0:512], lhsT=SMB[:, 1024 + kc * 4:1024 + (kc + 1) * 4],
                        rhs=h_fm[:, kc * 2048 + nt4 * 512:kc * 2048 + (nt4 + 1) * 512],
                        start=(kc == 0), stop=(kc == NC - 1)), tick=1 if kc == NC - 1 else 0)
                p.wait("se", "te")
                p.se(lambda e, nt4=nt4: e.activation(out=LOWF[0:4, nt4 * 512:(nt4 + 1) * 512],
                                                     in_=PS[2][0:4, 0:512], func=AF.Sigmoid), tick=1)
                p.wait("te", "se")
            p.wait("gp", "se")
            for kc in range(NC):
                p.dma("gp", lambda e, kc=kc: e.dma_start(out=bf(A[4])[:, kc * 1024:(kc + 1) * 1024],
                                                         in_=aW["Wid"][kc * 128:(kc + 1) * 128, :]))
            p.wait("te", "gp")
            for t in range(NT):
                for og in range(2):
                    if t * 2 + og >= 4:
                        p.wait("te", "gp")
                    for kc in range(NC):
                        p.te(lambda e, t=t, og=og, kc=kc: e.matmul(
                            out=PS[4 + (t * 2 + og) % 4],
                            lhsT=h_fm[:, kc * 2048 + t * 128:kc * 2048 + (t + 1) * 128],
                            rhs=bf(A[4])[:, kc * 1024 + og * 512:kc * 1024 + (og + 1) * 512],
                            start=(kc == 0), stop=(kc == NC - 1)), tick=1 if kc == NC - 1 else 0)
                    p.wait("se", "te")
                    p.se(lambda e, t=t, og=og: e.activation(
                        out=SMB[:, og * 512:(og + 1) * 512], in_=PS[4 + (t * 2 + og) % 4],
                        func=AF.Identity), tick=1)
                p.wait("gp", "se")
                p.dma("gp", lambda e, t=t: e.dma_start(out=id_s[t * 128:(t + 1) * 128, :],
                                                       in_=SMB[:, 0:1024]))
                p.wait("se", "gp")
            p.barrier()
            p.ve(lambda e: e.tensor_copy(out=LOWB[0:4, 0:2048], in_=LOWF[0:4, 0:2048]), tick=1)
            p.wait("te", "ve")
            for t in range(NT):
                p.te(lambda e, t=t: e.transpose(out=PS[5].bitcast(BF16)[:, 0:4],
                                                in_=LOWB[0:4, t * 128:(t + 1) * 128],
                                                identity=ident[0:4, 0:4]), tick=1)
                p.wait("ve", "te")
                p.ve(lambda e, t=t: e.tensor_copy(out=SM[:, 256 + 4 * t:256 + 4 * t + 4],
                                                  in_=PS[5].bitcast(BF16)[:, 0:4]), tick=1)
                p.wait("te", "ve")
            p.dma("sy", lambda e: e.dma_start(out=SM[:, 320:352].rearrange("p (c k) -> p c k", k=4),
                                              in_=acw["q"][:].rearrange("(c p) k -> p c k", p=128),
                                              allow_slow_non_contiguous=True))
            p.dma("sy", lambda e: e.dma_start(out=SM[:, 352:384].rearrange("p (c k) -> p c k", k=4),
                                              in_=acw["k"][:].rearrange("(c p) k -> p c k", p=128),
                                              allow_slow_non_contiguous=True))
            p.dma("sy", lambda e: e.dma_start(out=SM[:, 384:416].rearrange("p (c k) -> p c k", k=4),
                                              in_=acw["v"][:].rearrange("(c p) k -> p c k", p=128),
                                              allow_slow_non_contiguous=True))
            p.dma("sy", lambda e: e.dma_start(out=SM[:, 416:440].rearrange("p (c k) -> p c k", k=3),
                                              in_=afirs[:].rearrange("(c p) k -> p c k", p=128),
                                              allow_slow_non_contiguous=True))
            p.dma("sy", lambda e: e.dma_start(out=SM[:, 440:944].rearrange("p (c k) -> p c k", k=63),
                                              in_=afirl[:].rearrange("(c p) k -> p c k", p=128),
                                              allow_slow_non_contiguous=True))
            p.barrier()

            def conv4(arena, wbase, st=[{}, {}, {}, 0]):
                dmk, tmk_c, smk_c = st[0], st[1], st[2]
                psets = [[PS47[n][:] for n in range(4)], [PS[n] for n in range(4)]]
                for c in range(NC):
                    gi = st[3]
                    row = bf(arena)[:, c * 2048:(c + 1) * 2048]
                    dslot = WK2[:].bitcast(BF16)[:, (gi % 2) * 512:(gi % 2) * 512 + 512]
                    pset = psets[gi % 2]
                    if gi >= 2:
                        p.wait("ve", "te", tmk_c[gi - 2])
                    for j in range(4):
                        wc = SM[:, wbase + c * 4 + j:wbase + c * 4 + j + 1]
                        p.ve(lambda e, j=j, wc=wc, dslot=dslot: e.tensor_scalar(
                            out=dslot[:, j * 128:(j + 1) * 128], in0=ident, scalar1=wc,
                            scalar2=None, op0=ALU.mult), tick=1 if j == 3 else 0)
                    dmk[gi] = p.n("ve")
                    p.wait("te", "ve", dmk[gi])
                    if gi >= 2:
                        p.wait("te", "se", smk_c[gi - 2])
                    for j in range(4):
                        s = 3 - j
                        for n in range(4):
                            lo = max(512 * n, s)
                            hi = 512 * n + 512
                            p.te(lambda e, j=j, n=n, lo=lo, hi=hi, s=s, row=row,
                                 dslot=dslot, pset=pset: e.matmul(
                                out=pset[n][:, lo - 512 * n:hi - 512 * n],
                                lhsT=dslot[:, j * 128:(j + 1) * 128], rhs=row[:, lo - s:hi - s],
                                start=(j == 0), stop=(j == 3), skip_group_check=True))
                        p.tick("te")
                    tmk_c[gi] = p.n("te")
                    p.wait("se", "te", tmk_c[gi])
                    for n in range(4):
                        p.se(lambda e, n=n, row=row, pset=pset: e.activation(
                            out=row[:, n * 512:(n + 1) * 512], in_=pset[n], func=AF.Silu),
                            tick=1 if n == 3 else 0)
                    smk_c[gi] = p.n("se")
                    st[3] += 1

            conv4(A[0], 320)
            conv4(A[1], 352)
            conv4(A[2], 384)
            p.barrier()

            def l2norm(arena, fr):
                for c in range(NC):
                    row = bf(arena)[:, c * 2048:(c + 1) * 2048]
                    p.wait("ve", "te")
                    p.ve(lambda e, row=row: e.tensor_mul(out=SMB[:, 0:2048], in0=row, in1=row), tick=1)
                    p.wait("te", "ve")
                    for n4 in range(4):
                        p.te(lambda e, c=c, n4=n4: e.matmul(
                            out=PS[4 + n4][0:4, 0:512],
                            lhsT=SEL12[:, (c * 3) * 12:(c * 3) * 12 + 4],
                            rhs=SMB[:, n4 * 512:(n4 + 1) * 512],
                            start=(c == 0), stop=(c == NC - 1), skip_group_check=True),
                            tick=1 if n4 == 3 else 0)
                p.wait("se", "te")
                p.wait("se", "sy")
                for n4 in range(4):
                    p.se(lambda e, n4=n4: e.activation(
                        out=LOWF[0:4, n4 * 512:(n4 + 1) * 512],
                        in_=PS[4 + n4][0:4, 0:512], func=AF.Sqrt, bias=SM[0:4, 1009:1010]), tick=1)
                p.wait("ve", "se")
                p.ve(lambda e: e.reciprocal(out=LOWF[4:8, 0:2048][0:4, :] if False else WK2[0:4, 0:2048],
                                            in_=LOWF[0:4, 0:2048]), tick=1)
                p.wait("sy", "ve")
                base = 0 if fr == 4 else 4
                p.dma("sy", lambda e, base=base: e.dma_start(
                    out=rstd_s[base:base + 4, :], in_=WK2[0:4, 0:2048]))
                p.wait("sy", "sy")
                xb = [p.xsem("l2b0"), p.xsem("l2b1")]
                bufs2 = [WK2[:, 0:2048], WK[:, 0:2048]]
                bmk = {}
                vemk2 = {}

                def _lb(hh):
                    if hh >= 2:
                        p.wait("sy", "ve", vemk2[hh - 2])
                    p.dmax("sy", lambda e, hh=hh, base=base: e.dma_start(
                        out=bufs2[hh % 2],
                        in_=rstd_s[base + hh:base + hh + 1, :].partition_broadcast(128)),
                        xb[hh % 2])
                    bmk[hh] = p.xn(xb[hh % 2])

                _lb(0)
                _lb(1)
                for hh in range(4):
                    p.xwait("ve", xb[hh % 2], bmk[hh])
                    for sub in range(2):
                        c = hh * 2 + sub
                        row = bf(arena)[:, c * 2048:(c + 1) * 2048]
                        p.ve(lambda e, row=row, hh=hh: e.tensor_mul(
                            out=row, in0=row, in1=bufs2[hh % 2]), tick=1 if sub == 1 else 0)
                    vemk2[hh] = p.n("ve")
                    if hh + 2 < 4:
                        _lb(hh + 2)

            l2norm(A[0], 4)
            l2norm(A[1], 12)
            p.barrier()

            JV = 23
            wb3, wb63 = 416, 440
            acc32 = WK[:, 0:2048]
            acc3 = WK[:, 2048:4096]
            acc8 = SMB[:, 0:2048]
            dmark = {}
            te_fir_mark = {}

            def ve_taps(K, wbase, taps, c, acc, dst=None):
                vrow = bf(A[2])[:, c * 2048:(c + 1) * 2048]
                ng = 0
                for g0 in range(0, len(taps), 8):
                    grp = taps[g0:g0 + 8]
                    for idx, j in enumerate(grp):
                        s = K - 1 - j
                        wc = SM[:, wbase + c * K + j:wbase + c * K + j + 1]
                        if idx == 0:
                            if s:
                                p.ve(lambda e, s=s: e.memset(acc8[:, 0:s], 0.0), drain=True)
                            p.ve(lambda e, vrow=vrow, wc=wc, s=s: e.tensor_scalar(
                                out=acc8[:, s:2048], in0=vrow[:, 0:2048 - s], scalar1=wc,
                                scalar2=None, op0=ALU.mult))
                        elif s == 0:
                            p.ve(lambda e, vrow=vrow, wc=wc: e.scalar_tensor_tensor(
                                out=acc8, in0=vrow, scalar=wc, in1=acc8,
                                op0=ALU.mult, op1=ALU.add))
                        else:
                            p.ve(lambda e, vrow=vrow, wc=wc, s=s: e.scalar_tensor_tensor(
                                out=acc8[:, s:2048], in0=vrow[:, 0:2048 - s], scalar=wc,
                                in1=acc8[:, s:2048], op0=ALU.mult, op1=ALU.add))
                    if ng == 0:
                        p.ve(lambda e, acc=acc: e.tensor_copy(out=acc, in_=acc8))
                    else:
                        p.ve(lambda e, acc=acc: e.tensor_add(out=acc, in0=acc, in1=acc8))
                    ng += 1
                if dst is not None:
                    p.ve(lambda e, c=c, dst=dst, acc=acc: e.tensor_copy(
                        out=bf(dst)[:, c * 2048:(c + 1) * 2048], in_=acc), tick=1)

            diagbuf = WK2[:].bitcast(BF16)
            for c in range(NC):
                vrow = bf(A[2])[:, c * 2048:(c + 1) * 2048]
                if c > 0:
                    p.wait("ve", "te", te_fir_mark[c - 1])
                nte = 63 - JV
                for i, j in enumerate(range(JV, 63)):
                    wc = SM[:, wb63 + c * 63 + j:wb63 + c * 63 + j + 1]
                    dst = diagbuf[:, i * 128:(i + 1) * 128] if i < 32 \
                        else STK[:, (i - 32) * 128:(i - 31) * 128]
                    p.ve(lambda e, dst=dst, wc=wc: e.tensor_scalar(
                        out=dst, in0=ident, scalar1=wc,
                        scalar2=None, op0=ALU.mult), tick=1 if i == nte - 1 else 0)
                dmark[c] = p.n("ve")
                p.wait("te", "ve", dmark[c])
                for i, j in enumerate(range(JV, 63)):
                    s = 62 - j
                    dg = diagbuf[:, i * 128:(i + 1) * 128] if i < 32 \
                        else STK[:, (i - 32) * 128:(i - 31) * 128]
                    for n in range(4):
                        lo = max(512 * n, s)
                        hi = 512 * n + 512
                        p.te(lambda e, dg=dg, vrow=vrow, n=n, lo=lo, hi=hi, s=s, i=i: e.matmul(
                            out=PS47[n][:, lo - 512 * n:hi - 512 * n],
                            lhsT=dg, rhs=vrow[:, lo - s:hi - s],
                            start=(i == 0), stop=(i == nte - 1), skip_group_check=True))
                    p.tick("te")
                te_fir_mark[c] = p.n("te")
                ve_taps(3, wb3, [0, 1, 2], c, acc3, dst=A[4])
                ve_taps(63, wb63, list(range(JV)), c, acc32, dst=None)
                p.wait("ve", "te", te_fir_mark[c])
                for n in range(4):
                    p.ve(lambda e, c=c, n=n: e.tensor_add(
                        out=bf(A[3])[:, c * 2048 + n * 512:c * 2048 + (n + 1) * 512],
                        in0=acc32[:, n * 512:(n + 1) * 512], in1=PS47[n][:]),
                        tick=1 if n == 3 else 0)
            p.barrier()

            def branch_sums(arena, bidx, first, last):
                for c in range(NC):
                    row = bf(arena)[:, c * 2048:(c + 1) * 2048]
                    for n4 in range(4):
                        p.te(lambda e, c=c, n4=n4, row=row: e.matmul(
                            out=PST[0:12, n4 * 512:(n4 + 1) * 512],
                            lhsT=SEL12[:, (c * 3 + 0) * 12:(c * 3 + 0) * 12 + 12],
                            rhs=row[:, n4 * 512:(n4 + 1) * 512],
                            start=(first and c == 0), stop=False, skip_group_check=True),
                            tick=1 if n4 == 3 else 0)
                    p.wait("se", "te")
                    p.wait("se", "ve")
                    p.se(lambda e, row=row: e.activation(out=SMB[:, 0:2048], in_=row, func=AF.Abs), tick=1)
                    p.wait("te", "se")
                    for n4 in range(4):
                        p.te(lambda e, c=c, n4=n4: e.matmul(
                            out=PST[0:12, n4 * 512:(n4 + 1) * 512],
                            lhsT=SEL12[:, (c * 3 + 1) * 12:(c * 3 + 1) * 12 + 12],
                            rhs=SMB[:, n4 * 512:(n4 + 1) * 512],
                            start=False, stop=False, skip_group_check=True),
                            tick=1 if n4 == 3 else 0)
                    p.wait("ve", "te")
                    p.ve(lambda e, row=row: e.tensor_mul(out=SMB[:, 0:2048], in0=row, in1=row), tick=1)
                    p.wait("te", "ve")
                    for n4 in range(4):
                        p.te(lambda e, c=c, n4=n4: e.matmul(
                            out=PST[0:12, n4 * 512:(n4 + 1) * 512],
                            lhsT=SEL12[:, (c * 3 + 2) * 12:(c * 3 + 2) * 12 + 12],
                            rhs=SMB[:, n4 * 512:(n4 + 1) * 512],
                            start=False, stop=(last and c == NC - 1), skip_group_check=True),
                            tick=1 if n4 == 3 else 0)
                p.wait("ve", "te")
                p.ve(lambda e: e.tensor_copy(out=LOWB[0:12, 0:2048], in_=PST[0:12, 0:2048]), tick=1)
                p.wait("te", "ve")
                for t in range(NT):
                    p.te(lambda e, t=t: e.transpose(out=PS[6].bitcast(BF16)[:, 0:12],
                                                    in_=LOWB[0:12, t * 128:(t + 1) * 128],
                                                    identity=ident[0:12, 0:12]), tick=1)
                    p.wait("ve", "te")
                    p.ve(lambda e, t=t: e.tensor_copy(out=WK2[:, t * 48:t * 48 + 12],
                                                      in_=PS[6].bitcast(BF16)[:, 0:12]), tick=1)
                    p.wait("te", "ve")
                W3 = WK2[:, 0:768].rearrange("p (t c) -> p t c", c=48)
                p.wait("se", "ve")
                p.se(lambda e: e.activation(out=W3[:, :, 16:20], in_=W3[:, :, 8:12], func=AF.Sqrt), tick=1)
                p.wait("ve", "se")
                p.ve(lambda e: e.tensor_scalar(out=W3[:, :, 20:28], in0=W3[:, :, 0:8],
                                               scalar1=1.0 / 256, scalar2=None, op0=ALU.mult), drain=True)
                p.ve(lambda e: e.tensor_mul(out=W3[:, :, 28:32], in0=W3[:, :, 20:24],
                                            in1=W3[:, :, 20:24]), drain=True)
                p.ve(lambda e: e.tensor_scalar(out=W3[:, :, 32:36], in0=W3[:, :, 8:12],
                                               scalar1=1.0 / 256, scalar2=None, op0=ALU.mult), drain=True)
                p.ve(lambda e: e.tensor_sub(out=W3[:, :, 36:40], in0=W3[:, :, 32:36],
                                            in1=W3[:, :, 28:32]), drain=True)
                for t in range(NT):
                    for s, col in ((0, 20), (1, 36), (2, 24), (3, 16)):
                        cc = bidx * 4 + s
                        p.ve(lambda e, t=t, cc=cc, col=col: e.tensor_copy(
                            out=STK[:, t * 64:(t + 1) * 64].rearrange("p (h x) -> p h x", x=16)[:, :, cc:cc + 1],
                            in_=WK2[:, t * 48 + col:t * 48 + col + 4].unsqueeze(2)),
                            tick=1 if (t == NT - 1 and s == 3) else 0)

            xsp = p.xsem("a3sp")
            p.wait("gp", "ve")
            p.wait("sy", "ve")
            for c in range(NC):
                p.dmax("gp", lambda e, c=c: e.dma_start(out=ls_s[c * 128:(c + 1) * 128, :],
                                                        in_=bf(A[4])[:, c * 2048:(c + 1) * 2048]), xsp)
                p.dmax("gp", lambda e, c=c: e.dma_start(out=ll_s[c * 128:(c + 1) * 128, :],
                                                        in_=bf(A[3])[:, c * 2048:(c + 1) * 2048]), xsp)
            for c in range(NC):
                p.dmax("sy", lambda e, c=c: e.dma_start(out=k_s[c * 128:(c + 1) * 128, :],
                                                        in_=bf(A[1])[:, c * 2048:(c + 1) * 2048]), xsp)
                p.dmax("sy", lambda e, c=c: e.dma_start(out=v_s[c * 128:(c + 1) * 128, :],
                                                        in_=bf(A[2])[:, c * 2048:(c + 1) * 2048]), xsp)
            spmk = p.xn(xsp)
            branch_sums(A[4], 0, True, True)
            p.xwait("sy", xsp, spmk)
            p.wait("sy", "te")
            p.wait("sy", "ve")
            p.wait("sy", "se")
            trload_tok(A[4], k_s, 1024)
            branch_sums(A[2], 3, True, True)
            p.wait("sy", "te")
            p.wait("sy", "ve")
            p.wait("sy", "se")
            trload_tok(A[2], v_s, 1024)
            branch_sums(A[3], 1, True, True)
            p.barrier()
            k_tok = bf(A[4])
            v_tok = bf(A[2])
            qhat = bf(A[0])
            khat = bf(A[1])

            u0, w0, a0, x0 = 0, 4096, 8192, 10240

            def a4a_gen(c, hh, slot, pA, pB, pC, xoff, wkoff):
                q2 = (2 * hh, 2 * hh + 1)
                coff = c * 128
                P_sb = bf(A[3])[:, xoff + 0:xoff + 128]
                PT_sb = bf(A[3])[:, xoff + 128:xoff + 256]
                E_sb = bf(A[3])[:, xoff + 256:xoff + 384]
                EI_sb = bf(A[3])[:, xoff + 384:xoff + 512]
                smb = (slot % 2) * 512
                wkp = WK2[:, wkoff:wkoff + 128]
                bcol = SM[:, 256 + 4 * c + hh:256 + 4 * c + hh + 1]
                for i, fc in enumerate(q2):
                    p.te(lambda e, fc=fc, coff=coff, i=i: e.matmul(
                        out=pA[:, 0:128], lhsT=khat[:, fc * 2048 + coff:fc * 2048 + coff + 128],
                        rhs=khat[:, fc * 2048 + coff:fc * 2048 + coff + 128],
                        start=(i == 0), stop=(i == 1), skip_group_check=True),
                        tick=1 if i == 1 else 0)
                tmk = p.n("te")
                yield
                p.wait("ve", "te", tmk)
                p.ve(lambda e: e.tensor_scalar(out=wkp, in0=pA[:, 0:128],
                                               scalar1=bcol, scalar2=-1.0,
                                               op0=ALU.mult, op1=ALU.mult), drain=True)
                p.ve(lambda e: e.tensor_tensor(out=P_sb, in0=wkp, in1=mS,
                                               op=ALU.mult), tick=1)
                vmk = p.n("ve")
                yield
                p.wait("te", "ve", vmk)
                p.te(lambda e: e.transpose(out=pB.bitcast(BF16)[:, 0:128], in_=P_sb,
                                           identity=ident), tick=1)
                tmk = p.n("te")
                yield
                p.wait("se", "te", tmk)
                p.se(lambda e: e.activation(out=PT_sb, in_=pB.bitcast(BF16)[:, 0:128],
                                            func=AF.Identity), tick=1)
                p.se(lambda e: e.activation(out=E_sb, in_=pB.bitcast(BF16)[:, 0:128],
                                            func=AF.Identity), tick=1)
                smk = p.n("se")
                yield
                etmp = WK2[:, 256 + wkoff:256 + wkoff + 128]
                vmk = 0
                for i in range(6):
                    p.wait("te", "se", smk)
                    if vmk:
                        p.wait("te", "ve", vmk)
                    p.te(lambda e: e.matmul(out=pA[:, 0:128], lhsT=PT_sb, rhs=P_sb,
                                            start=True, stop=True, skip_group_check=True), tick=1)
                    tmk = p.n("te")
                    yield
                    p.wait("se", "te", tmk)
                    p.se(lambda e: e.activation(out=P_sb, in_=pA[:, 0:128],
                                                func=AF.Identity), tick=1)
                    smk = p.n("se")
                    yield
                    p.wait("te", "se", smk)
                    p.te(lambda e: e.transpose(out=pB.bitcast(BF16)[:, 0:128], in_=P_sb,
                                               identity=ident), tick=1)
                    p.te(lambda e: e.matmul(out=pC[:, 0:128], lhsT=P_sb, rhs=E_sb,
                                            start=True, stop=True, skip_group_check=True), tick=1)
                    tmk = p.n("te")
                    yield
                    p.wait("se", "te", tmk)
                    p.se(lambda e: e.activation(out=PT_sb, in_=pB.bitcast(BF16)[:, 0:128],
                                                func=AF.Identity), tick=1)
                    smk = p.n("se")
                    yield
                    p.wait("ve", "te", tmk)
                    p.ve(lambda e, etmp=etmp: e.tensor_add(out=etmp, in0=pC[:, 0:128], in1=E_sb))
                    p.wait("ve", "se", smk)
                    p.ve(lambda e, etmp=etmp: e.tensor_add(out=E_sb, in0=etmp, in1=PT_sb), tick=1)
                    vmk = p.n("ve")
                    yield
                p.ve(lambda e: e.tensor_tensor(out=EI_sb, in0=E_sb, in1=ident, op=ALU.add))
                uoff = u0 + slot * 256
                woff = w0 + slot * 256
                aoff = a0 + slot * 128
                p.ve(lambda e, smb=smb: e.tensor_scalar(
                    out=SMB[:, smb:smb + 256], in0=v_tok[:, c * 1024 + hh * 256:c * 1024 + (hh + 1) * 256],
                    scalar1=bcol, scalar2=None, op0=ALU.mult))
                p.ve(lambda e, smb=smb: e.tensor_scalar(
                    out=SMB[:, smb + 256:smb + 512], in0=k_tok[:, c * 1024 + hh * 256:c * 1024 + (hh + 1) * 256],
                    scalar1=bcol, scalar2=-1.0, op0=ALU.mult, op1=ALU.mult), tick=1)
                vmk = p.n("ve")
                yield
                p.wait("te", "ve", vmk)
                p.wait("te", "se", smk)
                p.te(lambda e, smb=smb: e.matmul(out=pA[:, 0:256], lhsT=EI_sb,
                                                 rhs=SMB[:, smb:smb + 256], start=True, stop=True,
                                                 skip_group_check=True), tick=1)
                for dh in range(2):
                    p.te(lambda e, dh=dh, smb=smb: e.matmul(
                        out=(pB if dh == 0 else pC)[:, 0:128],
                        lhsT=SMB[:, smb + 256 + dh * 128:smb + 256 + (dh + 1) * 128],
                        rhs=EI_sb, start=True, stop=True, skip_group_check=True),
                        tick=1 if dh == 1 else 0)
                tmk = p.n("te")
                yield
                p.wait("se", "te", tmk)
                p.se(lambda e, uoff=uoff: e.activation(out=bf(A[3])[:, uoff:uoff + 256],
                                                       in_=pA[:, 0:256], func=AF.Identity), tick=1)
                p.se(lambda e, woff=woff: e.activation(out=bf(A[3])[:, woff:woff + 128],
                                                       in_=pB[:, 0:128], func=AF.Identity), tick=1)
                p.se(lambda e, woff=woff: e.activation(out=bf(A[3])[:, woff + 128:woff + 256],
                                                       in_=pC[:, 0:128], func=AF.Identity), tick=1)
                smk = p.n("se")
                yield
                p.wait("te", "se", smk)
                for i, fc in enumerate(q2):
                    p.te(lambda e, fc=fc, coff=coff, i=i: e.matmul(
                        out=pA[:, 0:128], lhsT=khat[:, fc * 2048 + coff:fc * 2048 + coff + 128],
                        rhs=qhat[:, fc * 2048 + coff:fc * 2048 + coff + 128],
                        start=(i == 0), stop=(i == 1), skip_group_check=True),
                        tick=1 if i == 1 else 0)
                tmk = p.n("te")
                yield
                p.wait("ve", "te", tmk)
                p.ve(lambda e, aoff=aoff: e.tensor_tensor(
                    out=bf(A[3])[:, aoff:aoff + 128], in0=pA[:, 0:128], in1=mUin,
                    op=ALU.mult), tick=1)

            def a4b_gen(c, hh, slot, first_chunk, pU, pO, pS, ueoff, marks):
                uoff = u0 + slot * 256
                woff = w0 + slot * 256
                aoff = a0 + slot * 128
                q2 = (2 * hh, 2 * hh + 1)
                coff = c * 128
                Ssb = SMB[:, (hh % 2) * 512:(hh % 2) * 512 + 512]
                Sf = WK[:, 2048 + (hh % 2) * 512:2048 + (hh % 2) * 512 + 512]
                if marks:
                    p.wait("te", "ve", marks["uvmk"])
                    p.wait("te", "se", marks["smk"])
                    p.wait("te", "ve", marks["svmk"])
                p.te(lambda e, uoff=uoff, first_chunk=first_chunk: e.matmul(
                    out=pU[:, 0:256], lhsT=ident, rhs=bf(A[3])[:, uoff:uoff + 256],
                    start=True, stop=first_chunk, skip_group_check=True),
                    tick=1 if first_chunk else 0)
                if not first_chunk:
                    for dh in range(2):
                        p.te(lambda e, woff=woff, dh=dh, Ssb=Ssb: e.matmul(
                            out=pU[:, 0:256],
                            lhsT=bf(A[3])[:, woff + dh * 128:woff + (dh + 1) * 128],
                            rhs=Ssb[:, dh * 256:(dh + 1) * 256],
                            start=False, stop=(dh == 1), skip_group_check=True),
                            tick=1 if dh == 1 else 0)
                tmk = p.n("te")
                yield
                p.wait("ve", "te", tmk)
                p.ve(lambda e, ueoff=ueoff: e.tensor_copy(out=SMB[:, ueoff:ueoff + 256],
                                                          in_=pU[:, 0:256]), tick=1)
                marks["uvmk"] = p.n("ve")
                yield
                p.wait("te", "ve", marks["uvmk"])
                if not first_chunk:
                    for i, fc in enumerate(q2):
                        p.te(lambda e, fc=fc, coff=coff, Ssb=Ssb, i=i: e.matmul(
                            out=pO[:, 0:256], lhsT=qhat[:, fc * 2048 + coff:fc * 2048 + coff + 128],
                            rhs=Ssb[:, i * 256:(i + 1) * 256], start=(i == 0), stop=False,
                            skip_group_check=True))
                p.te(lambda e, aoff=aoff, first_chunk=first_chunk, ueoff=ueoff: e.matmul(
                    out=pO[:, 0:256], lhsT=bf(A[3])[:, aoff:aoff + 128],
                    rhs=SMB[:, ueoff:ueoff + 256], start=first_chunk, stop=True,
                    skip_group_check=True), tick=1)
                tmkO = p.n("te")
                for dh in range(2):
                    p.te(lambda e, c=c, hh=hh, dh=dh, ueoff=ueoff: e.matmul(
                        out=pS[:, dh * 256:(dh + 1) * 256],
                        lhsT=k_tok[:, c * 1024 + hh * 256 + dh * 128:c * 1024 + hh * 256 + (dh + 1) * 128],
                        rhs=SMB[:, ueoff:ueoff + 256],
                        start=(dh == 0), stop=(dh == 1), skip_group_check=True),
                        tick=1 if dh == 1 else 0)
                tmkS = p.n("te")
                yield
                p.wait("se", "te", tmkO)
                p.se(lambda e, hh=hh: e.activation(
                    out=SMB[:, 1536 + (hh % 2) * 256:1536 + (hh % 2) * 256 + 256],
                    in_=pO[:, 0:256], func=AF.Identity), tick=1)
                marks["smk"] = p.n("se")
                yield
                p.wait("ve", "te", tmkS)
                if first_chunk:
                    p.ve(lambda e, Sf=Sf: e.tensor_copy(out=Sf, in_=pS[:]))
                else:
                    p.ve(lambda e, Sf=Sf: e.tensor_add(out=Sf, in0=Sf, in1=pS[:]))
                p.ve(lambda e, Ssb=Ssb, Sf=Sf: e.tensor_copy(out=Ssb, in_=Sf), tick=1)
                marks["svmk"] = p.n("ve")

            def drive(gens, seq=False, stagger=0):
                if seq:
                    for g in gens:
                        for _ in g:
                            pass
                    return
                alive = [True] * len(gens)
                for _ in range(stagger):
                    try:
                        next(gens[0])
                    except StopIteration:
                        alive[0] = False
                while any(alive):
                    for i, g in enumerate(gens):
                        if alive[i]:
                            try:
                                next(g)
                            except StopIteration:
                                alive[i] = False

            for pair in range(2):
                h0 = 2 * pair
                for half in range(2):
                    for c8 in range(8):
                        c = half * 8 + c8
                        drive([a4a_gen(c, h0, c8 * 2, PS[0], PS[1], PS[2], 10240, 0),
                               a4a_gen(c, h0 + 1, c8 * 2 + 1, PS[6], PS[7], PS[3], 10752, 128)],
                              seq=A4A_SEQ, stagger=2)
                    p.barrier()
                    if half == 1:
                        for hh2 in range(2):
                            p.ve(lambda e, hh2=hh2: e.tensor_copy(
                                out=SMB[:, hh2 * 512:(hh2 + 1) * 512],
                                in_=WK[:, 2048 + hh2 * 512:2048 + (hh2 + 1) * 512]),
                                tick=1 if hh2 == 1 else 0)
                        p.wait("te", "ve")
                    bmarks = [{}, {}]
                    for c8 in range(8):
                        c = half * 8 + c8
                        drive([a4b_gen(c, h0, c8 * 2, c == 0, PS[3], PS[5], PS[4],
                                       1024, bmarks[0]),
                               a4b_gen(c, h0 + 1, c8 * 2 + 1, c == 0, PS[0], PS[1], PS[2],
                                       1280, bmarks[1])], seq=A4B_SEQ)
                        p.wait("gp", "se")
                        p.dma("gp", lambda e, c=c, pair=pair: e.dma_start(
                            out=delta_s[c * 128:(c + 1) * 128, pair * 512:(pair + 1) * 512],
                            in_=SMB[:, 1536:2048]))
                        p.wait("se", "gp")
                    p.barrier()
            p.barrier()

            trload_fm(A[1], delta_s, NC)
            p.barrier()
            branch_sums(A[1], 2, True, True)
            p.barrier()
            if _dbg:
                p.dma("sy", lambda e: e.dma_start(out=stk_s[:, :], in_=STK[:, :]))
                p.barrier()

            p.dma("gp", lambda e: e.dma_start(out=SMB[0:16, 1024:2048], in_=agW1[1024:1040, :]))
            p.dma("gp", lambda e: e.dma_start(out=CS2[:, 130:162].rearrange("p (c o) -> p c o", o=4),
                                              in_=agW2[:].rearrange("(c p) o -> p c o", p=128),
                                              allow_slow_non_contiguous=True))
            p.dma("sy", lambda e: e.dma_start(out=SM[:, 320:328],
                                              in_=agb1[:].rearrange("(c p) o -> p (c o)", p=128),
                                              allow_slow_non_contiguous=True))
            p.dma("sy", lambda e: e.dma_start(out=SM[0:1, 928:944], in_=alt[:]))
            p.dma("sy", lambda e: e.dma_start(out=SM[0:1, 944:960], in_=afp[:]))
            p.dma("sy", lambda e: e.dma_start(out=SM[0:4, 964:965],
                                              in_=agb2[:].rearrange("o c -> c o"),
                                              allow_slow_non_contiguous=True))
            for c in range(NC):
                p.dma("sy", lambda e, c=c: e.dma_start(out=bf(A[0])[:, c * 2048:(c + 1) * 2048],
                                                       in_=base_s[c * 128:(c + 1) * 128, :]))
            p.barrier()
            p.se(lambda e: e.activation(out=SM[0:1, 968:984], in_=SM[0:1, 928:944],
                                        func=AF.Exp, scale=-1.0), tick=1)
            p.se(lambda e: e.activation(out=SM[0:1, 984:1000], in_=SM[0:1, 944:960],
                                        func=AF.Sigmoid), tick=1)
            p.wait("ve", "se")
            p.ve(lambda e: e.tensor_scalar(out=SM[0:1, 984:1000], in0=SM[0:1, 984:1000],
                                           scalar1=0.05, scalar2=None, op0=ALU.mult), drain=True)
            p.ve(lambda e: e.reduce_sum(out=SM[0:1, 1000:1004],
                                        in_=SM[0:1, 984:1000].rearrange("o (h c) -> o h c", h=4),
                                        axis=AX.X), drain=True)
            p.ve(lambda e: e.tensor_scalar(out=SM[0:1, 1000:1004], in0=SM[0:1, 1000:1004],
                                           scalar1=-1.0, scalar2=1.0, op0=ALU.mult, op1=ALU.add), tick=1)
            p.wait("sy", "ve")
            p.dma("sy", lambda e: e.dma_start(out=bc_s[160:176], in_=SM[0:1, 968:984]))
            p.dma("sy", lambda e: e.dma_start(out=bc_s[176:192], in_=SM[0:1, 984:1000]))
            p.dma("sy", lambda e: e.dma_start(out=bc_s[192:196], in_=SM[0:1, 1000:1004]))
            p.wait("sy", "sy")
            p.dma("sy", lambda e: e.dma_start(out=SM[0:4, 960:964],
                                              in_=bc_s[160:176].rearrange("(h c) -> c h", c=4),
                                              allow_slow_non_contiguous=True))
            p.dma("sy", lambda e: e.dma_start(
                out=SM[:, 776:792],
                in_=bc_s[176:192].rearrange("(o c) -> o c", o=1).partition_broadcast(128)))
            p.dma("sy", lambda e: e.dma_start(
                out=SM[:, 792:796],
                in_=bc_s[192:196].rearrange("(o c) -> o c", o=1).partition_broadcast(128)))
            p.barrier()
            for hh in range(4):
                p.wait("te", "ve")
                for t in range(NT):
                    p.te(lambda e, t=t, hh=hh: e.transpose(
                        out=PS[6].bitcast(BF16)[0:16, 0:128],
                        in_=STK[:, t * 64 + hh * 16:t * 64 + hh * 16 + 16],
                        identity=ident), tick=1)
                    p.wait("ve", "te")
                    p.ve(lambda e, t=t: e.tensor_copy(
                        out=LOWB[0:16, t * 128:(t + 1) * 128],
                        in_=PS[6].bitcast(BF16)[0:16, 0:128]), tick=1)
                    p.wait("te", "ve")
                se_marks = {}
                for mc in range(NC):
                    for nt4 in range(4):
                        gi = mc * 4 + nt4
                        if gi >= 2:
                            p.wait("te", "se", se_marks[gi - 2])
                        p.te(lambda e, hh=hh, mc=mc, nt4=nt4: e.matmul(
                            out=PS[nt4 % 2],
                            lhsT=SMB[0:16, 1024 + mc * 128:1024 + (mc + 1) * 128],
                            rhs=LOWB[0:16, nt4 * 512:(nt4 + 1) * 512],
                            start=True, stop=False))
                        p.te(lambda e, mc=mc, nt4=nt4: e.matmul(
                            out=PS[nt4 % 2], lhsT=ident,
                            rhs=bf(A[0])[:, mc * 2048 + nt4 * 512:mc * 2048 + (nt4 + 1) * 512],
                            start=False, stop=True), tick=1)
                        p.wait("se", "te")
                        p.se(lambda e, mc=mc, nt4=nt4: e.activation(
                            out=bf(A[4])[:, mc * 2048 + nt4 * 512:mc * 2048 + (nt4 + 1) * 512],
                            in_=PS[nt4 % 2], func=AF.Gelu,
                            bias=SM[:, 320 + mc:321 + mc]), tick=1)
                        se_marks[gi] = p.n("se")
                p.wait("te", "se")
                for nt4 in range(4):
                    for kc in range(NC):
                        p.te(lambda e, kc=kc, nt4=nt4: e.matmul(
                            out=PS[2][0:4, 0:512], lhsT=CS2[:, 130 + kc * 4:130 + (kc + 1) * 4],
                            rhs=bf(A[4])[:, kc * 2048 + nt4 * 512:kc * 2048 + (nt4 + 1) * 512],
                            start=(kc == 0), stop=(kc == NC - 1)), tick=1 if kc == NC - 1 else 0)
                    p.wait("ve", "te")
                    p.wait("ve", "gp")
                    p.ve(lambda e, hh=hh, nt4=nt4: e.tensor_scalar(
                        out=LOWF[0:4, nt4 * 512:(nt4 + 1) * 512],
                        in0=PS[2][0:4, 0:512],
                        scalar1=SM[0:4, 964:965], scalar2=SM[0:4, 960 + hh:961 + hh],
                        op0=ALU.add, op1=ALU.mult), tick=1)
                    p.wait("te", "ve")
                    p.wait("gp", "ve")
                    p.dma("gp", lambda e, hh=hh, nt4=nt4: e.dma_start(
                        out=logits_s[4 * hh:4 * hh + 4, nt4 * 512:(nt4 + 1) * 512],
                        in_=LOWF[0:4, nt4 * 512:(nt4 + 1) * 512]))
            p.wait("gp", "ve")
            p.wait("gp", "gp")
            p.dma("gp", lambda e: e.dma_start(out=LOWB[0:16, 0:2048], in_=logits_s[:]))
            p.wait("te", "gp")
            for t in range(NT):
                p.te(lambda e, t=t: e.transpose(out=PS[6].bitcast(BF16)[:, 0:16],
                                                in_=LOWB[0:16, t * 128:(t + 1) * 128],
                                                identity=ident[0:16, 0:16]), tick=1)
                p.wait("se", "te")
                p.se(lambda e, t=t: e.activation(out=WK2[:, 0:16], in_=PS[6].bitcast(BF16)[:, 0:16],
                                                 func=AF.Exp), tick=1)
                p.wait("ve", "se")
                p.ve(lambda e: e.reduce_sum(out=WK2[:, 16:20],
                                            in_=WK2[:, 0:16].rearrange("p (h c) -> p h c", h=4),
                                            axis=AX.X), drain=True)
                p.ve(lambda e: e.reciprocal(out=WK2[:, 20:24], in_=WK2[:, 16:20]), drain=True)
                p.ve(lambda e: e.tensor_mul(out=WK2[:, 20:24], in0=WK2[:, 20:24], in1=SM[:, 792:796]), drain=True)
                p.ve(lambda e, t=t: e.tensor_tensor(
                    out=WK2[:, 24:40].rearrange("p (h c) -> p h c", h=4),
                    in0=WK2[:, 0:16].rearrange("p (h c) -> p h c", h=4),
                    in1=WK2[:, 20:24].unsqueeze(2).broadcast_to([128, 4, 4]),
                    op=ALU.mult), drain=True)
                p.ve(lambda e, t=t: e.tensor_add(out=SM[:, 256 + 16 * t:256 + 16 * t + 16],
                                                 in0=WK2[:, 24:40], in1=SM[:, 776:792]),
                     tick=1)
                p.wait("te", "ve")
                p.wait("se", "ve")
            p.barrier()

            load_R_a6 = True
            for t in range(NT):
                p.dma("sy", lambda e, t=t: e.dma_start(out=R_tile(t), in_=R_s[t * 128:(t + 1) * 128, :]))
            for kc in range(NC):
                p.dma("gp", lambda e, kc=kc: e.dma_start(out=bf(A[4])[:, kc * 1024:(kc + 1) * 1024],
                                                         in_=aW["Wo"][kc * 128:(kc + 1) * 128, :]))
            p.dma("sy", lambda e: e.dma_start(out=SM[:, 512:768],
                                              in_=aonw[0:1, :].partition_broadcast(128)))
            p.dma("sy", lambda e: e.dma_start(out=SM[:, 768:772],
                                              in_=amisc[0:1, 0:4].partition_broadcast(128)))
            p.dma("sy", lambda e: e.dma_start(out=SM[0:1, 912:928], in_=amisc[:]))
            p.wait("se", "sy")
            p.se(lambda e: e.activation(out=SM[0:1, 1004:1008], in_=SM[0:1, 916:920],
                                        func=AF.Sigmoid), tick=1)
            p.wait("sy", "se")
            p.dma("sy", lambda e: e.dma_start(out=bc_s[196:200], in_=SM[0:1, 1004:1008]))
            p.wait("sy", "sy")
            p.dma("sy", lambda e: e.dma_start(
                out=SM[:, 772:776],
                in_=bc_s[196:200].rearrange("(o c) -> o c", o=1).partition_broadcast(128)))
            p.barrier()
            for t in range(NT):
                p.ve(lambda e, t=t: e.tensor_add(
                    out=SM[:, 256 + 16 * t:256 + 16 * t + 16].rearrange("p (h c) -> p h c", c=4)[:, :, 0:1],
                    in0=SM[:, 256 + 16 * t:256 + 16 * t + 16].rearrange("p (h c) -> p h c", c=4)[:, :, 0:1],
                    in1=SM[:, 772:776].unsqueeze(2)), tick=1 if t == NT - 1 else 0)
            for t in range(NT):
                p.wait("sy", "te")
                p.wait("gp", "te")
                p.dma("sy", lambda e, t=t: e.dma_start_transpose(
                    out=bf(A[3])[:, 0:1024], in_=ls_s[:, t * 128:(t + 1) * 128]))
                p.dma("sy", lambda e, t=t: e.dma_start_transpose(
                    out=bf(A[3])[:, 1024:2048], in_=ll_s[:, t * 128:(t + 1) * 128]))
                p.dma("gp", lambda e, t=t: e.dma_start(
                    out=bf(A[3])[:, 2048:3072], in_=delta_s[t * 128:(t + 1) * 128, :]))
                p.dma("gp", lambda e, t=t: e.dma_start(
                    out=bf(A[3])[:, 3072:4096], in_=id_s[t * 128:(t + 1) * 128, :]))
                p.wait("ve", "sy")
                p.wait("ve", "gp")
                pcol = lambda t, hh, cc: SM[:, 256 + 16 * t + 4 * hh + cc:256 + 16 * t + 4 * hh + cc + 1]
                for hh in range(4):
                    acc = WK[:, hh * 256:(hh + 1) * 256]
                    sl = hh * 256
                    p.ve(lambda e, t=t, hh=hh, acc=acc, sl=sl: e.tensor_scalar(
                        out=acc, in0=bf(A[3])[:, sl:sl + 256], scalar1=pcol(t, hh, 0),
                        scalar2=None, op0=ALU.mult))
                    p.ve(lambda e, t=t, hh=hh, acc=acc, sl=sl: e.scalar_tensor_tensor(
                        out=acc, in0=bf(A[3])[:, 1024 + sl:1024 + sl + 256], scalar=pcol(t, hh, 1),
                        in1=acc, op0=ALU.mult, op1=ALU.add))
                    p.ve(lambda e, t=t, hh=hh, acc=acc, sl=sl: e.scalar_tensor_tensor(
                        out=acc, in0=bf(A[3])[:, 2048 + sl:2048 + sl + 256], scalar=pcol(t, hh, 2),
                        in1=acc, op0=ALU.mult, op1=ALU.add))
                    p.ve(lambda e, t=t, hh=hh, acc=acc, sl=sl: e.scalar_tensor_tensor(
                        out=acc, in0=v_tok[:, t * 1024 + sl:t * 1024 + sl + 256], scalar=pcol(t, hh, 3),
                        in1=acc, op0=ALU.mult, op1=ALU.add))
                    p.ve(lambda e, t=t, hh=hh, acc=acc, sl=sl: e.scalar_tensor_tensor(
                        out=acc, in0=bf(A[3])[:, 3072 + sl:3072 + sl + 256],
                        scalar=SM[:, 768 + hh:769 + hh],
                        in1=acc, op0=ALU.mult, op1=ALU.add), tick=1)
                p.wait("se", "ve")
                p.se(lambda e: e.activation(out=WK2[:, 1024:2048], in_=WK[:, 0:1024],
                                            func=AF.Square), tick=1)
                p.wait("ve", "se")
                p.ve(lambda e: e.reduce_sum(out=WK2[:, 0:4],
                                            in_=WK2[:, 1024:2048].rearrange("p (h d) -> p h d", h=4),
                                            axis=AX.X), tick=1)
                p.wait("se", "ve")
                p.se(lambda e: e.activation(out=WK2[:, 4:8], in_=WK2[:, 0:4], func=AF.Sqrt,
                                            scale=1.0 / 256, bias=SM[:, 1008:1009]), tick=1)
                p.wait("ve", "se")
                p.ve(lambda e: e.reciprocal(out=WK2[:, 8:12], in_=WK2[:, 4:8]), drain=True)
                for hh in range(4):
                    p.ve(lambda e, hh=hh: e.scalar_tensor_tensor(
                        out=SMB[:, hh * 256:(hh + 1) * 256], in0=WK[:, hh * 256:(hh + 1) * 256],
                        scalar=WK2[:, 8 + hh:9 + hh], in1=SM[:, 512:768],
                        op0=ALU.mult, op1=ALU.mult), tick=1 if hh == 3 else 0)
                p.wait("te", "ve")
                for kc in range(NC):
                    p.te(lambda e, kc=kc: e.transpose(
                        out=PS[6].bitcast(BF16)[:, 0:128],
                        in_=SMB[:, kc * 128:(kc + 1) * 128], identity=ident), tick=1)
                    p.wait("ve", "te")
                    p.ve(lambda e, kc=kc: e.tensor_copy(out=SMB[:, 1024 + kc * 128:1024 + (kc + 1) * 128],
                                                        in_=PS[6].bitcast(BF16)[:, 0:128]), tick=1)
                    p.wait("te", "ve")
                for og in range(2):
                    for kc in range(NC):
                        p.te(lambda e, kc=kc, og=og: e.matmul(
                            out=PS[og],
                            lhsT=SMB[:, 1024 + kc * 128:1024 + (kc + 1) * 128],
                            rhs=bf(A[4])[:, kc * 1024 + og * 512:kc * 1024 + (og + 1) * 512],
                            start=(kc == 0), stop=(kc == NC - 1)), tick=1 if kc == NC - 1 else 0)
                p.wait("ve", "te")
                for og in range(2):
                    p.ve(lambda e, t=t, og=og: e.tensor_add(
                        out=R_tile(t)[:, og * 512:(og + 1) * 512],
                        in0=R_tile(t)[:, og * 512:(og + 1) * 512],
                        in1=PS[og]), tick=1 if og == 1 else 0)
                p.wait("te", "ve")
            p.barrier()

        if "attn_a" in enabled:
            emit_attn_a()
        for tok in enabled:
            if tok.startswith("ffn1"):
                emit_ffn(tok[4:] or "abcd")
        if "attn_b" in enabled:
            emit_attn_b()
        if "ffn2" in enabled:
            emit_ffn()

        for t in range(NT):
            p.dma("sy", lambda e, t=t: e.dma_start(out=out[t * 128:(t + 1) * 128, :], in_=R_tile(t)))
        p.wait("sy", "sy")

        block = ctx.enter_context(nc.Block())
        p.finalize(block)

    return nc


def _consts():
    i = np.arange(128)
    c = {}
    c["c_id"] = np.eye(128, dtype=ml_dtypes.bfloat16)
    c["c_mS"] = (i[:, None] > i[None, :]).astype(ml_dtypes.bfloat16)
    c["c_mUin"] = (i[None, :] >= i[:, None]).astype(ml_dtypes.bfloat16)
    c["c_ones1"] = np.ones((1, 128), dtype=ml_dtypes.bfloat16)
    selB = np.zeros((128, 2), dtype=np.float32)
    selB[0:64, 0] = 1.0
    selB[64:128, 1] = 1.0
    c["c_selB"] = selB.astype(ml_dtypes.bfloat16)
    sel12 = np.zeros((128, 8, 3, 12), dtype=np.float32)
    for ch in range(8):
        for q in range(3):
            sel12[:, ch, q, q * 4 + ch // 2] = 1.0
    c["c_sel12"] = sel12.reshape(128, 288).astype(ml_dtypes.bfloat16)
    return c


def kernel(**inputs):
    B = 8
    x = np.ascontiguousarray(np.asarray(inputs["L0_attn_a_hs"], dtype=np.float32))
    nc = build()

    def prep(name):
        return np.ascontiguousarray(np.asarray(inputs[name], dtype=np.float32))

    m = {
        "aWq": prep("L0_attn_a_Wq"), "aWk": prep("L0_attn_a_Wk"), "aWv": prep("L0_attn_a_Wv"),
        "aWid": prep("L0_attn_a_Wid"), "aWo": prep("L0_attn_a_Wo"), "aWb": prep("L0_attn_a_Wb"),
        "acwq": prep("L0_attn_a_cwq"), "acwk": prep("L0_attn_a_cwk"), "acwv": prep("L0_attn_a_cwv"),
        "afirs": prep("L0_attn_a_firs").reshape(1024, 3),
        "afirl": prep("L0_attn_a_firl").reshape(1024, 63),
        "agW1": prep("L0_attn_a_gW1"), "agb1": prep("L0_attn_a_gb1").reshape(1024, 1),
        "agW2": prep("L0_attn_a_gW2"), "agb2": prep("L0_attn_a_gb2").reshape(1, 4),
        "amisc": np.concatenate([prep("L0_attn_a_alpha_id"), prep("L0_attn_a_conv_res_logit"),
                                 np.zeros(8, np.float32)]).reshape(1, 16),
        "alt": prep("L0_attn_a_log_temp").reshape(1, 16),
        "afp": prep("L0_attn_a_floor_param").reshape(1, 16),
        "aonw": prep("L0_attn_a_o_norm_w").reshape(1, 256),
        "bWq": prep("L0_attn_b_Wq"), "bWk": prep("L0_attn_b_Wk"), "bWv": prep("L0_attn_b_Wv"),
        "bbq": prep("L0_attn_b_bq").reshape(1024, 1), "bbk": prep("L0_attn_b_bk").reshape(256, 1),
        "bbv": prep("L0_attn_b_bv").reshape(256, 1),
        "fwg": prep("L0_ffn_w_gate"), "fwu": prep("L0_ffn_w_up"), "fwd": prep("L0_ffn_w_down"),
    }
    m.update(_consts())
    in_maps = []
    for b in range(B):
        mm_ = dict(m)
        mm_["x"] = np.ascontiguousarray(x[b])
        in_maps.append(mm_)
    res = run_bass_kernel_spmd(nc, in_maps, core_ids=list(range(B)))
    return np.stack([res.results[b]["out"] for b in range(B)], axis=0).astype(np.float32)


# version 26
# speedup vs baseline: 1.1650x; 1.1650x over previous
import numpy as np
import ml_dtypes
import concourse.bass as bass
import concourse.mybir as mybir
from concourse.bass_utils import run_bass_kernel_spmd
from contextlib import ExitStack

F32 = mybir.dt.float32
BF16 = mybir.dt.bfloat16
FP16 = mybir.dt.float16
AF = mybir.ActivationFunctionType
ALU = mybir.AluOpType
AX = mybir.AxisListType

T = 2048
D = 1024
NT = 16
NC = 8
INNER = 4096
FFN_B_IGS = 8
FFN_B_TS = 16
FFN_B_PARTS = "msvd"

ENGS = ("te", "ve", "se", "sy", "gp")
A4A_SEQ = False
A4B_SEQ = False
SUBLAYERS = ("attn_a", "ffn1", "attn_b", "ffn2")


class Prog:
    def __init__(self, nc, ctx):
        self.nc = nc
        self.ctx = ctx
        self.q = {e: [] for e in ENGS}
        self.xs = {}
        self.phase_idx = 0
        self._new_sems()

    def _new_sems(self):
        self.sems = {e: self.ctx.enter_context(self.nc.semaphore(f"s{self.phase_idx}_{e}"))
                     for e in ENGS}
        self.cnt = {e: 0 for e in ENGS}

    def _roll(self, ch):
        self.phase_idx += 1
        self.sems[ch] = self.ctx.enter_context(
            self.nc.semaphore(f"s{self.phase_idx}_{ch}"))
        self.cnt[ch] = 0

    def _emit(self, eng, fn, tick=0, drain=False):
        if tick:
            self.cnt[eng] += tick
        self.q[eng].append([fn, self.sems[eng], tick, True, drain])

    def te(self, fn, tick=0):
        self._emit("te", fn, tick)

    def ve(self, fn, tick=0, drain=False):
        self._emit("ve", fn, tick, drain)

    def se(self, fn, tick=0, drain=False):
        self._emit("se", fn, tick, drain)

    def dma(self, eng, fn):
        self._emit(eng, fn, 16)

    def wait(self, eng, ch, n=None):
        if n is None:
            n = self.cnt[ch]
        if n <= 0:
            return
        sem = self.sems[ch]
        self.q[eng].append([lambda e, sem=sem, n=n: e.wait_ge(sem, n), None, 0, False, False])

    def n(self, ch):
        return self.cnt[ch]

    def xsem(self, name):
        if name not in self.xs:
            self.xs[name] = [self.ctx.enter_context(self.nc.semaphore(f"x_{name}")), 0]
        return name

    def dmax(self, eng, fn, xname):
        s = self.xs[xname]
        s[1] += 16
        self.q[eng].append([fn, s[0], 16, True, False])

    def xwait(self, eng, xname, n=None):
        s = self.xs[xname]
        if n is None:
            n = s[1]
        if n > 0:
            self.q[eng].append([lambda e, sem=s[0], n=n: e.wait_ge(sem, n), None, 0, False, False])

    def xn(self, xname):
        return self.xs[xname][1]

    def tick(self, eng):
        for ent in reversed(self.q[eng]):
            if ent[3]:
                if ent[1] is not self.sems[eng]:
                    break
                ent[2] += 1
                self.cnt[eng] += 1
                return

    def barrier(self):
        for e in ("te", "ve", "se"):
            self.tick(e)
        counts = dict(self.cnt)
        sems = dict(self.sems)
        for e in ENGS:
            for o in ENGS:
                if o != e and counts[o] > 0:
                    self.q[e].append([lambda eng, sem=sems[o], n=counts[o]: eng.wait_ge(sem, n),
                                      None, 0, False, False])
        for name, (xsem, xcnt) in self.xs.items():
            if xcnt > 0:
                for e in ENGS:
                    self.q[e].append([lambda eng, sem=xsem, n=xcnt: eng.wait_ge(sem, n),
                                      None, 0, False, False])
        for ch in ENGS:
            if self.cnt[ch] > 30000:
                self._roll(ch)

    def finalize(self, block):
        qs = {}
        for eng, ents in self.q.items():
            fns = []
            for fn, sem, tick, is_op, drain in ents:
                if tick:
                    if drain and is_op:
                        def wrapped(e, fn=fn, sem=sem, tick=tick):
                            fn(e).then_inc(sem, tick)
                            e.drain()
                        fns.append(wrapped)
                    else:
                        fns.append(lambda e, fn=fn, sem=sem, tick=tick: fn(e).then_inc(sem, tick))
                elif drain and is_op:
                    def wrapped2(e, fn=fn):
                        fn(e)
                        e.drain()
                    fns.append(wrapped2)
                else:
                    fns.append(fn)
            qs[eng] = fns

        @block.sync
        def _(e):
            for fn in qs["sy"]:
                fn(e)

        @block.gpsimd
        def _(e):
            for fn in qs["gp"]:
                fn(e)

        @block.vector
        def _(e):
            for fn in qs["ve"]:
                fn(e)

        @block.scalar
        def _(e):
            for fn in qs["se"]:
                fn(e)

        @block.tensor
        def _(e):
            for fn in qs["te"]:
                fn(e)


def build(enabled=SUBLAYERS):
    nc = bass.Bass("TRN2", target_bir_lowering=False, detect_race_conditions=False)

    def din(name, shape, dt=F32):
        return nc.dram_tensor(name, shape, dt, kind="ExternalInput")

    x_in = din("x", [T, D])
    aW = {n: din("a" + n, [D, D]) for n in ("Wq", "Wk", "Wv", "Wid", "Wo")}
    aWb = din("aWb", [D, 4])
    acw = {n: din("acw" + n, [D, 4]) for n in ("q", "k", "v")}
    afirs = din("afirs", [D, 3])
    afirl = din("afirl", [D, 63])
    agW1 = din("agW1", [1040, D])
    agb1 = din("agb1", [D, 1])
    agW2 = din("agW2", [D, 4])
    agb2 = din("agb2", [1, 4])
    amisc = din("amisc", [1, 16])
    alt = din("alt", [1, 16])
    afp = din("afp", [1, 16])
    aonw = din("aonw", [1, 256])
    bWq = din("bWq", [D, D]); bWk = din("bWk", [D, 256]); bWv = din("bWv", [D, 256])
    bbq = din("bbq", [D, 1]); bbk = din("bbk", [256, 1]); bbv = din("bbv", [256, 1])
    fw = {"g": din("fwg", [INNER, D]), "u": din("fwu", [INNER, D]), "d": din("fwd", [D, INNER])}
    c_id = din("c_id", [128, 128], BF16)
    c_mS = din("c_mS", [128, 128], BF16)
    c_mUin = din("c_mUin", [128, 128], BF16)
    c_ones1 = din("c_ones1", [1, 128], BF16)
    c_selB = din("c_selB", [128, 2], BF16)
    c_sel12 = din("c_sel12", [128, 288], BF16)

    out = nc.dram_tensor("out", [T, D], F32, kind="ExternalOutput")

    import os
    _dbg = os.environ.get("KDBG", "") == "1"

    def scr(name, shape, dt=BF16):
        return nc.dram_tensor(name, shape, dt, kind="ExternalOutput" if _dbg else "Internal")

    R_s = scr("R_s", [T, D], F32)
    h_s = scr("h_s", [T, D])
    k_s = scr("k_s", [D, T])
    v_s = scr("v_s", [D, T])
    ls_s = scr("ls_s", [D, T]); ll_s = scr("ll_s", [D, T])
    base_s = scr("base_s", [D, T])
    id_s = scr("id_s", [T, D])
    delta_s = scr("delta_s", [T, D])
    xq_s = scr("xq_s", [T, D])
    gu_s = scr("gu_s", [T, INNER])
    guq_s = scr("guq_s", [T, INNER])
    wq_s = {"g": scr("wqg_s", [INNER, D]), "u": scr("wqu_s", [INNER, D]),
            "d": scr("wqd_s", [D, INNER])}
    wqT_s = {"g": scr("wqgT_s", [NC, 128, INNER]), "u": scr("wquT_s", [NC, 128, INNER]),
             "d": scr("wqdT_s", [32, 128, D])}
    bv_s = scr("bv_s", [256, T])
    stk_s = scr("stk_s", [128, 1024])
    bc_s = scr("bc_s", [256], F32)
    rstd_s = scr("rstd_s", [8, T], F32)
    logits_s = scr("logits_s", [16, T], F32)

    with ExitStack() as ctx:
        sb = lambda n, s, d=F32: ctx.enter_context(nc.sbuf_tensor(n, s, d))

        A = [sb(f"A{i}", [128, 16384], BF16) for i in range(5)]
        WK = sb("WK", [128, 4096], F32)
        WK2 = sb("WK2", [128, 2048], F32)
        SM = sb("SM", [128, 1024], F32)
        SMB = sb("SMB", [128, 2048], BF16)
        STK = sb("STK", [128, 1024], BF16)
        LOWF = sb("LOWF", [32, 2048], F32)
        LOWB = sb("LOWB", [64, 2048], BF16)
        CST = sb("CST", [128, 384], BF16)
        CS2 = sb("CS2", [128, 192], BF16)
        SEL12 = sb("SEL12", [128, 288], BF16)

        ident = CST[:, 0:128]
        mS = CST[:, 128:256]
        mUin = CST[:, 256:384]
        ones1 = CS2[0:1, 0:128]
        selB = CS2[:, 128:130]

        PSBIG = ctx.enter_context(nc.psum_tensor("psbig", [128, 2048], F32))
        PS47 = [ctx.enter_context(nc.psum_tensor(f"ps{i}", [128, 512], F32)) for i in range(4)]
        PS = [PSBIG[:, i * 512:(i + 1) * 512] for i in range(4)] + [t[:] for t in PS47]
        PST = PSBIG[0:12, :]

        p = Prog(nc, ctx)

        def bf(a):
            return a[:].bitcast(BF16)

        def f32(a):
            return a[:].bitcast(F32)

        def R_tile(t):
            a = A[0] if t < 8 else A[1]
            return f32(a)[:, (t % 8) * 1024:(t % 8 + 1) * 1024]

        def emit_ln(dst_fn):
            for t in range(NT):
                p.ve(lambda e, t=t: e.bn_stats(out=SM[:, 12 * t:12 * t + 6],
                                               in_=R_tile(t)[:, 0:512]))
                p.ve(lambda e, t=t: e.bn_stats(out=SM[:, 12 * t + 6:12 * t + 12],
                                               in_=R_tile(t)[:, 512:1024]))
            for t in range(NT):
                p.ve(lambda e, t=t: e.bn_aggr(out=SM[:, 192 + 2 * t:192 + 2 * t + 2],
                                              in_=SM[:, 12 * t:12 * t + 12]),
                     tick=1 if t == NT - 1 else 0)
            p.wait("se", "ve")
            p.se(lambda e: e.activation(
                out=SM[:, 224:224 + NT].rearrange("p (t c) -> p t c", c=1),
                in_=SM[:, 192:192 + 2 * NT].rearrange("p (t c) -> p t c", c=2)[:, :, 1:2],
                func=AF.Sqrt, bias=SM[:, 1008:1009]), tick=1)
            p.wait("ve", "se")
            p.ve(lambda e: e.reciprocal(out=SM[:, 240:240 + NT],
                                        in_=SM[:, 224:224 + NT]), drain=True)
            for t in range(NT):
                p.ve(lambda e, t=t: e.tensor_scalar(
                    out=dst_fn(t), in0=R_tile(t),
                    scalar1=SM[:, 192 + 2 * t:192 + 2 * t + 1],
                    scalar2=SM[:, 240 + t:241 + t],
                    op0=ALU.subtract, op1=ALU.mult),
                    tick=1 if t == NT - 1 else 0)

        def trload_fm(dst_arena, src_dram, n_chunks, rows=T):
            for c in range(n_chunks):
                p.dma("sy", lambda e, c=c: e.dma_start_transpose(
                    out=bf(dst_arena)[:, c * rows:(c + 1) * rows],
                    in_=src_dram[:, c * 128:(c + 1) * 128]))

        def trload_tok(dst_arena, src_dram, width):
            for t in range(NT):
                p.dma("sy", lambda e, t=t: e.dma_start_transpose(
                    out=bf(dst_arena)[:, t * width:(t + 1) * width],
                    in_=src_dram[:, t * 128:(t + 1) * 128]))

        def trload_tok_off(dst_arena, off, src_dram, width):
            for t in range(NT):
                p.dma("sy", lambda e, t=t: e.dma_start_transpose(
                    out=bf(dst_arena)[:, off + t * width:off + (t + 1) * width],
                    in_=src_dram[:, t * 128:(t + 1) * 128]))

        def bcast(dst_sm_slice, src_bf16_row, n):
            p.wait("te", "ve")
            p.te(lambda e: e.matmul(out=PS[7][:, 0:n], lhsT=ones1, rhs=src_bf16_row,
                                    start=True, stop=True), tick=1)
            p.wait("ve", "te")
            p.ve(lambda e: e.tensor_copy(out=dst_sm_slice, in_=PS[7][:, 0:n]), tick=1)

        for t in range(NT):
            p.dma("sy", lambda e, t=t: e.dma_start(out=R_tile(t), in_=x_in[t * 128:(t + 1) * 128, :]))
        p.dma("sy", lambda e: e.dma_start(out=CST[:, 0:128], in_=c_id[:]))
        p.dma("sy", lambda e: e.dma_start(out=CST[:, 128:256], in_=c_mS[:]))
        p.dma("sy", lambda e: e.dma_start(out=CST[:, 256:384], in_=c_mUin[:]))
        p.dma("sy", lambda e: e.dma_start(out=CS2[0:1, 0:128], in_=c_ones1[:]))
        p.dma("sy", lambda e: e.dma_start(out=CS2[:, 128:130], in_=c_selB[:]))
        p.dma("sy", lambda e: e.dma_start(out=SEL12[:], in_=c_sel12[:]))
        p.ve(lambda e: e.memset(SM[:, 1008:1009], 1e-5), tick=1)
        p.ve(lambda e: e.memset(SM[:, 1009:1010], 1e-6), tick=1)
        p.ve(lambda e: e.memset(SM[:, 1016:1017], 1.0), tick=1)
        p.ve(lambda e: e.memset(SM[:, 1018:1019], 1536.0), tick=1)
        p.barrier()

        def emit_wprep(key, rows, cols, scol):
            w_dram = fw[key]
            nrt = rows // 128
            if cols <= 2048:
                bufs = [WK[:, 0:cols], WK[:, cols:2 * cols]]
            else:
                bufs = [WK[:, 0:cols], f32(A[2])[:, 4096:4096 + cols]]
            xl = [p.xsem("wpl0"), p.xsem("wpl1")]
            xs = [p.xsem("wps0"), p.xsem("wps1")]
            l_marks = {}
            ve_marks = {}

            def _load(rt, guard_ch, guard_marks):
                q = "gp" if rt % 3 == 2 else "sy"
                if rt >= 2 and (rt - 2) in guard_marks:
                    p.wait(q, guard_ch, guard_marks[rt - 2])
                p.dmax(q, lambda e, rt=rt, b=bufs[rt % 2]: e.dma_start(
                    out=b, in_=w_dram[rt * 128:(rt + 1) * 128, :]), xl[rt % 2])
                l_marks[rt] = p.xn(xl[rt % 2])

            for rt in range(min(2, nrt)):
                _load(rt, "ve", ve_marks)
            for rt in range(nrt):
                p.xwait("ve", xl[rt % 2], l_marks[rt])
                b = bufs[rt % 2]
                if rt == 0:
                    p.ve(lambda e, b=b: e.reduce_sum(out=SM[:, 512:513], in_=b, axis=AX.X,
                                                     apply_absolute_value=True), tick=1, drain=True)
                else:
                    p.ve(lambda e, b=b: e.reduce_sum(out=SM[:, 513:514], in_=b, axis=AX.X,
                                                     apply_absolute_value=True), drain=True)
                    p.ve(lambda e: e.tensor_add(out=SM[:, 512:513], in0=SM[:, 512:513],
                                                in1=SM[:, 513:514]), tick=1, drain=True)
                ve_marks[rt] = p.n("ve")
                if rt + 2 < nrt:
                    _load(rt + 2, "ve", ve_marks)
            p.wait("sy", "ve")
            p.dma("sy", lambda e: e.dma_start(out=bc_s[0:128], in_=SM[:, 512:513]))
            p.wait("sy", "sy")
            p.dma("sy", lambda e: e.dma_start(out=WK2[0:1, 0:128],
                                              in_=bc_s[0:128].rearrange("(o c) -> o c", o=1)))
            p.wait("ve", "sy")
            p.ve(lambda e: e.reduce_sum(out=SM[0:1, 519:520], in_=WK2[0:1, 0:128], axis=AX.X), drain=True)
            p.ve(lambda e: e.tensor_scalar(out=SM[0:1, 520:521], in0=SM[0:1, 519:520],
                                           scalar1=0.5 / (rows * cols), scalar2=None, op0=ALU.mult), drain=True)
            p.ve(lambda e: e.tensor_scalar(out=SM[0:1, 521:522], in0=SM[0:1, 519:520],
                                           scalar1=1.0 / (rows * cols), scalar2=1e-5,
                                           op0=ALU.mult, op1=ALU.max), tick=1)
            p.wait("sy", "ve")
            p.dma("sy", lambda e: e.dma_start(out=bc_s[128:130], in_=SM[0:1, 520:522]))
            p.wait("sy", "sy")
            p.dma("sy", lambda e, scol=scol: e.dma_start(
                out=SM[:, scol:scol + 2],
                in_=bc_s[128:130].rearrange("(o c) -> o c", o=1).partition_broadcast(128)))
            thr = SM[:, scol:scol + 1]
            l_marks = {}
            se_marks = {}
            ve2_marks = {}
            gp_marks = {}

            def _load2(rt):
                q = "gp" if rt % 3 == 2 else "sy"
                if rt >= 2 and (rt - 2) in se_marks:
                    p.wait(q, "se", se_marks[rt - 2])
                p.dmax(q, lambda e, rt=rt, b=bufs[rt % 2]: e.dma_start(
                    out=b, in_=w_dram[rt * 128:(rt + 1) * 128, :]), xl[rt % 2])
                l_marks[rt] = p.xn(xl[rt % 2])

            p.wait("ve", "sy")
            p.wait("gp", "ve")
            p.wait("sy", "ve")
            for rt in range(min(2, nrt)):
                _load2(rt)
            for rt in range(nrt):
                b = bufs[rt % 2]
                ubuf = f32(A[2])[:, (rt % 2) * cols:(rt % 2 + 1) * cols] if cols <= 2048 \
                    else f32(A[2])[:, 0:cols]
                sgn = f32(A[4])[:, (rt % 2) * cols:(rt % 2 + 1) * cols] if cols <= 2048 \
                    else f32(A[4])[:, 0:cols]
                qout = bf(A[3])[:, (rt % 2) * cols:(rt % 2 + 1) * cols]
                p.xwait("se", xl[rt % 2], l_marks[rt])
                if cols > 2048 and rt >= 1:
                    p.wait("se", "ve", ve2_marks[rt - 1])
                elif rt >= 2:
                    p.wait("se", "ve", ve2_marks[rt - 2])
                p.se(lambda e, ubuf=ubuf, b=b: e.activation(out=ubuf, in_=b, func=AF.Abs), tick=1)
                p.se(lambda e, sgn=sgn, b=b: e.sign(out=sgn, in_=b), tick=1)
                se_marks[rt] = p.n("se")
                p.wait("ve", "se", se_marks[rt])
                if rt >= 2:
                    p.xwait("ve", xs[rt % 2], gp_marks[rt - 2])
                p.ve(lambda e, ubuf=ubuf: e.tensor_scalar(out=ubuf, in0=ubuf, scalar1=thr,
                                                          scalar2=None, op0=ALU.is_ge))
                p.ve(lambda e, ubuf=ubuf, sgn=sgn, qout=qout: e.tensor_tensor(
                    out=qout, in0=ubuf, in1=sgn, op=ALU.mult), tick=1)
                ve2_marks[rt] = p.n("ve")
                p.wait("gp", "ve", ve2_marks[rt])
                p.dmax("gp", lambda e, rt=rt, qout=qout: e.dma_start(
                    out=wq_s[key][rt * 128:(rt + 1) * 128, :], in_=qout), xs[rt % 2])
                gp_marks[rt] = p.xn(xs[rt % 2])
                if rt + 2 < nrt:
                    _load2(rt + 2)
            p.barrier()
            if key in ("g", "u"):
                return
            tr_marks = {}
            st_marks = {}
            half = 8192
            for kc in range(cols // 128):
                tb = bf(A[3])[:, (kc % 2) * half:(kc % 2) * half + rows]
                if kc >= 2:
                    p.xwait("sy", xs[kc % 2], st_marks[kc - 2])
                p.dmax("sy", lambda e, kc=kc, tb=tb: e.dma_start_transpose(
                    out=tb, in_=wq_s[key][:, kc * 128:(kc + 1) * 128]), xl[kc % 2])
                tr_marks[kc] = p.xn(xl[kc % 2])
                p.xwait("gp", xl[kc % 2], tr_marks[kc])
                p.dmax("gp", lambda e, kc=kc, tb=tb: e.dma_start(
                    out=wqT_s[key][kc, :, :], in_=tb), xs[kc % 2])
                st_marks[kc] = p.xn(xs[kc % 2])
            p.barrier()

        need_ffn = any(tok.startswith(("ffn", "wprep")) for tok in enabled)
        if need_ffn:
            emit_wprep("g", INNER, D, 1010)
            emit_wprep("u", INNER, D, 1012)
            emit_wprep("d", D, INNER, 1014)

        def emit_ffn(ph="abcd"):
            for t in range(NT):
                p.ve(lambda e, t=t: e.bn_stats(out=SM[:, 12 * t:12 * t + 6],
                                               in_=R_tile(t)[:, 0:512]))
                p.ve(lambda e, t=t: e.bn_stats(out=SM[:, 12 * t + 6:12 * t + 12],
                                               in_=R_tile(t)[:, 512:1024]))
            for t in range(NT):
                p.ve(lambda e, t=t: e.bn_aggr(out=SM[:, 192 + 2 * t:192 + 2 * t + 2],
                                              in_=SM[:, 12 * t:12 * t + 12]),
                     tick=1 if t == NT - 1 else 0)
            p.wait("se", "ve")
            p.se(lambda e: e.activation(
                out=SM[:, 224:224 + NT].rearrange("p (t c) -> p t c", c=1),
                in_=SM[:, 192:192 + 2 * NT].rearrange("p (t c) -> p t c", c=2)[:, :, 1:2],
                func=AF.Sqrt, bias=SM[:, 1008:1009]), tick=1)
            p.wait("ve", "se")
            p.ve(lambda e: e.reciprocal(out=SM[:, 240:240 + NT], in_=SM[:, 224:224 + NT]), tick=1)
            hbuf = WK2[:, 0:1024]
            t16 = SMB[:].bitcast(FP16)[:, 0:1024]
            xq = bf(A[3])[:, 0:1024]
            fxq = [p.xsem("fxq0"), p.xsem("fxq1")]
            fxt = [p.xsem("fxt0"), p.xsem("fxt1")]
            xqmk = {}
            xtmk = {}
            for t in range(NT):
                if t > 0:
                    p.wait("ve", "se")
                p.ve(lambda e, t=t: e.tensor_scalar(
                    out=hbuf, in0=R_tile(t),
                    scalar1=SM[:, 192 + 2 * t:192 + 2 * t + 1],
                    scalar2=SM[:, 240 + t:241 + t],
                    op0=ALU.subtract, op1=ALU.mult))
                p.ve(lambda e, t=t: e.reduce_max(out=SM[:, 512 + t:513 + t], in_=hbuf,
                                                 axis=AX.X, apply_absolute_value=True), drain=True)
                p.ve(lambda e, t=t: e.tensor_scalar(out=SM[:, 544 + t:545 + t],
                                                    in0=SM[:, 512 + t:513 + t],
                                                    scalar1=1e-5, scalar2=None, op0=ALU.max), drain=True)
                p.ve(lambda e, t=t: e.reciprocal(out=SM[:, 560 + t:561 + t],
                                                 in_=SM[:, 544 + t:545 + t]), drain=True)
                p.ve(lambda e, t=t: e.tensor_scalar(out=SM[:, 560 + t:561 + t],
                                                    in0=SM[:, 560 + t:561 + t],
                                                    scalar1=127.0, scalar2=None, op0=ALU.mult), tick=1)
                p.wait("se", "ve")
                asc = SM[:, 560 + t:561 + t]
                p.se(lambda e, asc=asc: e.activation(out=t16, in_=hbuf, func=AF.Identity,
                                                     scale=asc, bias=SM[:, 1018:1019]), tick=1)
                p.wait("ve", "se")
                if t > 0:
                    p.wait("ve", "gp")
                p.ve(lambda e: e.tensor_scalar(out=xq, in0=t16, scalar1=-1536.0,
                                               scalar2=None, op0=ALU.add), tick=1)
                p.wait("gp", "ve")
                p.dmax("gp", lambda e, t=t: e.dma_start(out=xq_s[t * 128:(t + 1) * 128, :], in_=xq),
                       fxq[t // 8])
                xqmk[t // 8] = p.xn(fxq[t // 8])
                if t == 7 or t == NT - 1:
                    hv = t // 8
                    p.xwait("sy", fxq[hv], xqmk[hv])
                    for c in range(NC):
                        p.dmax("sy", lambda e, c=c, hv=hv: e.dma_start_transpose(
                            out=bf(A[2])[:, c * 2048 + hv * 1024:c * 2048 + hv * 1024 + 1024],
                            in_=xq_s[hv * 1024:(hv + 1) * 1024, c * 128:(c + 1) * 128]), fxt[hv])
                    xtmk[hv] = p.xn(fxt[hv])
            p.ve(lambda e: e.reciprocal(out=SM[:, 576:576 + NT], in_=SM[:, 560:560 + NT]), drain=True)
            p.ve(lambda e: e.tensor_scalar(out=SM[:, 592:592 + NT], in0=SM[:, 576:576 + NT],
                                           scalar1=SM[:, 1013:1014], scalar2=None, op0=ALU.mult), drain=True)
            p.ve(lambda e: e.tensor_scalar(out=SM[:, 576:576 + NT], in0=SM[:, 576:576 + NT],
                                           scalar1=SM[:, 1011:1012], scalar2=None, op0=ALU.mult), tick=1, drain=True)
            if "b" not in ph:
                p.barrier()
                return
            p.ve(lambda e: e.memset(SM[:, 512:512 + NT], 0.0), tick=1)
            fbw = [p.xsem("fbw0"), p.xsem("fbw1")]
            ve_marks = {}
            gp_marks = {}
            te_ig_marks = {}
            wmkb = {}

            def _bload(ig):
                if ig >= 2:
                    p.wait("sy", "te", te_ig_marks[ig - 2])
                for kc in range(NC):
                    p.dmax("sy", lambda e, ig=ig, kc=kc: e.dma_start_transpose(
                        out=bf(A[3])[:, (ig % 2) * 4096 + kc * 512:(ig % 2) * 4096 + (kc + 1) * 512],
                        in_=wq_s["g"][ig * 512:(ig + 1) * 512, kc * 128:(kc + 1) * 128]), fbw[ig % 2])
                    p.dmax("sy", lambda e, ig=ig, kc=kc: e.dma_start_transpose(
                        out=bf(A[4])[:, (ig % 2) * 4096 + kc * 512:(ig % 2) * 4096 + (kc + 1) * 512],
                        in_=wq_s["u"][ig * 512:(ig + 1) * 512, kc * 128:(kc + 1) * 128]), fbw[ig % 2])
                wmkb[ig] = p.xn(fbw[ig % 2])

            p.xwait("sy", fxq[1], xqmk[1])
            for ig in range(min(2, FFN_B_IGS)):
                _bload(ig)
            for ig in range(FFN_B_IGS):
                p.xwait("te", fbw[ig % 2], wmkb[ig])
                wo = (ig % 2) * 4096
                for t in range(FFN_B_TS):
                    if t % 8 == 0:
                        p.xwait("te", fxt[t // 8], xtmk[t // 8])
                    gi = ig * FFN_B_TS + t
                    pa = PS[(2 * t) % 4]
                    pb = PS[(2 * t) % 4 + 1]
                    if gi >= 2:
                        p.wait("te", "ve", ve_marks[gi - 2])
                    if "m" in FFN_B_PARTS:
                        for kc in range(NC):
                            p.te(lambda e, pa=pa, kc=kc, t=t, wo=wo: e.matmul(
                                out=pa, lhsT=bf(A[2])[:, kc * 2048 + t * 128:kc * 2048 + (t + 1) * 128],
                                rhs=bf(A[3])[:, wo + kc * 512:wo + (kc + 1) * 512],
                                start=(kc == 0), stop=(kc == NC - 1)))
                        for kc in range(NC):
                            p.te(lambda e, pb=pb, kc=kc, t=t, wo=wo: e.matmul(
                                out=pb, lhsT=bf(A[2])[:, kc * 2048 + t * 128:kc * 2048 + (t + 1) * 128],
                                rhs=bf(A[4])[:, wo + kc * 512:wo + (kc + 1) * 512],
                                start=(kc == 0), stop=(kc == NC - 1)))
                    else:
                        for pp in (pa, pb):
                            p.te(lambda e, pp=pp: e.matmul(out=pp, lhsT=ident, rhs=bf(A[3])[:, 0:512],
                                                           start=True, stop=True))
                    p.tick("te")
                    p.wait("se", "te")
                    p.wait("se", "ve")
                    if "s" in FFN_B_PARTS:
                        p.se(lambda e, pa=pa, t=t: e.activation(out=WK2[:, 0:512], in_=pa,
                                                                func=AF.Silu,
                                                                scale=SM[:, 576 + t:577 + t]), tick=1)
                    else:
                        p.se(lambda e: e.nop(), tick=1)
                    p.wait("ve", "se")
                    gu = SMB[:, 1024 + (t % 2) * 512:1024 + (t % 2) * 512 + 512]
                    if gi >= 2:
                        p.wait("ve", "gp", gp_marks[gi - 2])
                    if "v" in FFN_B_PARTS:
                        p.ve(lambda e, pb=pb, t=t, gu=gu: e.scalar_tensor_tensor(
                            out=gu, in0=pb, scalar=SM[:, 592 + t:593 + t], in1=WK2[:, 0:512],
                            op0=ALU.mult, op1=ALU.mult))
                        p.ve(lambda e, gu=gu: e.reduce_max(out=SM[:, 528:529], in_=gu,
                                                           axis=AX.X, apply_absolute_value=True), drain=True)
                        p.ve(lambda e, t=t: e.tensor_tensor(out=SM[:, 512 + t:513 + t],
                                                            in0=SM[:, 512 + t:513 + t],
                                                            in1=SM[:, 528:529], op=ALU.max), tick=1, drain=True)
                    else:
                        p.ve(lambda e, gu=gu: e.memset(gu, 0.0), tick=1)
                    ve_marks[gi] = p.n("ve")
                    p.wait("gp", "ve")
                    if "d" in FFN_B_PARTS:
                        p.dma("gp", lambda e, t=t, ig=ig, gu=gu: e.dma_start(
                            out=gu_s[t * 128:(t + 1) * 128, ig * 512:(ig + 1) * 512], in_=gu))
                    gp_marks[gi] = p.n("gp")
                te_ig_marks[ig] = p.n("te")
                if ig + 2 < FFN_B_IGS:
                    _bload(ig + 2)
            p.barrier()
            if "c" not in ph:
                return
            p.ve(lambda e: e.tensor_scalar(out=SM[:, 544:544 + NT], in0=SM[:, 512:512 + NT],
                                           scalar1=1e-5, scalar2=None, op0=ALU.max), drain=True)
            p.ve(lambda e: e.reciprocal(out=SM[:, 560:560 + NT], in_=SM[:, 544:544 + NT]), drain=True)
            p.ve(lambda e: e.tensor_scalar(out=SM[:, 560:560 + NT], in0=SM[:, 560:560 + NT],
                                           scalar1=127.0, scalar2=None, op0=ALU.mult), drain=True)
            p.ve(lambda e: e.reciprocal(out=SM[:, 576:576 + NT], in_=SM[:, 560:560 + NT]), drain=True)
            p.ve(lambda e: e.tensor_scalar(out=SM[:, 576:576 + NT], in0=SM[:, 576:576 + NT],
                                           scalar1=SM[:, 1015:1016], scalar2=None, op0=ALU.mult), tick=1)
            gtb = [bf(A[4])[:, 0:4096], bf(A[4])[:, 4096:8192]]
            t164 = bf(A[4]).bitcast(FP16)[:, 8192:12288]
            gqb = [WK[:].bitcast(BF16)[:, 0:4096], WK[:].bitcast(BF16)[:, 4096:8192]]
            fcg = [p.xsem("fcg0"), p.xsem("fcg1")]
            fcq = [p.xsem("fcq0"), p.xsem("fcq1")]
            fdt = [p.xsem("fdt0"), p.xsem("fdt1")]
            fdw = [p.xsem("fdw0"), p.xsem("fdw1")]
            lg = {}
            semk = {}
            vemk = {}
            qmk = {}
            tmk = {}

            def _cload(t):
                if t >= 2:
                    p.wait("sy", "se", semk[t - 2])
                p.dmax("sy", lambda e, t=t, b=gtb[t % 2]: e.dma_start(
                    out=b, in_=gu_s[t * 128:(t + 1) * 128, :]), fcg[t % 2])
                lg[t] = p.xn(fcg[t % 2])

            def emit_transposes(tg):
                p.xwait("sy", fcq[0], qmk[4 * tg + 2])
                p.xwait("sy", fcq[1], qmk[4 * tg + 3])
                for kc in range(32):
                    dst = A[2] if kc < 16 else A[3]
                    off = (tg % 2) * 8192 + (kc % 16) * 512
                    p.dmax("sy", lambda e, tg=tg, kc=kc, dst=dst, off=off: e.dma_start_transpose(
                        out=bf(dst)[:, off:off + 512],
                        in_=guq_s[tg * 512:(tg + 1) * 512, kc * 128:(kc + 1) * 128]), fdt[tg % 2])
                tmk[tg] = p.xn(fdt[tg % 2])

            p.wait("se", "ve")
            _cload(0)
            _cload(1)
            for t in range(NT):
                p.xwait("se", fcg[t % 2], lg[t])
                if t >= 1:
                    p.wait("se", "ve", vemk[t - 1])
                asc = SM[:, 560 + t:561 + t]
                p.se(lambda e, asc=asc, b=gtb[t % 2]: e.activation(
                    out=t164, in_=b, func=AF.Identity, scale=asc, bias=SM[:, 1018:1019]), tick=1)
                semk[t] = p.n("se")
                p.wait("ve", "se", semk[t])
                if t >= 2:
                    p.xwait("ve", fcq[t % 2], qmk[t - 2])
                p.ve(lambda e, q=gqb[t % 2]: e.tensor_scalar(out=q, in0=t164, scalar1=-1536.0,
                                                             scalar2=None, op0=ALU.add), tick=1)
                vemk[t] = p.n("ve")
                p.wait("gp", "ve", vemk[t])
                p.dmax("gp", lambda e, t=t, q=gqb[t % 2]: e.dma_start(
                    out=guq_s[t * 128:(t + 1) * 128, :], in_=q), fcq[t % 2])
                qmk[t] = p.xn(fcq[t % 2])
                if t + 2 < NT:
                    _cload(t + 2)
                if t % 4 == 3 and t // 4 < 2:
                    emit_transposes(t // 4)
                if "d" in ph:
                    if t == 3:
                        C0 = p.n("te")
                        wmk0 = {}
                        for kc in range(2):
                            p.dmax("gp", lambda e, kc=kc: e.dma_start(
                                out=SMB[:, (kc % 2) * 1024:(kc % 2 + 1) * 1024],
                                in_=wqT_s["d"][kc, :, :]), fdw[kc % 2])
                            wmk0[kc] = p.xn(fdw[kc % 2])
                        next_w = [2]
                    elif t >= 4:
                        for kc in range(next_w[0], min(next_w[0] + 3, 32)):
                            p.wait("gp", "te", C0 + kc - 1)
                            p.dmax("gp", lambda e, kc=kc: e.dma_start(
                                out=SMB[:, (kc % 2) * 1024:(kc % 2 + 1) * 1024],
                                in_=wqT_s["d"][kc, :, :]), fdw[kc % 2])
                            wmk0[kc] = p.xn(fdw[kc % 2])
                        next_w[0] = min(next_w[0] + 3, 32)
            if "d" not in ph:
                p.barrier()
                return
            assert next_w[0] == 32
            assert p.n("te") == C0, "TE tick count drifted from prediction"
            prev_temk = None
            for tg in range(4):
                p.xwait("te", fdt[tg % 2], tmk[tg])
                temk = {}
                if tg == 0:
                    wmk = wmk0
                else:
                    wmk = {}
                tg_wmk = wmk

                def _wload(kc, prev_temk=prev_temk, wmk=tg_wmk, temk=temk):
                    if kc >= 2:
                        p.wait("gp", "te", temk[kc - 2])
                    elif prev_temk is not None:
                        p.wait("gp", "te", prev_temk[30 + kc])
                    p.dmax("gp", lambda e, kc=kc: e.dma_start(
                        out=SMB[:, (kc % 2) * 1024:(kc % 2 + 1) * 1024],
                        in_=wqT_s["d"][kc, :, :]), fdw[kc % 2])
                    wmk[kc] = p.xn(fdw[kc % 2])

                if tg > 0:
                    _wload(0)
                    _wload(1)
                for kc in range(32):
                    p.xwait("te", fdw[kc % 2], wmk[kc])
                    src = A[2] if kc < 16 else A[3]
                    soff = (tg % 2) * 8192 + (kc % 16) * 512
                    for tt in range(4):
                        for og in range(2):
                            p.te(lambda e, kc=kc, tt=tt, og=og, src=src, soff=soff: e.matmul(
                                out=PS[tt * 2 + og],
                                lhsT=bf(src)[:, soff + tt * 128:soff + (tt + 1) * 128],
                                rhs=SMB[:, (kc % 2) * 1024 + og * 512:(kc % 2) * 1024 + (og + 1) * 512],
                                start=(kc == 0), stop=(kc == 31), skip_group_check=True))
                    p.tick("te")
                    temk[kc] = p.n("te")
                    if tg > 0 and kc + 2 < 32:
                        _wload(kc + 2)
                prev_temk = temk
                if tg + 2 < 4:
                    p.wait("sy", "te", temk[31])
                    emit_transposes(tg + 2)
                p.wait("ve", "te")
                for tt in range(4):
                    t = tg * 4 + tt
                    for og in range(2):
                        p.ve(lambda e, t=t, tt=tt, og=og: e.scalar_tensor_tensor(
                            out=R_tile(t)[:, og * 512:(og + 1) * 512], in0=PS[tt * 2 + og],
                            scalar=SM[:, 576 + t:577 + t],
                            in1=R_tile(t)[:, og * 512:(og + 1) * 512],
                            op0=ALU.mult, op1=ALU.add),
                            tick=1 if (tt == 3 and og == 1) else 0)
                p.wait("te", "ve")
            p.barrier()

        def emit_attn_b():
            emit_ln(lambda t: bf(A[3])[:, t * 1024:(t + 1) * 1024])
            p.wait("gp", "ve")
            for t in range(NT):
                p.dma("gp", lambda e, t=t: e.dma_start(out=h_s[t * 128:(t + 1) * 128, :],
                                                       in_=bf(A[3])[:, t * 1024:(t + 1) * 1024]))
            p.barrier()
            trload_fm(A[2], h_s, NC)
            p.dma("sy", lambda e: e.dma_start(out=SM[:, 700:708],
                                              in_=bbq[:].rearrange("(c p) o -> p (c o)", p=128), allow_slow_non_contiguous=True))
            p.dma("sy", lambda e: e.dma_start(out=SM[:, 708:710],
                                              in_=bbk[:].rearrange("(c p) o -> p (c o)", p=128), allow_slow_non_contiguous=True))
            p.dma("sy", lambda e: e.dma_start(out=SM[:, 710:712],
                                              in_=bbv[:].rearrange("(c p) o -> p (c o)", p=128), allow_slow_non_contiguous=True))
            p.barrier()
            qT = bf(A[3])
            kT = bf(A[4])[:, 0:4096]
            vT = bf(A[4])[:, 4096:8192]
            jobs = [(bWq, 8, lambda mc, o: qT[:, mc * 2048 + o:mc * 2048 + o + 512], 700),
                    (bWk, 2, lambda mc, o: kT[:, mc * 2048 + o:mc * 2048 + o + 512], 708),
                    (bWv, 2, lambda mc, o: vT[:, mc * 2048 + o:mc * 2048 + o + 512], 710)]
            bpw = [p.xsem("bpw0"), p.xsem("bpw1")]
            bpq = [(wd, dstf, bcol, mc) for (wd, ncout, dstf, bcol) in jobs
                   for mc in range(ncout)]
            bwmk = {}
            btemk = {}

            def _bpl(gi):
                wd, _, _, mc = bpq[gi]
                if gi >= 2:
                    p.wait("gp", "te", btemk[gi - 2])
                half = (gi % 2) * 1024
                for kc in range(NC):
                    p.dmax("gp", lambda e, wd=wd, mc=mc, kc=kc, half=half: e.dma_start(
                        out=SMB[:, half + kc * 128:half + (kc + 1) * 128],
                        in_=wd[kc * 128:(kc + 1) * 128, mc * 128:(mc + 1) * 128]), bpw[gi % 2])
                bwmk[gi] = p.xn(bpw[gi % 2])

            p.wait("gp", "te")
            _bpl(0)
            _bpl(1)
            for gi, (wd, dstf, bcol, mc) in enumerate(bpq):
                half = (gi % 2) * 1024
                p.xwait("te", bpw[gi % 2], bwmk[gi])
                se_marks = {}
                for nt4 in range(4):
                    if nt4 >= 2:
                        p.wait("te", "se", se_marks[nt4 - 2])
                    for kc in range(NC):
                        p.te(lambda e, kc=kc, nt4=nt4, half=half: e.matmul(
                            out=PS[nt4 % 2],
                            lhsT=SMB[:, half + kc * 128:half + (kc + 1) * 128],
                            rhs=bf(A[2])[:, kc * 2048 + nt4 * 512:kc * 2048 + (nt4 + 1) * 512],
                            start=(kc == 0), stop=(kc == NC - 1)))
                    p.tick("te")
                    p.wait("se", "te")
                    p.se(lambda e, dstf=dstf, mc=mc, nt4=nt4, bcol=bcol: e.activation(
                        out=dstf(mc, nt4 * 512), in_=PS[nt4 % 2], func=AF.Identity,
                        bias=SM[:, bcol + mc:bcol + mc + 1]), tick=1)
                    se_marks[nt4] = p.n("se")
                btemk[gi] = p.n("te")
                if gi + 2 < len(bpq):
                    _bpl(gi + 2)
            p.barrier()
            for (src, nch, scol) in ((qT, 8, 760), (kT, 2, 792)):
                for c in range(nch):
                    p.wait("ve", "te")
                    p.ve(lambda e, src=src, c=c: e.tensor_mul(
                        out=SMB[:, 0:2048], in0=src[:, c * 2048:(c + 1) * 2048],
                        in1=src[:, c * 2048:(c + 1) * 2048]), tick=1)
                    p.wait("te", "ve")
                    for n4 in range(4):
                        p.te(lambda e, n4=n4: e.matmul(
                            out=PS[n4][0:2, 0:512], lhsT=selB,
                            rhs=SMB[:, n4 * 512:(n4 + 1) * 512], start=True, stop=True))
                    p.tick("te")
                    p.wait("ve", "te")
                    for n4 in range(4):
                        p.ve(lambda e, n4=n4, c=c, scol=scol: e.reduce_max(
                            out=SM[0:2, scol + 4 * c + n4:scol + 4 * c + n4 + 1],
                            in_=PS[n4][0:2, 0:512], axis=AX.X),
                            tick=1 if n4 == 3 else 0, drain=True)
            for c in range(8):
                p.ve(lambda e, c=c: e.reduce_max(out=SM[0:2, 826 + c:827 + c],
                                                 in_=SM[0:2, 760 + 4 * c:760 + 4 * c + 4], axis=AX.X), drain=True)
            for c in range(2):
                p.ve(lambda e, c=c: e.reduce_max(out=SM[0:2, 834 + c:835 + c],
                                                 in_=SM[0:2, 792 + 4 * c:792 + 4 * c + 4], axis=AX.X),
                     tick=1 if c == 1 else 0)
            p.wait("se", "ve")
            p.se(lambda e: e.activation(out=SM[0:2, 836:846], in_=SM[0:2, 826:836], func=AF.Sqrt), tick=1)
            p.wait("sy", "se")
            p.dma("sy", lambda e: e.dma_start(out=SM[0:1, 846:854], in_=SM[0:1, 836:844]))
            p.dma("sy", lambda e: e.dma_start(out=SM[0:1, 854:862], in_=SM[1:2, 836:844]))
            p.dma("sy", lambda e: e.dma_start(out=SM[0:1, 862:864], in_=SM[0:1, 844:846]))
            p.dma("sy", lambda e: e.dma_start(out=SM[0:1, 864:866], in_=SM[1:2, 844:846]))
            p.wait("ve", "sy")
            for h in range(16):
                qpos = (h % 2) * 8 + h // 2
                g = h // 4
                kpos = (g % 2) * 2 + g // 2
                p.ve(lambda e, h=h, qpos=qpos, kpos=kpos: e.tensor_mul(
                    out=SM[0:1, 866 + h:867 + h], in0=SM[0:1, 846 + qpos:847 + qpos],
                    in1=SM[0:1, 862 + kpos:863 + kpos]), drain=True)
            p.ve(lambda e: e.tensor_scalar(out=SM[0:1, 866:882], in0=SM[0:1, 866:882],
                                           scalar1=-0.125, scalar2=None, op0=ALU.mult), tick=1)
            p.wait("sy", "ve")
            p.dma("sy", lambda e: e.dma_start(out=bc_s[130:146], in_=SM[0:1, 866:882]))
            p.wait("sy", "sy")
            p.dma("sy", lambda e: e.dma_start(
                out=SM[:, 880:896],
                in_=bc_s[130:146].rearrange("(o c) -> o c", o=1).partition_broadcast(128)))
            p.barrier()
            for c in range(2):
                p.dma("gp", lambda e, c=c: e.dma_start(out=bv_s[c * 128:(c + 1) * 128, :],
                                                       in_=vT[:, c * 2048:(c + 1) * 2048]))
            p.barrier()
            trload_tok_off(A[4], 4096, bv_s, 256)
            p.barrier()
            for g in range(4):
                gc, gr = divmod(g, 2)
                for half in range(2):
                    p.dma("sy", lambda e, g=g, gc=gc, gr=gr, half=half: e.dma_start(
                        out=bf(A[2])[half * 64:half * 64 + 64, g * 2048:(g + 1) * 2048],
                        in_=kT[gr * 64:gr * 64 + 64, gc * 2048:(gc + 1) * 2048]))
            p.barrier()
            kT2 = bf(A[2])[:, 0:8192]
            v_tok = bf(A[4])[:, 4096:8192]
            vext = bf(A[4])[:, 8192:8192 + 16 * 65]
            e1 = bf(A[2])[:, 8192:16384]
            te_pv_prev = None
            pending_tail = [None]

            def flush_tail():
                if pending_tail[0] is None:
                    return
                h_, qg_, evac_mk = pending_tail[0]
                pending_tail[0] = None
                p.wait("te", "se", evac_mk)
                for sub in range(4):
                    t = qg_ * 4 + sub
                    p.te(lambda e, sub=sub: e.transpose(
                        out=PS[7].bitcast(BF16)[:, 0:65],
                        in_=SMB[0:65, sub * 128:(sub + 1) * 128],
                        identity=ident[0:65, 0:65]), tick=1)
                    p.wait("ve", "te")
                    p.ve(lambda e: e.reciprocal(out=SM[:, 896:897],
                                                in_=PS[7].bitcast(BF16)[:, 64:65]), drain=True)
                    p.ve(lambda e, t=t, h_=h_: e.scalar_tensor_tensor(
                        out=R_tile(t)[:, h_ * 64:(h_ + 1) * 64],
                        in0=PS[7].bitcast(BF16)[:, 0:64], scalar=SM[:, 896:897],
                        in1=R_tile(t)[:, h_ * 64:(h_ + 1) * 64],
                        op0=ALU.mult, op1=ALU.add), tick=1)
                    p.wait("te", "ve")

            for h in range(16):
                g = h // 4
                qc, qr = divmod(h, 2)
                if h % 4 == 0:
                    flush_tail()
                    p.wait("ve", "te")
                    for t in range(NT):
                        p.ve(lambda e, t=t, g=g: e.tensor_copy(
                            out=vext[:, t * 65:t * 65 + 64],
                            in_=v_tok[:, t * 256 + g * 64:t * 256 + (g + 1) * 64]))
                    p.ve(lambda e: e.memset(
                        bf(A[4])[:, 8192:8192 + 1040].rearrange("p (t c) -> p t c", c=65)[:, :, 64:65], 1.0),
                        tick=1)
                    p.wait("te", "ve")
                for qg in range(4):
                    se_marks = {}
                    te_smarks = {}
                    te_pv = {}
                    groups = [list(range(grp * 3, min(grp * 3 + 3, 16))) for grp in range(6)]

                    def emit_scores(grp):
                        base = 0 if grp % 2 == 0 else 3
                        for j, kt in enumerate(groups[grp]):
                            p.te(lambda e, j=j, kt=kt, base=base, g=g, qr=qr, qc=qc, qg=qg: e.matmul(
                                out=PS[base + j],
                                lhsT=kT2[qr * 64:qr * 64 + 64, g * 2048 + kt * 128:g * 2048 + (kt + 1) * 128],
                                rhs=qT[qr * 64:qr * 64 + 64, qc * 2048 + qg * 512:qc * 2048 + (qg + 1) * 512],
                                start=True, stop=True, skip_group_check=True))
                        p.tick("te")
                        te_smarks[grp] = p.n("te")

                    def emit_exps(grp):
                        base = 0 if grp % 2 == 0 else 3
                        p.wait("se", "te", te_smarks[grp])
                        if te_pv_prev is not None:
                            p.wait("se", "te", te_pv_prev[grp])
                        for j, kt in enumerate(groups[grp]):
                            p.se(lambda e, j=j, kt=kt, base=base, h=h: e.activation(
                                out=e1[:, kt * 512:(kt + 1) * 512], in_=PS[base + j],
                                func=AF.Exp, bias=SM[:, 880 + h:881 + h], scale=0.125), tick=1)
                        se_marks[grp] = p.n("se")

                    def emit_pv(grp):
                        p.wait("te", "se", se_marks[grp])
                        for kt in groups[grp]:
                            p.te(lambda e, kt=kt: e.matmul(
                                out=PS[6][0:65, :],
                                lhsT=vext[:, kt * 65:(kt + 1) * 65],
                                rhs=e1[:, kt * 512:(kt + 1) * 512],
                                start=(kt == 0), stop=(kt == 15), skip_group_check=True))
                        p.tick("te")
                        te_pv[grp] = p.n("te")

                    emit_scores(0)
                    emit_exps(0)
                    emit_scores(1)
                    emit_exps(1)
                    flush_tail()
                    for grp in range(2, 6):
                        emit_pv(grp - 2)
                        emit_scores(grp)
                        emit_exps(grp)
                    emit_pv(4)
                    emit_pv(5)
                    te_pv_prev = te_pv
                    p.wait("se", "te")
                    p.se(lambda e: e.activation(out=SMB[0:65, 0:512], in_=PS[6][0:65, :],
                                                func=AF.Identity), tick=1)
                    evac_mk = p.n("se")
                    pending_tail[0] = (h, qg, evac_mk)
            flush_tail()
            p.barrier()

        def emit_attn_a():
            emit_ln(lambda t: bf(A[2])[:, t * 1024:(t + 1) * 1024])
            p.wait("gp", "ve")
            for t in range(NT):
                p.dma("gp", lambda e, t=t: e.dma_start(out=h_s[t * 128:(t + 1) * 128, :],
                                                       in_=bf(A[2])[:, t * 1024:(t + 1) * 1024]))
            for t in range(NT):
                p.dma("sy", lambda e, t=t: e.dma_start(out=R_s[t * 128:(t + 1) * 128, :],
                                                       in_=R_tile(t)))
            p.barrier()
            trload_fm(A[3], h_s, NC)
            p.barrier()
            h_fm = bf(A[3])

            pw = [p.xsem("apw0"), p.xsem("apw1")]
            jobs_a = [(aW["Wq"], A[0]), (aW["Wk"], A[1]), (aW["Wv"], A[2]), (agW1, A[4])]
            pq = [(w, d, mc) for (w, d) in jobs_a for mc in range(NC)]
            pwmk = {}
            ptemk = {}

            def _pload(gi):
                w_dram, _, mc = pq[gi]
                if gi >= 2:
                    p.wait("gp", "te", ptemk[gi - 2])
                half = (gi % 2) * 1024
                for kc in range(NC):
                    p.dmax("gp", lambda e, kc=kc, mc=mc, half=half, w_dram=w_dram: e.dma_start(
                        out=SMB[:, half + kc * 128:half + (kc + 1) * 128],
                        in_=w_dram[kc * 128:(kc + 1) * 128, mc * 128:(mc + 1) * 128]), pw[gi % 2])
                pwmk[gi] = p.xn(pw[gi % 2])

            _pload(0)
            _pload(1)
            for gi, (w_dram, dst_arena, mc) in enumerate(pq):
                half = (gi % 2) * 1024
                p.xwait("te", pw[gi % 2], pwmk[gi])
                se_marks = {}
                for nt4 in range(4):
                    if nt4 >= 2:
                        p.wait("te", "se", se_marks[nt4 - 2])
                    for kc in range(NC):
                        p.te(lambda e, kc=kc, nt4=nt4, half=half: e.matmul(
                            out=PS[nt4 % 2],
                            lhsT=SMB[:, half + kc * 128:half + (kc + 1) * 128],
                            rhs=h_fm[:, kc * 2048 + nt4 * 512:kc * 2048 + (nt4 + 1) * 512],
                            start=(kc == 0), stop=(kc == NC - 1)), tick=1 if kc == NC - 1 else 0)
                    p.wait("se", "te")
                    p.se(lambda e, mc=mc, nt4=nt4, dst_arena=dst_arena: e.activation(
                        out=bf(dst_arena)[:, mc * 2048 + nt4 * 512:mc * 2048 + (nt4 + 1) * 512],
                        in_=PS[nt4 % 2], func=AF.Identity), tick=1)
                    se_marks[nt4] = p.n("se")
                ptemk[gi] = p.n("te")
                if gi + 2 < len(pq):
                    _pload(gi + 2)
            p.wait("gp", "se")
            for c in range(NC):
                p.dma("gp", lambda e, c=c: e.dma_start(out=base_s[c * 128:(c + 1) * 128, :],
                                                       in_=bf(A[4])[:, c * 2048:(c + 1) * 2048]))
            p.wait("gp", "te")
            for kc in range(NC):
                p.dma("gp", lambda e, kc=kc: e.dma_start(out=SMB[0:128, 1024 + kc * 4:1024 + (kc + 1) * 4],
                                                         in_=aWb[kc * 128:(kc + 1) * 128, :]))
            p.wait("te", "gp")
            for nt4 in range(4):
                for kc in range(NC):
                    p.te(lambda e, kc=kc, nt4=nt4: e.matmul(
                        out=PS[2][0:4, 0:512], lhsT=SMB[:, 1024 + kc * 4:1024 + (kc + 1) * 4],
                        rhs=h_fm[:, kc * 2048 + nt4 * 512:kc * 2048 + (nt4 + 1) * 512],
                        start=(kc == 0), stop=(kc == NC - 1)), tick=1 if kc == NC - 1 else 0)
                p.wait("se", "te")
                p.se(lambda e, nt4=nt4: e.activation(out=LOWF[0:4, nt4 * 512:(nt4 + 1) * 512],
                                                     in_=PS[2][0:4, 0:512], func=AF.Sigmoid), tick=1)
                p.wait("te", "se")
            p.wait("gp", "se")
            for kc in range(NC):
                p.dma("gp", lambda e, kc=kc: e.dma_start(out=bf(A[4])[:, kc * 1024:(kc + 1) * 1024],
                                                         in_=aW["Wid"][kc * 128:(kc + 1) * 128, :]))
            p.wait("te", "gp")
            for t in range(NT):
                for og in range(2):
                    if t * 2 + og >= 4:
                        p.wait("te", "gp")
                    for kc in range(NC):
                        p.te(lambda e, t=t, og=og, kc=kc: e.matmul(
                            out=PS[4 + (t * 2 + og) % 4],
                            lhsT=h_fm[:, kc * 2048 + t * 128:kc * 2048 + (t + 1) * 128],
                            rhs=bf(A[4])[:, kc * 1024 + og * 512:kc * 1024 + (og + 1) * 512],
                            start=(kc == 0), stop=(kc == NC - 1)), tick=1 if kc == NC - 1 else 0)
                    p.wait("se", "te")
                    p.se(lambda e, t=t, og=og: e.activation(
                        out=SMB[:, og * 512:(og + 1) * 512], in_=PS[4 + (t * 2 + og) % 4],
                        func=AF.Identity), tick=1)
                p.wait("gp", "se")
                p.dma("gp", lambda e, t=t: e.dma_start(out=id_s[t * 128:(t + 1) * 128, :],
                                                       in_=SMB[:, 0:1024]))
                p.wait("se", "gp")
            p.barrier()
            p.ve(lambda e: e.tensor_copy(out=LOWB[0:4, 0:2048], in_=LOWF[0:4, 0:2048]), tick=1)
            p.wait("te", "ve")
            for t in range(NT):
                p.te(lambda e, t=t: e.transpose(out=PS[5].bitcast(BF16)[:, 0:4],
                                                in_=LOWB[0:4, t * 128:(t + 1) * 128],
                                                identity=ident[0:4, 0:4]), tick=1)
                p.wait("ve", "te")
                p.ve(lambda e, t=t: e.tensor_copy(out=SM[:, 256 + 4 * t:256 + 4 * t + 4],
                                                  in_=PS[5].bitcast(BF16)[:, 0:4]), tick=1)
                p.wait("te", "ve")
            p.dma("sy", lambda e: e.dma_start(out=SM[:, 320:352].rearrange("p (c k) -> p c k", k=4),
                                              in_=acw["q"][:].rearrange("(c p) k -> p c k", p=128),
                                              allow_slow_non_contiguous=True))
            p.dma("sy", lambda e: e.dma_start(out=SM[:, 352:384].rearrange("p (c k) -> p c k", k=4),
                                              in_=acw["k"][:].rearrange("(c p) k -> p c k", p=128),
                                              allow_slow_non_contiguous=True))
            p.dma("sy", lambda e: e.dma_start(out=SM[:, 384:416].rearrange("p (c k) -> p c k", k=4),
                                              in_=acw["v"][:].rearrange("(c p) k -> p c k", p=128),
                                              allow_slow_non_contiguous=True))
            p.dma("sy", lambda e: e.dma_start(out=SM[:, 416:440].rearrange("p (c k) -> p c k", k=3),
                                              in_=afirs[:].rearrange("(c p) k -> p c k", p=128),
                                              allow_slow_non_contiguous=True))
            p.dma("sy", lambda e: e.dma_start(out=SM[:, 440:944].rearrange("p (c k) -> p c k", k=63),
                                              in_=afirl[:].rearrange("(c p) k -> p c k", p=128),
                                              allow_slow_non_contiguous=True))
            p.barrier()

            def conv4(arena, wbase, st=[{}, {}, {}, 0]):
                dmk, tmk_c, smk_c = st[0], st[1], st[2]
                psets = [[PS47[n][:] for n in range(4)], [PS[n] for n in range(4)]]
                for c in range(NC):
                    gi = st[3]
                    row = bf(arena)[:, c * 2048:(c + 1) * 2048]
                    dslot = WK2[:].bitcast(BF16)[:, (gi % 2) * 512:(gi % 2) * 512 + 512]
                    pset = psets[gi % 2]
                    if gi >= 2:
                        p.wait("ve", "te", tmk_c[gi - 2])
                    for j in range(4):
                        wc = SM[:, wbase + c * 4 + j:wbase + c * 4 + j + 1]
                        p.ve(lambda e, j=j, wc=wc, dslot=dslot: e.tensor_scalar(
                            out=dslot[:, j * 128:(j + 1) * 128], in0=ident, scalar1=wc,
                            scalar2=None, op0=ALU.mult), tick=1 if j == 3 else 0)
                    dmk[gi] = p.n("ve")
                    p.wait("te", "ve", dmk[gi])
                    if gi >= 2:
                        p.wait("te", "se", smk_c[gi - 2])
                    for j in range(4):
                        s = 3 - j
                        for n in range(4):
                            lo = max(512 * n, s)
                            hi = 512 * n + 512
                            p.te(lambda e, j=j, n=n, lo=lo, hi=hi, s=s, row=row,
                                 dslot=dslot, pset=pset: e.matmul(
                                out=pset[n][:, lo - 512 * n:hi - 512 * n],
                                lhsT=dslot[:, j * 128:(j + 1) * 128], rhs=row[:, lo - s:hi - s],
                                start=(j == 0), stop=(j == 3), skip_group_check=True))
                        p.tick("te")
                    tmk_c[gi] = p.n("te")
                    p.wait("se", "te", tmk_c[gi])
                    for n in range(4):
                        p.se(lambda e, n=n, row=row, pset=pset: e.activation(
                            out=row[:, n * 512:(n + 1) * 512], in_=pset[n], func=AF.Silu),
                            tick=1 if n == 3 else 0)
                    smk_c[gi] = p.n("se")
                    st[3] += 1

            conv4(A[0], 320)
            conv4(A[1], 352)
            conv4(A[2], 384)
            p.barrier()

            def l2norm(arena, fr):
                for c in range(NC):
                    row = bf(arena)[:, c * 2048:(c + 1) * 2048]
                    p.wait("ve", "te")
                    p.ve(lambda e, row=row: e.tensor_mul(out=SMB[:, 0:2048], in0=row, in1=row), tick=1)
                    p.wait("te", "ve")
                    for n4 in range(4):
                        p.te(lambda e, c=c, n4=n4: e.matmul(
                            out=PS[4 + n4][0:4, 0:512],
                            lhsT=SEL12[:, (c * 3) * 12:(c * 3) * 12 + 4],
                            rhs=SMB[:, n4 * 512:(n4 + 1) * 512],
                            start=(c == 0), stop=(c == NC - 1), skip_group_check=True),
                            tick=1 if n4 == 3 else 0)
                p.wait("se", "te")
                p.wait("se", "sy")
                for n4 in range(4):
                    p.se(lambda e, n4=n4: e.activation(
                        out=LOWF[0:4, n4 * 512:(n4 + 1) * 512],
                        in_=PS[4 + n4][0:4, 0:512], func=AF.Sqrt, bias=SM[0:4, 1009:1010]), tick=1)
                p.wait("ve", "se")
                p.ve(lambda e: e.reciprocal(out=LOWF[4:8, 0:2048][0:4, :] if False else WK2[0:4, 0:2048],
                                            in_=LOWF[0:4, 0:2048]), tick=1)
                p.wait("sy", "ve")
                base = 0 if fr == 4 else 4
                p.dma("sy", lambda e, base=base: e.dma_start(
                    out=rstd_s[base:base + 4, :], in_=WK2[0:4, 0:2048]))
                p.wait("sy", "sy")
                xb = [p.xsem("l2b0"), p.xsem("l2b1")]
                bufs2 = [WK2[:, 0:2048], WK[:, 0:2048]]
                bmk = {}
                vemk2 = {}

                def _lb(hh):
                    if hh >= 2:
                        p.wait("sy", "ve", vemk2[hh - 2])
                    p.dmax("sy", lambda e, hh=hh, base=base: e.dma_start(
                        out=bufs2[hh % 2],
                        in_=rstd_s[base + hh:base + hh + 1, :].partition_broadcast(128)),
                        xb[hh % 2])
                    bmk[hh] = p.xn(xb[hh % 2])

                _lb(0)
                _lb(1)
                for hh in range(4):
                    p.xwait("ve", xb[hh % 2], bmk[hh])
                    for sub in range(2):
                        c = hh * 2 + sub
                        row = bf(arena)[:, c * 2048:(c + 1) * 2048]
                        p.ve(lambda e, row=row, hh=hh: e.tensor_mul(
                            out=row, in0=row, in1=bufs2[hh % 2]), tick=1 if sub == 1 else 0)
                    vemk2[hh] = p.n("ve")
                    if hh + 2 < 4:
                        _lb(hh + 2)

            l2norm(A[0], 4)
            l2norm(A[1], 12)
            p.barrier()

            JV = 23
            wb3, wb63 = 416, 440
            acc32 = WK[:, 0:2048]
            acc3 = WK[:, 2048:4096]
            acc8 = SMB[:, 0:2048]
            dmark = {}
            te_fir_mark = {}

            def ve_taps(K, wbase, taps, c, acc, dst=None):
                vrow = bf(A[2])[:, c * 2048:(c + 1) * 2048]
                ng = 0
                for g0 in range(0, len(taps), 8):
                    grp = taps[g0:g0 + 8]
                    for idx, j in enumerate(grp):
                        s = K - 1 - j
                        wc = SM[:, wbase + c * K + j:wbase + c * K + j + 1]
                        if idx == 0:
                            if s:
                                p.ve(lambda e, s=s: e.memset(acc8[:, 0:s], 0.0), drain=True)
                            p.ve(lambda e, vrow=vrow, wc=wc, s=s: e.tensor_scalar(
                                out=acc8[:, s:2048], in0=vrow[:, 0:2048 - s], scalar1=wc,
                                scalar2=None, op0=ALU.mult))
                        elif s == 0:
                            p.ve(lambda e, vrow=vrow, wc=wc: e.scalar_tensor_tensor(
                                out=acc8, in0=vrow, scalar=wc, in1=acc8,
                                op0=ALU.mult, op1=ALU.add))
                        else:
                            p.ve(lambda e, vrow=vrow, wc=wc, s=s: e.scalar_tensor_tensor(
                                out=acc8[:, s:2048], in0=vrow[:, 0:2048 - s], scalar=wc,
                                in1=acc8[:, s:2048], op0=ALU.mult, op1=ALU.add))
                    if ng == 0:
                        p.ve(lambda e, acc=acc: e.tensor_copy(out=acc, in_=acc8))
                    else:
                        p.ve(lambda e, acc=acc: e.tensor_add(out=acc, in0=acc, in1=acc8))
                    ng += 1
                if dst is not None:
                    p.ve(lambda e, c=c, dst=dst, acc=acc: e.tensor_copy(
                        out=bf(dst)[:, c * 2048:(c + 1) * 2048], in_=acc), tick=1)

            diagbuf = WK2[:].bitcast(BF16)
            for c in range(NC):
                vrow = bf(A[2])[:, c * 2048:(c + 1) * 2048]
                if c > 0:
                    p.wait("ve", "te", te_fir_mark[c - 1])
                nte = 63 - JV
                for i, j in enumerate(range(JV, 63)):
                    wc = SM[:, wb63 + c * 63 + j:wb63 + c * 63 + j + 1]
                    dst = diagbuf[:, i * 128:(i + 1) * 128] if i < 32 \
                        else STK[:, (i - 32) * 128:(i - 31) * 128]
                    p.ve(lambda e, dst=dst, wc=wc: e.tensor_scalar(
                        out=dst, in0=ident, scalar1=wc,
                        scalar2=None, op0=ALU.mult), tick=1 if i == nte - 1 else 0)
                dmark[c] = p.n("ve")
                p.wait("te", "ve", dmark[c])
                for i, j in enumerate(range(JV, 63)):
                    s = 62 - j
                    dg = diagbuf[:, i * 128:(i + 1) * 128] if i < 32 \
                        else STK[:, (i - 32) * 128:(i - 31) * 128]
                    for n in range(4):
                        lo = max(512 * n, s)
                        hi = 512 * n + 512
                        p.te(lambda e, dg=dg, vrow=vrow, n=n, lo=lo, hi=hi, s=s, i=i: e.matmul(
                            out=PS47[n][:, lo - 512 * n:hi - 512 * n],
                            lhsT=dg, rhs=vrow[:, lo - s:hi - s],
                            start=(i == 0), stop=(i == nte - 1), skip_group_check=True))
                    p.tick("te")
                te_fir_mark[c] = p.n("te")
                ve_taps(3, wb3, [0, 1, 2], c, acc3, dst=A[4])
                ve_taps(63, wb63, list(range(JV)), c, acc32, dst=None)
                p.wait("ve", "te", te_fir_mark[c])
                for n in range(4):
                    p.ve(lambda e, c=c, n=n: e.tensor_add(
                        out=bf(A[3])[:, c * 2048 + n * 512:c * 2048 + (n + 1) * 512],
                        in0=acc32[:, n * 512:(n + 1) * 512], in1=PS47[n][:]),
                        tick=1 if n == 3 else 0)
            p.barrier()

            def branch_sums(arena, bidx, first, last):
                for c in range(NC):
                    row = bf(arena)[:, c * 2048:(c + 1) * 2048]
                    for n4 in range(4):
                        p.te(lambda e, c=c, n4=n4, row=row: e.matmul(
                            out=PST[0:12, n4 * 512:(n4 + 1) * 512],
                            lhsT=SEL12[:, (c * 3 + 0) * 12:(c * 3 + 0) * 12 + 12],
                            rhs=row[:, n4 * 512:(n4 + 1) * 512],
                            start=(first and c == 0), stop=False, skip_group_check=True),
                            tick=1 if n4 == 3 else 0)
                    p.wait("se", "te")
                    p.wait("se", "ve")
                    p.se(lambda e, row=row: e.activation(out=SMB[:, 0:2048], in_=row, func=AF.Abs), tick=1)
                    p.wait("te", "se")
                    for n4 in range(4):
                        p.te(lambda e, c=c, n4=n4: e.matmul(
                            out=PST[0:12, n4 * 512:(n4 + 1) * 512],
                            lhsT=SEL12[:, (c * 3 + 1) * 12:(c * 3 + 1) * 12 + 12],
                            rhs=SMB[:, n4 * 512:(n4 + 1) * 512],
                            start=False, stop=False, skip_group_check=True),
                            tick=1 if n4 == 3 else 0)
                    p.wait("ve", "te")
                    p.ve(lambda e, row=row: e.tensor_mul(out=SMB[:, 0:2048], in0=row, in1=row), tick=1)
                    p.wait("te", "ve")
                    for n4 in range(4):
                        p.te(lambda e, c=c, n4=n4: e.matmul(
                            out=PST[0:12, n4 * 512:(n4 + 1) * 512],
                            lhsT=SEL12[:, (c * 3 + 2) * 12:(c * 3 + 2) * 12 + 12],
                            rhs=SMB[:, n4 * 512:(n4 + 1) * 512],
                            start=False, stop=(last and c == NC - 1), skip_group_check=True),
                            tick=1 if n4 == 3 else 0)
                p.wait("ve", "te")
                p.ve(lambda e: e.tensor_copy(out=LOWB[0:12, 0:2048], in_=PST[0:12, 0:2048]), tick=1)
                p.wait("te", "ve")
                for t in range(NT):
                    p.te(lambda e, t=t: e.transpose(out=PS[6].bitcast(BF16)[:, 0:12],
                                                    in_=LOWB[0:12, t * 128:(t + 1) * 128],
                                                    identity=ident[0:12, 0:12]), tick=1)
                    p.wait("ve", "te")
                    p.ve(lambda e, t=t: e.tensor_copy(out=WK2[:, t * 48:t * 48 + 12],
                                                      in_=PS[6].bitcast(BF16)[:, 0:12]), tick=1)
                    p.wait("te", "ve")
                W3 = WK2[:, 0:768].rearrange("p (t c) -> p t c", c=48)
                p.wait("se", "ve")
                p.se(lambda e: e.activation(out=W3[:, :, 16:20], in_=W3[:, :, 8:12], func=AF.Sqrt), tick=1)
                p.wait("ve", "se")
                p.ve(lambda e: e.tensor_scalar(out=W3[:, :, 20:28], in0=W3[:, :, 0:8],
                                               scalar1=1.0 / 256, scalar2=None, op0=ALU.mult), drain=True)
                p.ve(lambda e: e.tensor_mul(out=W3[:, :, 28:32], in0=W3[:, :, 20:24],
                                            in1=W3[:, :, 20:24]), drain=True)
                p.ve(lambda e: e.tensor_scalar(out=W3[:, :, 32:36], in0=W3[:, :, 8:12],
                                               scalar1=1.0 / 256, scalar2=None, op0=ALU.mult), drain=True)
                p.ve(lambda e: e.tensor_sub(out=W3[:, :, 36:40], in0=W3[:, :, 32:36],
                                            in1=W3[:, :, 28:32]), drain=True)
                for t in range(NT):
                    for s, col in ((0, 20), (1, 36), (2, 24), (3, 16)):
                        cc = bidx * 4 + s
                        p.ve(lambda e, t=t, cc=cc, col=col: e.tensor_copy(
                            out=STK[:, t * 64:(t + 1) * 64].rearrange("p (h x) -> p h x", x=16)[:, :, cc:cc + 1],
                            in_=WK2[:, t * 48 + col:t * 48 + col + 4].unsqueeze(2)),
                            tick=1 if (t == NT - 1 and s == 3) else 0)

            xsp = p.xsem("a3sp")
            p.wait("gp", "ve")
            p.wait("sy", "ve")
            for c in range(NC):
                p.dmax("gp", lambda e, c=c: e.dma_start(out=ls_s[c * 128:(c + 1) * 128, :],
                                                        in_=bf(A[4])[:, c * 2048:(c + 1) * 2048]), xsp)
                p.dmax("gp", lambda e, c=c: e.dma_start(out=ll_s[c * 128:(c + 1) * 128, :],
                                                        in_=bf(A[3])[:, c * 2048:(c + 1) * 2048]), xsp)
            for c in range(NC):
                p.dmax("sy", lambda e, c=c: e.dma_start(out=k_s[c * 128:(c + 1) * 128, :],
                                                        in_=bf(A[1])[:, c * 2048:(c + 1) * 2048]), xsp)
                p.dmax("sy", lambda e, c=c: e.dma_start(out=v_s[c * 128:(c + 1) * 128, :],
                                                        in_=bf(A[2])[:, c * 2048:(c + 1) * 2048]), xsp)
            spmk = p.xn(xsp)
            branch_sums(A[4], 0, True, True)
            p.xwait("sy", xsp, spmk)
            p.wait("sy", "te")
            p.wait("sy", "ve")
            p.wait("sy", "se")
            trload_tok(A[4], k_s, 1024)
            branch_sums(A[2], 3, True, True)
            p.wait("sy", "te")
            p.wait("sy", "ve")
            p.wait("sy", "se")
            trload_tok(A[2], v_s, 1024)
            branch_sums(A[3], 1, True, True)
            p.barrier()
            k_tok = bf(A[4])
            v_tok = bf(A[2])
            qhat = bf(A[0])
            khat = bf(A[1])

            u0, w0, a0, x0 = 0, 4096, 8192, 10240

            def a4a_gen(c, hh, slot, pA, pB, pC, xoff, wkoff):
                q2 = (2 * hh, 2 * hh + 1)
                coff = c * 128
                P_sb = bf(A[3])[:, xoff + 0:xoff + 128]
                PT_sb = bf(A[3])[:, xoff + 128:xoff + 256]
                E_sb = bf(A[3])[:, xoff + 256:xoff + 384]
                EI_sb = bf(A[3])[:, xoff + 384:xoff + 512]
                smb = (slot % 2) * 512
                wkp = WK2[:, wkoff:wkoff + 128]
                bcol = SM[:, 256 + 4 * c + hh:256 + 4 * c + hh + 1]
                for i, fc in enumerate(q2):
                    p.te(lambda e, fc=fc, coff=coff, i=i: e.matmul(
                        out=pA[:, 0:128], lhsT=khat[:, fc * 2048 + coff:fc * 2048 + coff + 128],
                        rhs=khat[:, fc * 2048 + coff:fc * 2048 + coff + 128],
                        start=(i == 0), stop=(i == 1), skip_group_check=True),
                        tick=1 if i == 1 else 0)
                tmk = p.n("te")
                yield
                p.wait("ve", "te", tmk)
                p.ve(lambda e: e.tensor_scalar(out=wkp, in0=pA[:, 0:128],
                                               scalar1=bcol, scalar2=-1.0,
                                               op0=ALU.mult, op1=ALU.mult), drain=True)
                p.ve(lambda e: e.tensor_tensor(out=P_sb, in0=wkp, in1=mS,
                                               op=ALU.mult), tick=1)
                vmk = p.n("ve")
                yield
                p.wait("te", "ve", vmk)
                p.te(lambda e: e.transpose(out=pB.bitcast(BF16)[:, 0:128], in_=P_sb,
                                           identity=ident), tick=1)
                tmk = p.n("te")
                yield
                p.wait("se", "te", tmk)
                p.se(lambda e: e.activation(out=PT_sb, in_=pB.bitcast(BF16)[:, 0:128],
                                            func=AF.Identity), tick=1)
                p.se(lambda e: e.activation(out=E_sb, in_=pB.bitcast(BF16)[:, 0:128],
                                            func=AF.Identity), tick=1)
                smk = p.n("se")
                yield
                etmp = WK2[:, 256 + wkoff:256 + wkoff + 128]
                vmk = 0
                for i in range(6):
                    p.wait("te", "se", smk)
                    if vmk:
                        p.wait("te", "ve", vmk)
                    p.te(lambda e: e.matmul(out=pA[:, 0:128], lhsT=PT_sb, rhs=P_sb,
                                            start=True, stop=True, skip_group_check=True), tick=1)
                    tmk = p.n("te")
                    yield
                    p.wait("se", "te", tmk)
                    p.se(lambda e: e.activation(out=P_sb, in_=pA[:, 0:128],
                                                func=AF.Identity), tick=1)
                    smk = p.n("se")
                    yield
                    p.wait("te", "se", smk)
                    p.te(lambda e: e.transpose(out=pB.bitcast(BF16)[:, 0:128], in_=P_sb,
                                               identity=ident), tick=1)
                    p.te(lambda e: e.matmul(out=pC[:, 0:128], lhsT=P_sb, rhs=E_sb,
                                            start=True, stop=True, skip_group_check=True), tick=1)
                    tmk = p.n("te")
                    yield
                    p.wait("se", "te", tmk)
                    p.se(lambda e: e.activation(out=PT_sb, in_=pB.bitcast(BF16)[:, 0:128],
                                                func=AF.Identity), tick=1)
                    smk = p.n("se")
                    yield
                    p.wait("ve", "te", tmk)
                    p.ve(lambda e, etmp=etmp: e.tensor_add(out=etmp, in0=pC[:, 0:128], in1=E_sb))
                    p.wait("ve", "se", smk)
                    p.ve(lambda e, etmp=etmp: e.tensor_add(out=E_sb, in0=etmp, in1=PT_sb), tick=1)
                    vmk = p.n("ve")
                    yield
                p.ve(lambda e: e.tensor_tensor(out=EI_sb, in0=E_sb, in1=ident, op=ALU.add))
                uoff = u0 + slot * 256
                woff = w0 + slot * 256
                aoff = a0 + slot * 128
                p.ve(lambda e, smb=smb: e.tensor_scalar(
                    out=SMB[:, smb:smb + 256], in0=v_tok[:, c * 1024 + hh * 256:c * 1024 + (hh + 1) * 256],
                    scalar1=bcol, scalar2=None, op0=ALU.mult))
                p.ve(lambda e, smb=smb: e.tensor_scalar(
                    out=SMB[:, smb + 256:smb + 512], in0=k_tok[:, c * 1024 + hh * 256:c * 1024 + (hh + 1) * 256],
                    scalar1=bcol, scalar2=-1.0, op0=ALU.mult, op1=ALU.mult), tick=1)
                vmk = p.n("ve")
                yield
                p.wait("te", "ve", vmk)
                p.wait("te", "se", smk)
                p.te(lambda e, smb=smb: e.matmul(out=pA[:, 0:256], lhsT=EI_sb,
                                                 rhs=SMB[:, smb:smb + 256], start=True, stop=True,
                                                 skip_group_check=True), tick=1)
                for dh in range(2):
                    p.te(lambda e, dh=dh, smb=smb: e.matmul(
                        out=(pB if dh == 0 else pC)[:, 0:128],
                        lhsT=SMB[:, smb + 256 + dh * 128:smb + 256 + (dh + 1) * 128],
                        rhs=EI_sb, start=True, stop=True, skip_group_check=True),
                        tick=1 if dh == 1 else 0)
                tmk = p.n("te")
                yield
                p.wait("se", "te", tmk)
                p.se(lambda e, uoff=uoff: e.activation(out=bf(A[3])[:, uoff:uoff + 256],
                                                       in_=pA[:, 0:256], func=AF.Identity), tick=1)
                p.se(lambda e, woff=woff: e.activation(out=bf(A[3])[:, woff:woff + 128],
                                                       in_=pB[:, 0:128], func=AF.Identity), tick=1)
                p.se(lambda e, woff=woff: e.activation(out=bf(A[3])[:, woff + 128:woff + 256],
                                                       in_=pC[:, 0:128], func=AF.Identity), tick=1)
                smk = p.n("se")
                yield
                p.wait("te", "se", smk)
                for i, fc in enumerate(q2):
                    p.te(lambda e, fc=fc, coff=coff, i=i: e.matmul(
                        out=pA[:, 0:128], lhsT=khat[:, fc * 2048 + coff:fc * 2048 + coff + 128],
                        rhs=qhat[:, fc * 2048 + coff:fc * 2048 + coff + 128],
                        start=(i == 0), stop=(i == 1), skip_group_check=True),
                        tick=1 if i == 1 else 0)
                tmk = p.n("te")
                yield
                p.wait("ve", "te", tmk)
                p.ve(lambda e, aoff=aoff: e.tensor_tensor(
                    out=bf(A[3])[:, aoff:aoff + 128], in0=pA[:, 0:128], in1=mUin,
                    op=ALU.mult), tick=1)

            def a4b_gen(c, hh, slot, first_chunk, pU, pO, pS, ueoff, marks):
                uoff = u0 + slot * 256
                woff = w0 + slot * 256
                aoff = a0 + slot * 128
                q2 = (2 * hh, 2 * hh + 1)
                coff = c * 128
                Ssb = SMB[:, (hh % 2) * 512:(hh % 2) * 512 + 512]
                Sf = WK[:, 2048 + (hh % 2) * 512:2048 + (hh % 2) * 512 + 512]
                if marks:
                    p.wait("te", "ve", marks["uvmk"])
                    p.wait("te", "se", marks["smk"])
                    p.wait("te", "ve", marks["svmk"])
                p.te(lambda e, uoff=uoff, first_chunk=first_chunk: e.matmul(
                    out=pU[:, 0:256], lhsT=ident, rhs=bf(A[3])[:, uoff:uoff + 256],
                    start=True, stop=first_chunk, skip_group_check=True),
                    tick=1 if first_chunk else 0)
                if not first_chunk:
                    for dh in range(2):
                        p.te(lambda e, woff=woff, dh=dh, Ssb=Ssb: e.matmul(
                            out=pU[:, 0:256],
                            lhsT=bf(A[3])[:, woff + dh * 128:woff + (dh + 1) * 128],
                            rhs=Ssb[:, dh * 256:(dh + 1) * 256],
                            start=False, stop=(dh == 1), skip_group_check=True),
                            tick=1 if dh == 1 else 0)
                tmk = p.n("te")
                yield
                p.wait("ve", "te", tmk)
                p.ve(lambda e, ueoff=ueoff: e.tensor_copy(out=SMB[:, ueoff:ueoff + 256],
                                                          in_=pU[:, 0:256]), tick=1)
                marks["uvmk"] = p.n("ve")
                yield
                p.wait("te", "ve", marks["uvmk"])
                if not first_chunk:
                    for i, fc in enumerate(q2):
                        p.te(lambda e, fc=fc, coff=coff, Ssb=Ssb, i=i: e.matmul(
                            out=pO[:, 0:256], lhsT=qhat[:, fc * 2048 + coff:fc * 2048 + coff + 128],
                            rhs=Ssb[:, i * 256:(i + 1) * 256], start=(i == 0), stop=False,
                            skip_group_check=True))
                p.te(lambda e, aoff=aoff, first_chunk=first_chunk, ueoff=ueoff: e.matmul(
                    out=pO[:, 0:256], lhsT=bf(A[3])[:, aoff:aoff + 128],
                    rhs=SMB[:, ueoff:ueoff + 256], start=first_chunk, stop=True,
                    skip_group_check=True), tick=1)
                tmkO = p.n("te")
                for dh in range(2):
                    p.te(lambda e, c=c, hh=hh, dh=dh, ueoff=ueoff: e.matmul(
                        out=pS[:, dh * 256:(dh + 1) * 256],
                        lhsT=k_tok[:, c * 1024 + hh * 256 + dh * 128:c * 1024 + hh * 256 + (dh + 1) * 128],
                        rhs=SMB[:, ueoff:ueoff + 256],
                        start=(dh == 0), stop=(dh == 1), skip_group_check=True),
                        tick=1 if dh == 1 else 0)
                tmkS = p.n("te")
                yield
                p.wait("se", "te", tmkO)
                p.se(lambda e, hh=hh: e.activation(
                    out=SMB[:, 1536 + (hh % 2) * 256:1536 + (hh % 2) * 256 + 256],
                    in_=pO[:, 0:256], func=AF.Identity), tick=1)
                marks["smk"] = p.n("se")
                yield
                p.wait("ve", "te", tmkS)
                if first_chunk:
                    p.ve(lambda e, Sf=Sf: e.tensor_copy(out=Sf, in_=pS[:]))
                else:
                    p.ve(lambda e, Sf=Sf: e.tensor_add(out=Sf, in0=Sf, in1=pS[:]))
                p.ve(lambda e, Ssb=Ssb, Sf=Sf: e.tensor_copy(out=Ssb, in_=Sf), tick=1)
                marks["svmk"] = p.n("ve")

            def drive(gens, seq=False, stagger=0):
                if seq:
                    for g in gens:
                        for _ in g:
                            pass
                    return
                alive = [True] * len(gens)
                for _ in range(stagger):
                    try:
                        next(gens[0])
                    except StopIteration:
                        alive[0] = False
                while any(alive):
                    for i, g in enumerate(gens):
                        if alive[i]:
                            try:
                                next(g)
                            except StopIteration:
                                alive[i] = False

            for pair in range(2):
                h0 = 2 * pair
                for half in range(2):
                    for c8 in range(8):
                        c = half * 8 + c8
                        drive([a4a_gen(c, h0, c8 * 2, PS[0], PS[1], PS[2], 10240, 0),
                               a4a_gen(c, h0 + 1, c8 * 2 + 1, PS[6], PS[7], PS[3], 10752, 128)],
                              seq=A4A_SEQ, stagger=2)
                    p.barrier()
                    if half == 1:
                        for hh2 in range(2):
                            p.ve(lambda e, hh2=hh2: e.tensor_copy(
                                out=SMB[:, hh2 * 512:(hh2 + 1) * 512],
                                in_=WK[:, 2048 + hh2 * 512:2048 + (hh2 + 1) * 512]),
                                tick=1 if hh2 == 1 else 0)
                        p.wait("te", "ve")
                    bmarks = [{}, {}]
                    for c8 in range(8):
                        c = half * 8 + c8
                        drive([a4b_gen(c, h0, c8 * 2, c == 0, PS[3], PS[5], PS[4],
                                       1024, bmarks[0]),
                               a4b_gen(c, h0 + 1, c8 * 2 + 1, c == 0, PS[0], PS[1], PS[2],
                                       1280, bmarks[1])], seq=A4B_SEQ)
                        p.wait("gp", "se")
                        p.dma("gp", lambda e, c=c, pair=pair: e.dma_start(
                            out=delta_s[c * 128:(c + 1) * 128, pair * 512:(pair + 1) * 512],
                            in_=SMB[:, 1536:2048]))
                        p.wait("se", "gp")
                    p.barrier()
            p.barrier()

            trload_fm(A[1], delta_s, NC)
            p.barrier()
            branch_sums(A[1], 2, True, True)
            p.barrier()
            if _dbg:
                p.dma("sy", lambda e: e.dma_start(out=stk_s[:, :], in_=STK[:, :]))
                p.barrier()

            p.dma("gp", lambda e: e.dma_start(out=SMB[0:16, 1024:2048], in_=agW1[1024:1040, :]))
            p.dma("gp", lambda e: e.dma_start(out=CS2[:, 130:162].rearrange("p (c o) -> p c o", o=4),
                                              in_=agW2[:].rearrange("(c p) o -> p c o", p=128),
                                              allow_slow_non_contiguous=True))
            p.dma("sy", lambda e: e.dma_start(out=SM[:, 320:328],
                                              in_=agb1[:].rearrange("(c p) o -> p (c o)", p=128),
                                              allow_slow_non_contiguous=True))
            p.dma("sy", lambda e: e.dma_start(out=SM[0:1, 928:944], in_=alt[:]))
            p.dma("sy", lambda e: e.dma_start(out=SM[0:1, 944:960], in_=afp[:]))
            p.dma("sy", lambda e: e.dma_start(out=SM[0:4, 964:965],
                                              in_=agb2[:].rearrange("o c -> c o"),
                                              allow_slow_non_contiguous=True))
            for c in range(NC):
                p.dma("sy", lambda e, c=c: e.dma_start(out=bf(A[0])[:, c * 2048:(c + 1) * 2048],
                                                       in_=base_s[c * 128:(c + 1) * 128, :]))
            p.barrier()
            p.se(lambda e: e.activation(out=SM[0:1, 968:984], in_=SM[0:1, 928:944],
                                        func=AF.Exp, scale=-1.0), tick=1)
            p.se(lambda e: e.activation(out=SM[0:1, 984:1000], in_=SM[0:1, 944:960],
                                        func=AF.Sigmoid), tick=1)
            p.wait("ve", "se")
            p.ve(lambda e: e.tensor_scalar(out=SM[0:1, 984:1000], in0=SM[0:1, 984:1000],
                                           scalar1=0.05, scalar2=None, op0=ALU.mult), drain=True)
            p.ve(lambda e: e.reduce_sum(out=SM[0:1, 1000:1004],
                                        in_=SM[0:1, 984:1000].rearrange("o (h c) -> o h c", h=4),
                                        axis=AX.X), drain=True)
            p.ve(lambda e: e.tensor_scalar(out=SM[0:1, 1000:1004], in0=SM[0:1, 1000:1004],
                                           scalar1=-1.0, scalar2=1.0, op0=ALU.mult, op1=ALU.add), tick=1)
            p.wait("sy", "ve")
            p.dma("sy", lambda e: e.dma_start(out=bc_s[160:176], in_=SM[0:1, 968:984]))
            p.dma("sy", lambda e: e.dma_start(out=bc_s[176:192], in_=SM[0:1, 984:1000]))
            p.dma("sy", lambda e: e.dma_start(out=bc_s[192:196], in_=SM[0:1, 1000:1004]))
            p.wait("sy", "sy")
            p.dma("sy", lambda e: e.dma_start(out=SM[0:4, 960:964],
                                              in_=bc_s[160:176].rearrange("(h c) -> c h", c=4),
                                              allow_slow_non_contiguous=True))
            p.dma("sy", lambda e: e.dma_start(
                out=SM[:, 776:792],
                in_=bc_s[176:192].rearrange("(o c) -> o c", o=1).partition_broadcast(128)))
            p.dma("sy", lambda e: e.dma_start(
                out=SM[:, 792:796],
                in_=bc_s[192:196].rearrange("(o c) -> o c", o=1).partition_broadcast(128)))
            p.barrier()
            for hh in range(4):
                p.wait("te", "ve")
                for t in range(NT):
                    p.te(lambda e, t=t, hh=hh: e.transpose(
                        out=PS[6].bitcast(BF16)[0:16, 0:128],
                        in_=STK[:, t * 64 + hh * 16:t * 64 + hh * 16 + 16],
                        identity=ident), tick=1)
                    p.wait("ve", "te")
                    p.ve(lambda e, t=t: e.tensor_copy(
                        out=LOWB[0:16, t * 128:(t + 1) * 128],
                        in_=PS[6].bitcast(BF16)[0:16, 0:128]), tick=1)
                    p.wait("te", "ve")
                se_marks = {}
                for mc in range(NC):
                    for nt4 in range(4):
                        gi = mc * 4 + nt4
                        if gi >= 2:
                            p.wait("te", "se", se_marks[gi - 2])
                        p.te(lambda e, hh=hh, mc=mc, nt4=nt4: e.matmul(
                            out=PS[nt4 % 2],
                            lhsT=SMB[0:16, 1024 + mc * 128:1024 + (mc + 1) * 128],
                            rhs=LOWB[0:16, nt4 * 512:(nt4 + 1) * 512],
                            start=True, stop=False))
                        p.te(lambda e, mc=mc, nt4=nt4: e.matmul(
                            out=PS[nt4 % 2], lhsT=ident,
                            rhs=bf(A[0])[:, mc * 2048 + nt4 * 512:mc * 2048 + (nt4 + 1) * 512],
                            start=False, stop=True), tick=1)
                        p.wait("se", "te")
                        p.se(lambda e, mc=mc, nt4=nt4: e.activation(
                            out=bf(A[4])[:, mc * 2048 + nt4 * 512:mc * 2048 + (nt4 + 1) * 512],
                            in_=PS[nt4 % 2], func=AF.Gelu,
                            bias=SM[:, 320 + mc:321 + mc]), tick=1)
                        se_marks[gi] = p.n("se")
                p.wait("te", "se")
                for nt4 in range(4):
                    for kc in range(NC):
                        p.te(lambda e, kc=kc, nt4=nt4: e.matmul(
                            out=PS[2][0:4, 0:512], lhsT=CS2[:, 130 + kc * 4:130 + (kc + 1) * 4],
                            rhs=bf(A[4])[:, kc * 2048 + nt4 * 512:kc * 2048 + (nt4 + 1) * 512],
                            start=(kc == 0), stop=(kc == NC - 1)), tick=1 if kc == NC - 1 else 0)
                    p.wait("ve", "te")
                    p.wait("ve", "gp")
                    p.ve(lambda e, hh=hh, nt4=nt4: e.tensor_scalar(
                        out=LOWF[0:4, nt4 * 512:(nt4 + 1) * 512],
                        in0=PS[2][0:4, 0:512],
                        scalar1=SM[0:4, 964:965], scalar2=SM[0:4, 960 + hh:961 + hh],
                        op0=ALU.add, op1=ALU.mult), tick=1)
                    p.wait("te", "ve")
                    p.wait("gp", "ve")
                    p.dma("gp", lambda e, hh=hh, nt4=nt4: e.dma_start(
                        out=logits_s[4 * hh:4 * hh + 4, nt4 * 512:(nt4 + 1) * 512],
                        in_=LOWF[0:4, nt4 * 512:(nt4 + 1) * 512]))
            p.wait("gp", "ve")
            p.wait("gp", "gp")
            p.dma("gp", lambda e: e.dma_start(out=LOWB[0:16, 0:2048], in_=logits_s[:]))
            p.wait("te", "gp")
            for t in range(NT):
                p.te(lambda e, t=t: e.transpose(out=PS[6].bitcast(BF16)[:, 0:16],
                                                in_=LOWB[0:16, t * 128:(t + 1) * 128],
                                                identity=ident[0:16, 0:16]), tick=1)
                p.wait("se", "te")
                p.se(lambda e, t=t: e.activation(out=WK2[:, 0:16], in_=PS[6].bitcast(BF16)[:, 0:16],
                                                 func=AF.Exp), tick=1)
                p.wait("ve", "se")
                p.ve(lambda e: e.reduce_sum(out=WK2[:, 16:20],
                                            in_=WK2[:, 0:16].rearrange("p (h c) -> p h c", h=4),
                                            axis=AX.X), drain=True)
                p.ve(lambda e: e.reciprocal(out=WK2[:, 20:24], in_=WK2[:, 16:20]), drain=True)
                p.ve(lambda e: e.tensor_mul(out=WK2[:, 20:24], in0=WK2[:, 20:24], in1=SM[:, 792:796]), drain=True)
                p.ve(lambda e, t=t: e.tensor_tensor(
                    out=WK2[:, 24:40].rearrange("p (h c) -> p h c", h=4),
                    in0=WK2[:, 0:16].rearrange("p (h c) -> p h c", h=4),
                    in1=WK2[:, 20:24].unsqueeze(2).broadcast_to([128, 4, 4]),
                    op=ALU.mult), drain=True)
                p.ve(lambda e, t=t: e.tensor_add(out=SM[:, 256 + 16 * t:256 + 16 * t + 16],
                                                 in0=WK2[:, 24:40], in1=SM[:, 776:792]),
                     tick=1)
                p.wait("te", "ve")
                p.wait("se", "ve")
            p.barrier()

            load_R_a6 = True
            for t in range(NT):
                p.dma("sy", lambda e, t=t: e.dma_start(out=R_tile(t), in_=R_s[t * 128:(t + 1) * 128, :]))
            for kc in range(NC):
                p.dma("gp", lambda e, kc=kc: e.dma_start(out=bf(A[4])[:, kc * 1024:(kc + 1) * 1024],
                                                         in_=aW["Wo"][kc * 128:(kc + 1) * 128, :]))
            p.dma("sy", lambda e: e.dma_start(out=SM[:, 512:768],
                                              in_=aonw[0:1, :].partition_broadcast(128)))
            p.dma("sy", lambda e: e.dma_start(out=SM[:, 768:772],
                                              in_=amisc[0:1, 0:4].partition_broadcast(128)))
            p.dma("sy", lambda e: e.dma_start(out=SM[0:1, 912:928], in_=amisc[:]))
            p.wait("se", "sy")
            p.se(lambda e: e.activation(out=SM[0:1, 1004:1008], in_=SM[0:1, 916:920],
                                        func=AF.Sigmoid), tick=1)
            p.wait("sy", "se")
            p.dma("sy", lambda e: e.dma_start(out=bc_s[196:200], in_=SM[0:1, 1004:1008]))
            p.wait("sy", "sy")
            p.dma("sy", lambda e: e.dma_start(
                out=SM[:, 772:776],
                in_=bc_s[196:200].rearrange("(o c) -> o c", o=1).partition_broadcast(128)))
            p.barrier()
            for t in range(NT):
                p.ve(lambda e, t=t: e.tensor_add(
                    out=SM[:, 256 + 16 * t:256 + 16 * t + 16].rearrange("p (h c) -> p h c", c=4)[:, :, 0:1],
                    in0=SM[:, 256 + 16 * t:256 + 16 * t + 16].rearrange("p (h c) -> p h c", c=4)[:, :, 0:1],
                    in1=SM[:, 772:776].unsqueeze(2)), tick=1 if t == NT - 1 else 0)
            for t in range(NT):
                p.wait("sy", "te")
                p.wait("gp", "te")
                p.dma("sy", lambda e, t=t: e.dma_start_transpose(
                    out=bf(A[3])[:, 0:1024], in_=ls_s[:, t * 128:(t + 1) * 128]))
                p.dma("sy", lambda e, t=t: e.dma_start_transpose(
                    out=bf(A[3])[:, 1024:2048], in_=ll_s[:, t * 128:(t + 1) * 128]))
                p.dma("gp", lambda e, t=t: e.dma_start(
                    out=bf(A[3])[:, 2048:3072], in_=delta_s[t * 128:(t + 1) * 128, :]))
                p.dma("gp", lambda e, t=t: e.dma_start(
                    out=bf(A[3])[:, 3072:4096], in_=id_s[t * 128:(t + 1) * 128, :]))
                p.wait("ve", "sy")
                p.wait("ve", "gp")
                pcol = lambda t, hh, cc: SM[:, 256 + 16 * t + 4 * hh + cc:256 + 16 * t + 4 * hh + cc + 1]
                for hh in range(4):
                    acc = WK[:, hh * 256:(hh + 1) * 256]
                    sl = hh * 256
                    p.ve(lambda e, t=t, hh=hh, acc=acc, sl=sl: e.tensor_scalar(
                        out=acc, in0=bf(A[3])[:, sl:sl + 256], scalar1=pcol(t, hh, 0),
                        scalar2=None, op0=ALU.mult))
                    p.ve(lambda e, t=t, hh=hh, acc=acc, sl=sl: e.scalar_tensor_tensor(
                        out=acc, in0=bf(A[3])[:, 1024 + sl:1024 + sl + 256], scalar=pcol(t, hh, 1),
                        in1=acc, op0=ALU.mult, op1=ALU.add))
                    p.ve(lambda e, t=t, hh=hh, acc=acc, sl=sl: e.scalar_tensor_tensor(
                        out=acc, in0=bf(A[3])[:, 2048 + sl:2048 + sl + 256], scalar=pcol(t, hh, 2),
                        in1=acc, op0=ALU.mult, op1=ALU.add))
                    p.ve(lambda e, t=t, hh=hh, acc=acc, sl=sl: e.scalar_tensor_tensor(
                        out=acc, in0=v_tok[:, t * 1024 + sl:t * 1024 + sl + 256], scalar=pcol(t, hh, 3),
                        in1=acc, op0=ALU.mult, op1=ALU.add))
                    p.ve(lambda e, t=t, hh=hh, acc=acc, sl=sl: e.scalar_tensor_tensor(
                        out=acc, in0=bf(A[3])[:, 3072 + sl:3072 + sl + 256],
                        scalar=SM[:, 768 + hh:769 + hh],
                        in1=acc, op0=ALU.mult, op1=ALU.add), tick=1)
                p.wait("se", "ve")
                p.se(lambda e: e.activation(out=WK2[:, 1024:2048], in_=WK[:, 0:1024],
                                            func=AF.Square), tick=1)
                p.wait("ve", "se")
                p.ve(lambda e: e.reduce_sum(out=WK2[:, 0:4],
                                            in_=WK2[:, 1024:2048].rearrange("p (h d) -> p h d", h=4),
                                            axis=AX.X), tick=1)
                p.wait("se", "ve")
                p.se(lambda e: e.activation(out=WK2[:, 4:8], in_=WK2[:, 0:4], func=AF.Sqrt,
                                            scale=1.0 / 256, bias=SM[:, 1008:1009]), tick=1)
                p.wait("ve", "se")
                p.ve(lambda e: e.reciprocal(out=WK2[:, 8:12], in_=WK2[:, 4:8]), drain=True)
                for hh in range(4):
                    p.ve(lambda e, hh=hh: e.scalar_tensor_tensor(
                        out=SMB[:, hh * 256:(hh + 1) * 256], in0=WK[:, hh * 256:(hh + 1) * 256],
                        scalar=WK2[:, 8 + hh:9 + hh], in1=SM[:, 512:768],
                        op0=ALU.mult, op1=ALU.mult), tick=1 if hh == 3 else 0)
                p.wait("te", "ve")
                for kc in range(NC):
                    p.te(lambda e, kc=kc: e.transpose(
                        out=PS[6].bitcast(BF16)[:, 0:128],
                        in_=SMB[:, kc * 128:(kc + 1) * 128], identity=ident), tick=1)
                    p.wait("ve", "te")
                    p.ve(lambda e, kc=kc: e.tensor_copy(out=SMB[:, 1024 + kc * 128:1024 + (kc + 1) * 128],
                                                        in_=PS[6].bitcast(BF16)[:, 0:128]), tick=1)
                    p.wait("te", "ve")
                for og in range(2):
                    for kc in range(NC):
                        p.te(lambda e, kc=kc, og=og: e.matmul(
                            out=PS[og],
                            lhsT=SMB[:, 1024 + kc * 128:1024 + (kc + 1) * 128],
                            rhs=bf(A[4])[:, kc * 1024 + og * 512:kc * 1024 + (og + 1) * 512],
                            start=(kc == 0), stop=(kc == NC - 1)), tick=1 if kc == NC - 1 else 0)
                p.wait("ve", "te")
                for og in range(2):
                    p.ve(lambda e, t=t, og=og: e.tensor_add(
                        out=R_tile(t)[:, og * 512:(og + 1) * 512],
                        in0=R_tile(t)[:, og * 512:(og + 1) * 512],
                        in1=PS[og]), tick=1 if og == 1 else 0)
                p.wait("te", "ve")
            p.barrier()

        if "attn_a" in enabled:
            emit_attn_a()
        for tok in enabled:
            if tok.startswith("ffn1"):
                emit_ffn(tok[4:] or "abcd")
        if "attn_b" in enabled:
            emit_attn_b()
        if "ffn2" in enabled:
            emit_ffn()

        for t in range(NT):
            p.dma("sy", lambda e, t=t: e.dma_start(out=out[t * 128:(t + 1) * 128, :], in_=R_tile(t)))
        p.wait("sy", "sy")

        block = ctx.enter_context(nc.Block())
        p.finalize(block)

    return nc


def _consts():
    i = np.arange(128)
    c = {}
    c["c_id"] = np.eye(128, dtype=ml_dtypes.bfloat16)
    c["c_mS"] = (i[:, None] > i[None, :]).astype(ml_dtypes.bfloat16)
    c["c_mUin"] = (i[None, :] >= i[:, None]).astype(ml_dtypes.bfloat16)
    c["c_ones1"] = np.ones((1, 128), dtype=ml_dtypes.bfloat16)
    selB = np.zeros((128, 2), dtype=np.float32)
    selB[0:64, 0] = 1.0
    selB[64:128, 1] = 1.0
    c["c_selB"] = selB.astype(ml_dtypes.bfloat16)
    sel12 = np.zeros((128, 8, 3, 12), dtype=np.float32)
    for ch in range(8):
        for q in range(3):
            sel12[:, ch, q, q * 4 + ch // 2] = 1.0
    c["c_sel12"] = sel12.reshape(128, 288).astype(ml_dtypes.bfloat16)
    return c


def kernel(**inputs):
    B = 8
    x = np.ascontiguousarray(np.asarray(inputs["L0_attn_a_hs"], dtype=np.float32))
    nc = build()

    def prep(name):
        return np.ascontiguousarray(np.asarray(inputs[name], dtype=np.float32))

    m = {
        "aWq": prep("L0_attn_a_Wq"), "aWk": prep("L0_attn_a_Wk"), "aWv": prep("L0_attn_a_Wv"),
        "aWid": prep("L0_attn_a_Wid"), "aWo": prep("L0_attn_a_Wo"), "aWb": prep("L0_attn_a_Wb"),
        "acwq": prep("L0_attn_a_cwq"), "acwk": prep("L0_attn_a_cwk"), "acwv": prep("L0_attn_a_cwv"),
        "afirs": prep("L0_attn_a_firs").reshape(1024, 3),
        "afirl": prep("L0_attn_a_firl").reshape(1024, 63),
        "agW1": prep("L0_attn_a_gW1"), "agb1": prep("L0_attn_a_gb1").reshape(1024, 1),
        "agW2": prep("L0_attn_a_gW2"), "agb2": prep("L0_attn_a_gb2").reshape(1, 4),
        "amisc": np.concatenate([prep("L0_attn_a_alpha_id"), prep("L0_attn_a_conv_res_logit"),
                                 np.zeros(8, np.float32)]).reshape(1, 16),
        "alt": prep("L0_attn_a_log_temp").reshape(1, 16),
        "afp": prep("L0_attn_a_floor_param").reshape(1, 16),
        "aonw": prep("L0_attn_a_o_norm_w").reshape(1, 256),
        "bWq": prep("L0_attn_b_Wq"), "bWk": prep("L0_attn_b_Wk"), "bWv": prep("L0_attn_b_Wv"),
        "bbq": prep("L0_attn_b_bq").reshape(1024, 1), "bbk": prep("L0_attn_b_bk").reshape(256, 1),
        "bbv": prep("L0_attn_b_bv").reshape(256, 1),
        "fwg": prep("L0_ffn_w_gate"), "fwu": prep("L0_ffn_w_up"), "fwd": prep("L0_ffn_w_down"),
    }
    m.update(_consts())
    in_maps = []
    for b in range(B):
        mm_ = dict(m)
        mm_["x"] = np.ascontiguousarray(x[b])
        in_maps.append(mm_)
    res = run_bass_kernel_spmd(nc, in_maps, core_ids=list(range(B)))
    return np.stack([res.results[b]["out"] for b in range(B)], axis=0).astype(np.float32)
